# version 17
# baseline (speedup 1.0000x reference)
import numpy as np
import ml_dtypes
from contextlib import ExitStack

import concourse.bass as bass
import concourse.mybir as mybir
from concourse.bass_utils import run_bass_kernel_spmd

F32 = mybir.dt.float32
BF16 = mybir.dt.bfloat16
AF = mybir.ActivationFunctionType
ALU = mybir.AluOpType
AX = mybir.AxisListType
bf16 = ml_dtypes.bfloat16
ts = bass.ts

B, T, D = 4, 4096, 1024
L = 2 * T
N1, N2 = 128, 64
STAGES = 3
E, FFN = 8, 1408
DH, TH = D // 2, T // 2
NQ = DH // 16
FSH = STAGES * DH // 4
FROW = 16384

DEBUG_OUTS = ()
BUILD_UNTIL = None
SIM_LOCAL = False


def _tables():
    n1 = np.arange(N1)[:, None]
    k1 = np.arange(N1)[None, :]
    n2 = np.arange(N2)[:, None]
    k2 = np.arange(N2)[None, :]
    W128 = np.exp(-2j * np.pi * n1 * k1 / N1)
    W64 = np.exp(-2j * np.pi * n2 * k2 / N2)
    W64i = np.exp(2j * np.pi * k2.T * n2.T / N2)
    W128i = np.exp(2j * np.pi * k1.T * n1.T / N1)
    tw = np.exp(-2j * np.pi * np.arange(N1)[:, None] * np.arange(N2)[None, :] / L)

    def b_(x):
        return np.ascontiguousarray(x).astype(np.float32).astype(bf16)

    t = {}
    w128c = np.concatenate([W128.real[:64, :], W128.imag[:64, :]], axis=1)
    t["W128c"] = b_(np.concatenate([w128c, w128c], axis=0))
    for nm, m in [("W64re", W64.real), ("W64im", W64.imag), ("W64imn", -W64.imag),
                  ("W64ire", W64i.real), ("W64iim", W64i.imag), ("W64iimn", -W64i.imag)]:
        t[nm] = b_(np.concatenate([m, m], axis=0))
    t["W128ire"] = b_(W128i.real[:, :64] / L)
    t["W128iimn"] = b_(-W128i.imag[:, :64] / L)
    t["twre"] = b_(np.tile(tw.real[:, None, :], (1, 8, 1)).reshape(128, 512))
    t["twim"] = b_(np.tile(tw.imag[:, None, :], (1, 8, 1)).reshape(128, 512))
    ctm = np.conj(tw).T
    ctp = np.concatenate([ctm, ctm], axis=0)
    t["ctre"] = b_(np.tile(ctp.real[:, None, :], (1, 4, 1)).reshape(128, 512))
    t["ctim"] = b_(np.tile(ctp.imag[:, None, :], (1, 4, 1)).reshape(128, 512))
    ident = np.eye(128, dtype=np.float32)
    t["identb"] = b_(ident)
    t["identf"] = ident
    t["ones1"] = b_(np.ones((1, 128), np.float32))
    pos_t = np.arange(T, dtype=np.float32)[:, None]
    i = np.arange(D)[None, :]
    angle = pos_t / np.power(np.float32(10000.0), (2 * (i // 2)).astype(np.float32) / D)
    pos = np.where(i % 2 == 0, np.sin(angle), np.cos(angle)).astype(np.float32)
    t["posT"] = b_(pos.T)
    return t


def _shard_inputs(ins):
    X = np.asarray(ins["L0_attn_X"])
    Wp = np.asarray(ins["L0_attn_Wp"])
    bp = np.asarray(ins["L0_attn_bp"])
    Wf = np.asarray(ins["L0_attn_Wf"])
    bff = np.asarray(ins["L0_attn_bf"])
    gw = np.asarray(ins["L0_ffn_gate_w"])
    w1 = np.asarray(ins["L0_ffn_w1"])
    w2 = np.asarray(ins["L0_ffn_w2"])
    tabs = _tables()
    maps = []
    for c in range(8):
        b, h = c >> 1, c & 1
        dcols = np.arange(h * DH, h * DH + DH)
        pcols = np.concatenate([s * D + dcols for s in range(STAGES + 1)])
        r = c >> 1
        s_ = np.repeat(np.arange(STAGES), 128)
        j_ = np.tile(np.arange(128), STAGES)
        fcols = s_ * D + h * DH + r * 128 + j_
        cmask = np.zeros((128, 16), np.float32)
        cmask[:, h] = 1.0
        m = {
            "Xb": np.ascontiguousarray(X[b]),
            "Xh": np.ascontiguousarray(X[b, h * TH:(h + 1) * TH]),
            "Wp_c": np.ascontiguousarray(Wp[:, pcols]),
            "bp_c": np.ascontiguousarray(bp[pcols]).reshape(16, 128).T.copy(),
            "Wf_c": np.ascontiguousarray(Wf[:, fcols]),
            "bf_c": np.ascontiguousarray(bff[fcols]).reshape(FSH // 128, 128).T.copy(),
            "gwT": np.ascontiguousarray(gw.T),
            "w1": w1,
            "w2": w2,
            "cmask": cmask,
        }
        m.update(tabs)
        maps.append(m)
    return maps


class P:
    def __init__(self):
        self.nc = bass.Bass("TRN2", target_bir_lowering=False)
        self.ctx = ExitStack()
        self.cnt = {}
        self.sems = {}
        self.lastw = {}
        self.res = {}

    def sem(self, name):
        self.sems[name] = self.ctx.enter_context(self.nc.semaphore(name))
        self.cnt[name] = 0

    def qsem(self, name):
        if name not in self.sems:
            self.sem(name)
        return name

    def inc(self, instr, name, n=1):
        self.qsem(name)
        instr.then_inc(self.sems[name], n)
        self.cnt[name] += n
        return self.cnt[name]

    def wait(self, eng, name, val=None):
        self.qsem(name)
        if val is None:
            val = self.cnt[name]
        if val <= 0:
            return
        key = (id(eng), name)
        if self.lastw.get(key, -1) >= val:
            return
        eng.wait_ge(self.sems[name], val)
        self.lastw[key] = val

    def use(self, resname, semname, count):
        d = self.res.setdefault(resname, {})
        d[semname] = max(d.get(semname, 0), count)

    def claim(self, eng, resname):
        for s, c in self.res.get(resname, {}).items():
            self.wait(eng, s, c)

    def sbuf(self, name, shape, dtype, scope=None):
        return (scope or self.ctx).enter_context(self.nc.sbuf_tensor(name, shape, dtype))

    def psum(self, name, shape, dtype=F32, scope=None):
        return (scope or self.ctx).enter_context(self.nc.psum_tensor(name, shape, dtype))


def build_program(debug=()):
    p = P()
    nc = p.nc

    def din(name, shape, dtype=F32):
        return nc.dram_tensor(name, shape, dtype, kind="ExternalInput").ap()

    Xb = din("Xb", [T, D]); Xh = din("Xh", [TH, D])
    Wp_c = din("Wp_c", [D, 4 * DH]); bp_c = din("bp_c", [128, 16])
    Wf_c = din("Wf_c", [D, FSH]); bf_c = din("bf_c", [128, FSH // 128])
    gwT = din("gwT", [D, E])
    w1 = din("w1", [E, D, 2 * FFN]); w2 = din("w2", [E, FFN, D])
    cmask = din("cmask", [128, 16])
    CT = {nm: din(nm, shp, BF16) for nm, shp in [
        ("W128c", [128, 256]), ("W64re", [128, 64]), ("W64im", [128, 64]),
        ("W64imn", [128, 64]), ("W64ire", [128, 64]), ("W64iim", [128, 64]),
        ("W64iimn", [128, 64]), ("W128ire", [128, 64]), ("W128iimn", [128, 64]),
        ("twre", [128, 512]), ("twim", [128, 512]), ("ctre", [128, 512]),
        ("ctim", [128, 512]), ("identb", [128, 128]), ("ones1", [1, 128]),
        ("posT", [D, T])]}
    identf = din("identf", [128, 128], F32)

    out = nc.dram_tensor("out", [TH, D], F32, kind="ExternalOutput").ap()
    dbg = {}
    for name in debug:
        shp = {"xnT": [D, T], "xnhT": [D, TH], "xp": [4 * DH, T],
               "fhbm": [FSH, T], "Fsh": [FSH, FROW], "Ffull": [4 * FSH, FROW],
               "y1": [128, NQ * 512], "y2": [128, NQ * 512], "attn": [DH, T],
               "attnfm": [D, TH], "comb": [128, 128], "ffn": [D, TH]}[name]
        dbg[name] = nc.dram_tensor("dbg_" + name, shp, F32, kind="ExternalOutput").ap()

    xp_hbm = nc.dram_tensor("xp_hbm", [4 * DH, T], BF16)
    f_hbm = nc.dram_tensor("f_hbm", [FSH, T], BF16)
    agF_in = nc.dram_tensor("agF_in", [FSH, FROW], BF16)
    agF_s = [nc.dram_tensor(f"agF_o{s}", [512, FROW], BF16) for s in range(STAGES)]
    agA_in = nc.dram_tensor("agA_in", [DH, T], BF16)
    agA_a = nc.dram_tensor("agA_a", [2 * 384, T], BF16)
    agA_b = nc.dram_tensor("agA_b", [2 * 128, T], BF16)

    for s in ["qs", "qa", "qg", "te", "dv", "ac", "gp", "cc"]:
        p.sem(s)
    TE, DV, ACT, GP, SY = nc.tensor, nc.vector, nc.scalar, nc.gpsimd, nc.sync

    def full_barrier():
        for e in [TE, DV, ACT, GP, SY]:
            for s in list(p.cnt.keys()):
                if s != "cc":
                    p.wait(e, s)

    xnhT = p.sbuf("xnhT", [128, 8 * TH], BF16)
    cons = p.sbuf("cons", [128, 2944], BF16)
    w128c_s = cons[:, 0:256]
    w64_s = {nm: cons[:, 256 + 64 * i:256 + 64 * i + 64]
             for i, nm in enumerate(["W64re", "W64im", "W64imn", "W64ire", "W64iim", "W64iimn"])}
    w128i_s = cons[:, 640:768]
    twre_s = cons[:, 768:1280]; twim_s = cons[:, 1280:1792]
    ctre_s = cons[:, 1792:2304]; ctim_s = cons[:, 2304:2816]
    idb_s = cons[:, 2816:2944]
    ones_s = p.sbuf("ones_s", [1, 128], BF16)
    bp_s = p.sbuf("bp_s", [128, 16], F32)
    bf_s = p.sbuf("bf_s", [128, FSH // 128], F32)
    cm_s = p.sbuf("cm_s", [128, 16], F32)
    eps_s = p.sbuf("eps_s", [128, 1], F32)

    scopeA = ExitStack()
    scopeB = ExitStack()
    scopeC = ExitStack()
    scopeD = ExitStack()

    blk = p.ctx.enter_context(nc.Block())


    def _finish_stub(open_scopes):
        for e in [TE, DV, ACT, GP, SY]:
            for s in list(p.cnt.keys()):
                p.wait(e, s)
        GP.memset(out_fm[:, 0:1024], 0.0)
        p.inc(GP.engine_nop(), "gp")
        p.wait(GP, "gp", p.cnt["gp"])
        for it in range(TH // 128):
            p.inc(GP.dma_start(out=out[it * 128:(it + 1) * 128, :], in_=out_fm[:, 0:1024]), "qg", 16)
        p.wait(GP, "qg", p.cnt["qg"])
        p.wait(SY, "qg", p.cnt["qg"])
        for sc in open_scopes:
            sc.close()
    p.inc(DV.memset(eps_s[:, :], 1e-5), "dv")
    p.inc(SY.dma_start(out=w128c_s, in_=CT["W128c"]), "qs", 16)
    for nm in ["W64re", "W64im", "W64imn", "W64ire", "W64iim", "W64iimn"]:
        p.inc(SY.dma_start(out=w64_s[nm], in_=CT[nm]), "qs", 16)
    p.inc(SY.dma_start(out=w128i_s[:, 0:64], in_=CT["W128ire"]), "qs", 16)
    p.inc(SY.dma_start(out=w128i_s[:, 64:128], in_=CT["W128iimn"]), "qs", 16)
    for dst, nm in [(twre_s, "twre"), (twim_s, "twim"), (ctre_s, "ctre"),
                    (ctim_s, "ctim"), (idb_s, "identb")]:
        p.inc(SY.dma_start(out=dst, in_=CT[nm]), "qs", 16)
    p.inc(SY.dma_start(out=ones_s[:, :], in_=CT["ones1"]), "qs", 16)
    p.inc(SY.dma_start(out=bp_s[:, :], in_=bp_c), "qs", 16)
    p.inc(SY.dma_start(out=bf_s[:, :], in_=bf_c), "qs", 16)
    p.inc(SY.dma_start(out=cm_s[:, :], in_=cmask), "qs", 16)
    consts_done = p.cnt["qs"]

    s1b = [p.sbuf("s1b0", [128, 2048], BF16)]
    xtw = s1b
    twt = [p.sbuf("twt0", [128, 2048], BF16)]
    s2in = [p.sbuf("s2in0", [128, 2048], BF16)]
    wsb = [p.sbuf("wsb0", [128, 2048], BF16)]
    i1b = [p.sbuf("i1b0", [128, 2048], BF16)]
    i1t = i1b
    s3in = [p.sbuf("s3in0", [128, 2048], BF16)]
    fscr = p.sbuf("fscr", [128, 5120], BF16)
    wf_sb = fscr[:, 0:3072]
    pos_sb = fscr[:, 3072:4096]
    f_ev = fscr[:, 4096:4608]
    fin_sb = fscr[:, 4608:5120]
    gw_sb = p.sbuf("gw_sb", [128, 64], BF16)
    comb_tok = p.sbuf("comb_tok", [128, 128], BF16)
    gst = p.sbuf("gst", [128, 8], F32)
    esb = p.sbuf("esb", [128, 8], F32)
    msk = p.sbuf("msk", [128, 16], F32)
    combf = p.sbuf("combf", [128, 8], F32)
    ps_A = [[p.psum(f"cpA_{i}", [128, 512]) for i in range(2)]]
    ps_T = [p.psum("cpT", [128, 1024], BF16)]
    ps_Z = [p.psum("cpZ", [128, 512])]
    ps_Zb = p.psum("cpZ1", [128, 512])
    ps_s1 = ps_A[0]
    f_psn3 = [ps_A[0][0], ps_A[0][1], ps_Z[0]]
    sc_ps = ps_Z[0][:, 0:8]
    NCI = FSH // 128
    f_written = [0, 0]
    F_thr = [0, 0, 0]

    def emit_fag(s_):
        p.wait(GP, "fz0", p.cnt["fz0"])
        p.wait(GP, "fzg", p.cnt.get("fzg", 0))
        if SIM_LOCAL:
            for r_ in range(4):
                p.inc(GP.dma_start(out=agF_s[s_].ap()[r_ * 128:(r_ + 1) * 128, :],
                                   in_=agF_in.ap()[s_ * 128:(s_ + 1) * 128, :]), "cc", 16)
            GP.wait_ge(p.sems["cc"], p.cnt["cc"])
        else:
            p.inc(GP.collective_compute(
                "AllGather", ALU.bypass, replica_groups=[[0, 2, 4, 6], [1, 3, 5, 7]],
                ins=[agF_in.ap()[s_ * 128:(s_ + 1) * 128, :].opt()],
                outs=[agF_s[s_].ap().opt()]), "cc", 1)
        F_thr[s_] = p.cnt["cc"]

    def fwd_first_half(par, fpar, rhs_lo, rhs_hi, fld_fn, st_qi, stp=None, wbuf=None):
        o = {}
        W, wres = (wsb[par], f"wsb{par}") if wbuf is None else wbuf
        A0, A1 = ps_A[par]
        p.wait(TE, "qs", consts_done)
        p.claim(TE, f"psA{par}")
        TE.matmul(A0[:, :], w128c_s[0:64, 0:128], rhs_lo, start=True, stop=True)
        m1 = p.inc(TE.matmul(A1[:, :], w128c_s[0:64, 128:256], rhs_lo, start=True, stop=True), "te")
        p.wait(ACT, "te", m1)
        p.claim(ACT, f"s1b{par}")
        ACT.activation(s1b[par][:, 0:512], A0[:, :], AF.Copy)
        e1 = p.inc(ACT.activation(s1b[par][:, 512:1024], A1[:, :], AF.Copy), "ac")
        p.wait(TE, "ac", e1)
        TE.matmul(A0[:, :], w128c_s[64:128, 0:128], rhs_hi, start=True, stop=True,
                  tile_position=(64, 0))
        m2 = p.inc(TE.matmul(A1[:, :], w128c_s[64:128, 128:256], rhs_hi, start=True, stop=True,
                             tile_position=(64, 0)), "te")
        p.wait(ACT, "te", m2)
        ACT.activation(s1b[par][:, 1024:1536], A0[:, :], AF.Copy)
        e2 = p.inc(ACT.activation(s1b[par][:, 1536:2048], A1[:, :], AF.Copy), "ac")
        o["s1ev"] = e2
        p.use(f"psA{par}", "ac", e2)
        p.claim(DV, f"xtw{par}")
        p.claim(DV, f"twt{par}")
        tq = twt[par]
        tws = []
        for g in range(2):
            p.wait(DV, "ac", e1 if g == 0 else e2)
            re = s1b[par][:, g * 1024:g * 1024 + 512]
            im = s1b[par][:, g * 1024 + 512:g * 1024 + 1024]
            DV.tensor_tensor(tq[:, 0:512], re, twre_s, ALU.mult)
            DV.tensor_tensor(tq[:, 512:1024], im, twim_s, ALU.mult)
            DV.tensor_tensor(tq[:, 1024:1536], re, twim_s, ALU.mult)
            p.inc(DV.tensor_tensor(tq[:, 1536:2048], im, twre_s, ALU.mult), "dv")
            p.wait(DV, "dv", p.cnt["dv"])
            DV.tensor_tensor(xtw[par][:, g * 1024:g * 1024 + 512], tq[:, 0:512],
                             tq[:, 512:1024], ALU.subtract)
            tw = p.inc(DV.tensor_tensor(xtw[par][:, g * 1024 + 512:g * 1024 + 1024],
                                        tq[:, 1024:1536], tq[:, 1536:2048], ALU.add), "dv")
            p.wait(DV, "dv", tw)
            tws.append(tw)
        o["tw"] = tw
        p.use(f"s1b{par}", "dv", tw)
        p.use(f"twt{par}", "dv", tw)
        if stp:
            stp(2)
        tevs = []
        for half in range(2):
            p.wait(TE, "dv", tws[half])
            if half == 0:
                p.claim(TE, f"psT{par}")
            else:
                p.claim(TE, f"psA{par}")
            for pr in range(4):
                for pl in range(2):
                    t8 = pr * 2 + pl
                    sl = xtw[par][:, half * 1024 + pl * 512 + pr * 128:
                                  half * 1024 + pl * 512 + pr * 128 + 128]
                    if half == 0:
                        ins = TE.transpose(ps_T[par][:, ts(t8, 128)], sl, idb_s)
                    else:
                        ins = TE.matmul((A0 if t8 < 4 else A1)[:, ts(t8 % 4, 128)],
                                        sl, idb_s, start=True, stop=True)
            tra = p.inc(ins, "te")
            p.wait(ACT, "te", tra)
            if half == 0:
                p.claim(ACT, f"s2in{par}")
                tev = p.inc(ACT.activation(s2in[par][:, 0:1024],
                                           ps_T[par][:, :], AF.Copy), "ac")
                p.use(f"psT{par}", "ac", tev)
            else:
                ACT.activation(s2in[par][:, 1024:1536], A0[:, :], AF.Copy)
                tev = p.inc(ACT.activation(s2in[par][:, 1536:2048], A1[:, :], AF.Copy), "ac")
                p.use(f"psA{par}", "ac", tev)
            tevs.append(tev)
        o["s2ev"] = ("ac", tev)
        p.use(f"xtw{par}", "te", p.cnt["te"])
        lay = s2in[par][:, :].rearrange("p (c pr pl k) -> p c pr pl k", c=2, pr=4, pl=2)
        fmc = []
        for c2 in range(4):
            Z = ps_Z[par] if c2 % 2 == 0 else ps_Zb
            g, prh = c2 // 2, (c2 % 2) * 2
            rc = lay[:, g, prh:prh + 2, 0, :]
            ic = lay[:, g, prh:prh + 2, 1, :]
            if stp:
                stp(1)
            p.wait(TE, "ac", tevs[g])
            if c2 == 0:
                p.claim(TE, f"psZ{par}")
            elif c2 == 1:
                p.claim(TE, "psZb")
            else:
                p.wait(TE, "dv", fmc[c2 - 2])
            TE.matmul(Z[0:64, 0:256], w64_s["W64re"][0:64, :], rc[0:64], start=True, stop=False)
            TE.matmul(Z[0:64, 0:256], w64_s["W64imn"][0:64, :], ic[0:64], start=False, stop=True)
            TE.matmul(Z[64:128, 0:256], w64_s["W64re"][64:128, :], rc[64:128], start=True, stop=False,
                      tile_position=(64, 64))
            TE.matmul(Z[64:128, 0:256], w64_s["W64imn"][64:128, :], ic[64:128], start=False, stop=True,
                      tile_position=(64, 64))
            TE.matmul(Z[0:64, 256:512], w64_s["W64im"][0:64, :], rc[0:64], start=True, stop=False)
            TE.matmul(Z[0:64, 256:512], w64_s["W64re"][0:64, :], ic[0:64], start=False, stop=True)
            TE.matmul(Z[64:128, 256:512], w64_s["W64im"][64:128, :], rc[64:128], start=True, stop=False,
                      tile_position=(64, 64))
            zmm = p.inc(TE.matmul(Z[64:128, 256:512], w64_s["W64re"][64:128, :], ic[64:128],
                                  start=False, stop=True, tile_position=(64, 64)), "te")
            p.wait(DV, "te", zmm)
            fw = fld_fn(c2) if fld_fn is not None else None
            wof = c2 * 512
            tq = twt[par]
            p.claim(DV, f"twt{par}")
            if c2 == 0:
                p.claim(DV, wres)
            if fw is not None:
                fr = fre_sb[par][:, 0:256].rearrange("p (w k) -> p w k", w=2) if False else None
                fre = fre_sb[fpar][:, c2 * 512:c2 * 512 + 256]
                fim = fre_sb[fpar][:, c2 * 512 + 256:c2 * 512 + 512]
                DV.tensor_tensor(tq[:, 0:256], Z[:, 0:256], fre, ALU.mult)
                DV.tensor_tensor(tq[:, 256:512], Z[:, 256:512], fim, ALU.mult)
                DV.tensor_tensor(tq[:, 512:768], Z[:, 0:256], fim, ALU.mult)
                p.inc(DV.tensor_tensor(tq[:, 768:1024], Z[:, 256:512], fre, ALU.mult), "dv")
                p.wait(DV, "dv", p.cnt["dv"])
                DV.tensor_tensor(W[:, wof:wof + 256], tq[:, 0:256], tq[:, 256:512], ALU.subtract)
                fm = p.inc(DV.tensor_tensor(W[:, wof + 256:wof + 512], tq[:, 512:768],
                                            tq[:, 768:1024], ALU.add), "dv")
                p.wait(DV, "dv", fm)
                p.use(f"twt{par}", "dv", fm)
                p.use("fre0", "dv", fm) if c2 == 3 else None
            else:
                DV.tensor_copy(W[:, wof:wof + 256], Z[:, 0:256])
                fm = p.inc(DV.tensor_copy(W[:, wof + 256:wof + 512], Z[:, 256:512]), "dv")
            fmc.append(fm)
        o["fm01"] = fmc[1]
        o["fm"] = fmc[3]
        p.use(f"psZ{par}", "dv", fmc[2])
        p.use("psZb", "dv", fmc[3])
        p.use(f"s2in{par}", "te", p.cnt["te"])
        return o

    def bwd_second_half(par, gpar, gate_to_y, qi, st, gld, o, stp=None):
        A0, A1 = ps_A[par]
        i1es = []
        for cp in range(2):
            if stp:
                stp(1)
            p.wait(TE, "dv", o["fm01"] if cp == 0 else o["fm"])
            p.claim(TE, f"psA{par}")
            for c2 in (2 * cp, 2 * cp + 1):
                wof = c2 * 512
                wre = wsb[par][:, wof:wof + 256].rearrange("p (pr k) -> p pr k", pr=2)
                wim = wsb[par][:, wof + 256:wof + 512].rearrange("p (pr k) -> p pr k", pr=2)
                co = (c2 % 2) * 256
                TE.matmul(A0[0:64, co:co + 256], w64_s["W64ire"][0:64, :], wre[0:64], start=True, stop=False)
                TE.matmul(A0[0:64, co:co + 256], w64_s["W64iimn"][0:64, :], wim[0:64], start=False, stop=True)
                TE.matmul(A0[64:128, co:co + 256], w64_s["W64ire"][64:128, :], wre[64:128], start=True,
                          stop=False, tile_position=(64, 64))
                TE.matmul(A0[64:128, co:co + 256], w64_s["W64iimn"][64:128, :], wim[64:128], start=False,
                          stop=True, tile_position=(64, 64))
                TE.matmul(A1[0:64, co:co + 256], w64_s["W64iim"][0:64, :], wre[0:64], start=True, stop=False)
                TE.matmul(A1[0:64, co:co + 256], w64_s["W64ire"][0:64, :], wim[0:64], start=False, stop=True)
                TE.matmul(A1[64:128, co:co + 256], w64_s["W64iim"][64:128, :], wre[64:128], start=True,
                          stop=False, tile_position=(64, 64))
                i1m = p.inc(TE.matmul(A1[64:128, co:co + 256], w64_s["W64ire"][64:128, :], wim[64:128],
                                      start=False, stop=True, tile_position=(64, 64)), "te")
            p.wait(ACT, "te", i1m)
            if cp == 0:
                p.claim(ACT, f"i1b{par}")
            ACT.activation(i1b[par][:, cp * 1024:cp * 1024 + 512], A0[:, :], AF.Copy)
            i1e = p.inc(ACT.activation(i1b[par][:, cp * 1024 + 512:cp * 1024 + 1024], A1[:, :], AF.Copy), "ac")
            p.use(f"psA{par}", "ac", i1e)
            i1es.append(i1e)
        p.claim(DV, f"i1t{par}")
        tq = twt[par]
        ctws = []
        for cp in range(2):
            p.wait(DV, "ac", i1es[cp])
            re = i1b[par][:, cp * 1024:cp * 1024 + 512]
            im = i1b[par][:, cp * 1024 + 512:cp * 1024 + 1024]
            ctr = ctre_s[:, 0:512]
            cti = ctim_s[:, 0:512]
            p.claim(DV, f"twt{par}")
            DV.tensor_tensor(tq[:, 0:512], re, ctr, ALU.mult)
            DV.tensor_tensor(tq[:, 512:1024], im, cti, ALU.mult)
            DV.tensor_tensor(tq[:, 1024:1536], re, cti, ALU.mult)
            p.inc(DV.tensor_tensor(tq[:, 1536:2048], im, ctr, ALU.mult), "dv")
            p.wait(DV, "dv", p.cnt["dv"])
            DV.tensor_tensor(i1t[par][:, cp * 1024:cp * 1024 + 512], tq[:, 0:512],
                             tq[:, 512:1024], ALU.subtract)
            ctw = p.inc(DV.tensor_tensor(i1t[par][:, cp * 1024 + 512:cp * 1024 + 1024],
                                         tq[:, 1024:1536], tq[:, 1536:2048], ALU.add), "dv")
            p.wait(DV, "dv", ctw)
            p.use(f"twt{par}", "dv", ctw)
            ctws.append(ctw)
        p.use(f"i1b{par}", "dv", ctw)
        if stp:
            stp(2)
        for half in range(2):
            p.wait(TE, "dv", ctws[half])
            if half == 0:
                p.claim(TE, f"psT{par}")
            else:
                p.claim(TE, f"psA{par}")
            for i in range(4):
                c2 = half * 2 + i // 2
                pr = i % 2
                cp, co = c2 // 2, (c2 % 2) * 256
                for pl in range(2):
                    t8 = i * 2 + pl
                    sl = i1t[par][:, cp * 1024 + pl * 512 + co + pr * 128:
                                  cp * 1024 + pl * 512 + co + pr * 128 + 128]
                    if half == 0:
                        ins = TE.transpose(ps_T[par][:, ts(t8, 128)], sl, idb_s)
                    else:
                        ins = TE.matmul((A0 if t8 < 4 else A1)[:, ts(t8 % 4, 128)],
                                        sl, idb_s, start=True, stop=True)
            trb = p.inc(ins, "te")
            p.wait(ACT, "te", trb)
            if half == 0:
                p.claim(ACT, f"s3in{par}")
                s3e = p.inc(ACT.activation(s3in[par][:, 0:1024],
                                           ps_T[par][:, :], AF.Copy), "ac")
                p.use(f"psT{par}", "ac", s3e)
            else:
                ACT.activation(s3in[par][:, 1024:1536], A0[:, :], AF.Copy)
                s3e = p.inc(ACT.activation(s3in[par][:, 1536:2048], A1[:, :], AF.Copy), "ac")
                p.use(f"psA{par}", "ac", s3e)
        p.use(f"i1t{par}", "te", p.cnt["te"])
        p.wait(TE, "ac", s3e)
        p.claim(TE, f"psZ{par}")
        Z = ps_Z[par]
        lay3 = s3in[par][:, :].rearrange("p (h i pl n) -> p h i pl n", h=2, i=4, pl=2)
        TE.matmul(Z[0:64, :], w128i_s[:, 0:64], lay3[:, 0, :, 0, :], start=True, stop=False)
        TE.matmul(Z[0:64, :], w128i_s[:, 64:128], lay3[:, 0, :, 1, :], start=False, stop=True)
        TE.matmul(Z[64:128, :], w128i_s[:, 0:64], lay3[:, 1, :, 0, :], start=True, stop=False,
                  tile_position=(0, 64))
        invsb = p.inc(TE.matmul(Z[64:128, :], w128i_s[:, 64:128], lay3[:, 1, :, 1, :],
                                start=False, stop=True, tile_position=(0, 64)), "te")
        p.use(f"s3in{par}", "te", invsb)
        p.wait(DV, "te", invsb)
        p.wait(DV, f"gl{gpar}")
        if gate_to_y:
            g = p.inc(DV.tensor_tensor(y_sb[:, ts(qi, 512)], Z[:, :], gt_sb[gpar][:, :], ALU.mult), "dv")
        else:
            p.claim(DV, "att0")
            g = p.inc(DV.tensor_tensor(att_sb[gpar][:, :], Z[:, :], gt_sb[gpar][:, :], ALU.mult), "dv")
            p.wait(SY, "dv", g)
            for half in range(2):
                dst = bass.AP(agA_in, (16 * qi + 8 * half) * T, [[64, 64], [T, 8], [1, 64]])
                p.inc(SY.dma_start(out=dst,
                                   in_=att_sb[gpar][64 * half:64 * half + 64, :]
                                   .rearrange("p (c n) -> p c n", c=8)), f"as{gpar}", 16)
            p.use("att0", f"as{gpar}", p.cnt[f"as{gpar}"])
        p.use(f"psZ{par}", "dv", g)
        p.use("gt0", "dv", g)
        return g


    def gating_unit(tt):
        p.claim(TE, "psZ0")
        for kt in range(8):
            ins = TE.matmul(sc_ps, xnhT[:, kt * TH + tt * 128: kt * TH + tt * 128 + 128],
                            gw_sb[:, kt * 8:kt * 8 + 8], start=(kt == 0), stop=(kt == 7))
        scm = p.inc(ins, "te")
        p.wait(DV, "te", scm)
        p.claim(DV, "gst")
        p.inc(DV.reduce_max(gst[:, 0:1], sc_ps, axis=AX.X, negate=True), "dv")
        p.wait(ACT, "dv", p.cnt["dv"])
        p.claim(ACT, "esb")
        p.inc(ACT.activation(esb[:, :], sc_ps, AF.Exp, bias=gst[:, 0:1]), "ac")
        p.use("psZ0", "ac", p.cnt["ac"])
        p.use("gst", "ac", p.cnt["ac"])
        p.wait(DV, "ac", p.cnt["ac"])
        p.inc(DV.reduce_max(gst[:, 1:2], esb[:, :], axis=AX.X), "dv")
        p.wait(DV, "dv", p.cnt["dv"])
        p.inc(DV.tensor_scalar(msk[:, 0:8], esb[:, :], gst[:, 1:2], None, ALU.is_equal), "dv")
        p.wait(DV, "dv", p.cnt["dv"])
        p.inc(DV.scalar_tensor_tensor(msk[:, 8:16], msk[:, 0:8], -2.0, esb[:, :], ALU.mult, ALU.add), "dv")
        p.wait(DV, "dv", p.cnt["dv"])
        p.inc(DV.reduce_max(gst[:, 2:3], msk[:, 8:16], axis=AX.X), "dv")
        p.wait(DV, "dv", p.cnt["dv"])
        p.inc(DV.tensor_scalar(msk[:, 8:16], msk[:, 8:16], gst[:, 2:3], None, ALU.is_equal), "dv")
        p.inc(DV.tensor_tensor(gst[:, 3:4], gst[:, 1:2], gst[:, 2:3], ALU.add), "dv")
        p.wait(DV, "dv", p.cnt["dv"])
        p.inc(DV.reciprocal(gst[:, 4:5], gst[:, 3:4]), "dv")
        p.wait(DV, "dv", p.cnt["dv"])
        p.inc(DV.tensor_tensor(gst[:, 5:6], gst[:, 1:2], gst[:, 4:5], ALU.mult), "dv")
        p.inc(DV.tensor_tensor(gst[:, 6:7], gst[:, 2:3], gst[:, 4:5], ALU.mult), "dv")
        p.wait(DV, "dv", p.cnt["dv"])
        p.inc(DV.tensor_scalar(combf[:, :], msk[:, 0:8], gst[:, 5:6], None, ALU.mult), "dv")
        p.wait(DV, "dv", p.cnt["dv"])
        p.inc(DV.scalar_tensor_tensor(combf[:, :], msk[:, 8:16], gst[:, 6:7], combf[:, :],
                                      ALU.mult, ALU.add), "dv")
        p.wait(DV, "dv", p.cnt["dv"])
        cb = p.inc(DV.tensor_copy(comb_tok[:, tt * 8:tt * 8 + 8], combf[:, :]), "dv")
        p.use("gst", "dv", cb)
        p.use("esb", "dv", cb)
        if "comb" in dbg:
            p.wait(GP, "dv", cb)
            p.inc(GP.dma_start(out=dbg["comb"][:, tt * 8:tt * 8 + 8], in_=combf[:, :]), "qg", 16)

    def fproj_gen():
        for kt_ in range(8):
            p.inc(GP.dma_start(out=wf_sb[:, ts(kt_, FSH)],
                               in_=Wf_c[kt_ * 128:(kt_ + 1) * 128, :]), "wf", 16)
        p.wait(TE, "wf")
        pos_used = {}
        fpk = 0
        for tc in range(8):
            for kt_ in range(8):
                lb = fpk % 2
                if fpk >= 2:
                    p.wait(GP, "te", pos_used[fpk - 2])
                p.inc(GP.dma_start(out=pos_sb[:, ts(lb, 512)],
                                   in_=CT["posT"][kt_ * 128:(kt_ + 1) * 128, ts(tc, 512)]),
                      f"po{lb}", 16)
                p.wait(TE, f"po{lb}")
                if tc == 0 and kt_ == 0:
                    p.claim(TE, "psA0")
                    p.claim(TE, "psZ0")
                elif kt_ == 0:
                    p.wait(TE, "ac", p.cnt["ac"])
                for ci_ in range(NCI):
                    ins = TE.matmul(f_psn3[ci_][:, :],
                                    wf_sb[:, kt_ * FSH + ci_ * 128: kt_ * FSH + ci_ * 128 + 128],
                                    pos_sb[:, ts(lb, 512)],
                                    start=(kt_ == 0), stop=(kt_ == 7))
                pos_used[fpk] = p.inc(ins, "te")
                fpk += 1
                yield
            mm = p.cnt["te"]
            p.wait(ACT, "te", mm)
            for ci_ in range(NCI):
                p.claim(ACT, "f_ev")
                fpv = p.inc(ACT.activation(f_ev[:, :], f_psn3[ci_][:, :],
                                           AF.Identity, bias=bf_s[:, ci_:ci_ + 1]), "ac")
                p.wait(SY, "ac", fpv)
                fst = p.inc(SY.dma_start(out=f_hbm.ap()[ci_ * 128:(ci_ + 1) * 128, ts(tc, 512)],
                                         in_=f_ev[:, :]), "fs0", 16)
                p.use("f_ev", "fs0", fst)
            yield
        p.use("psA0", "ac", p.cnt["ac"])
        p.use("psZ0", "ac", p.cnt["ac"])
        f_written[0] = p.cnt["fs0"]
        if "fhbm" in dbg:
            p.wait(GP, "fs0", f_written[0])
            p.inc(GP.dma_start(out=dbg["fhbm"], in_=f_hbm.ap()), "qg", 16)

    def fprep_gen():
        p.wait(ACT, "fs0", f_written[0])
        for qi_ in range(FSH // 16):
            row0 = 16 * qi_
            p.claim(ACT, "fin_sb")
            for half in range(2):
                fsrc = bass.AP(f_hbm, (row0 + 8 * half) * T, [[64, 64], [T, 8], [1, 64]])
                p.inc(ACT.dma_start(out=fin_sb[64 * half:64 * half + 64, :]
                                    .rearrange("p (c n) -> p c n", c=8), in_=fsrc), "fi0", 16)
            p.wait(TE, "fi0")
            wb = (wsb[0], "wsb0") if qi_ % 2 == 0 else (s3in[0], "s3in0")
            fq, fsem = GP, ("fz0" if qi_ % 2 == 0 else "fzg")
            o = fwd_first_half(0, 0, fin_sb[0:64, :], fin_sb[64:128, :], None, None,
                               wbuf=wb)
            p.use("fin_sb", "te", p.cnt["te"])
            p.wait(fq, "dv", o["fm"])
            for c2 in range(4):
                for pl in range(2):
                    for half in range(2):
                        dst = bass.AP(agF_in, (row0 + 4 * c2 + half) * FROW + pl * 8192,
                                      [[128, 64], [2 * FROW, 2], [1, 128]])
                        ins = fq.dma_start(out=dst,
                                           in_=wb[0][64 * half:64 * half + 64,
                                                     c2 * 512 + pl * 256:c2 * 512 + pl * 256 + 256]
                                           .rearrange("p (pr k) -> p pr k", pr=2))
                        p.inc(ins, fsem, 16)
            p.use(wb[1], fsem, p.cnt[fsem])
            if qi_ == 7:
                emit_fag(0)
            yield

    def _pre_units():
        yield from fproj_gen()
        yield from fprep_gen()
    _preit = _pre_units()
    _pre_done = [False]

    def step_pre(n=1):
        if _pre_done[0]:
            return
        for _ in range(n):
            try:
                next(_preit)
            except StopIteration:
                _pre_done[0] = True
                return

    xnT = p.sbuf("xnT", [128, 8 * T], BF16, scope=scopeA)

    ln_xt = p.sbuf("ln_xt", [128, 2 * D], F32, scope=scopeA)
    ln_sq = p.sbuf("ln_sq", [128, D], F32, scope=scopeA)
    ln_st = p.sbuf("ln_st", [128, 16], F32, scope=scopeA)
    ln_nb = p.sbuf("ln_nb", [128, 2 * D], BF16, scope=scopeA)
    ln_ps = p.psum("ln_ps", [128, 1024], BF16, scope=scopeA)

    def ln_phase(src_ap, n_tok, dstT, tag, stp=None):
        xt, sq, stat, xnb, pst = ln_xt, ln_sq, ln_st, ln_nb, ln_ps
        ntile = n_tok // 128
        tr_done, ev_done, tr_evac = {}, {}, {}
        for it in range(ntile):
            if stp:
                stp()
            bo = (it % 2) * D
            so = (it % 2) * 8
            xs = xt[:, bo:bo + D]
            qx = f"lx{tag}{it % 2}"
            if it >= 2:
                p.wait(SY, "ac", ev_done[it - 2])
            p.inc(SY.dma_start(out=xs, in_=src_ap[it * 128:(it + 1) * 128, :]), qx, 16)
            p.wait(DV, qx)
            if it >= 2:
                p.wait(DV, "ac", ev_done[it - 2])
            p.inc(DV.reduce_sum(stat[:, so + 0:so + 1], xs, axis=AX.X), "dv")
            p.wait(ACT, qx)
            if it >= 2:
                p.wait(ACT, "dv", p.cnt["dv"])
            p.inc(ACT.activation(sq[:, :], xs, AF.Square), "ac")
            p.wait(DV, "ac", p.cnt["ac"])
            p.inc(DV.reduce_sum(stat[:, so + 1:so + 2], sq[:, :], axis=AX.X), "dv")
            p.wait(DV, "dv", p.cnt["dv"])
            p.inc(DV.tensor_scalar(stat[:, so + 2:so + 3], stat[:, so + 0:so + 1], 1.0 / D, None, ALU.mult), "dv")
            p.inc(DV.tensor_scalar(stat[:, so + 3:so + 4], stat[:, so + 1:so + 2], 1.0 / D, None, ALU.mult), "dv")
            p.wait(DV, "dv", p.cnt["dv"])
            p.inc(DV.tensor_tensor(stat[:, so + 4:so + 5], stat[:, so + 2:so + 3], stat[:, so + 2:so + 3], ALU.mult), "dv")
            p.wait(DV, "dv", p.cnt["dv"])
            p.inc(DV.tensor_tensor(stat[:, so + 5:so + 6], stat[:, so + 3:so + 4], stat[:, so + 4:so + 5], ALU.subtract), "dv")
            p.wait(ACT, "dv", p.cnt["dv"])
            p.inc(ACT.activation(stat[:, so + 6:so + 7], stat[:, so + 5:so + 6], AF.Ln, bias=eps_s[:, :]), "ac")
            p.wait(ACT, "ac", p.cnt["ac"])
            p.inc(ACT.activation(stat[:, so + 6:so + 7], stat[:, so + 6:so + 7], AF.Exp, scale=-0.5), "ac")
            p.wait(DV, "ac", p.cnt["ac"])
            p.inc(DV.tensor_tensor(stat[:, so + 7:so + 8], stat[:, so + 2:so + 3], stat[:, so + 6:so + 7], ALU.mult), "dv")
            p.wait(DV, "dv", p.cnt["dv"])
            p.inc(DV.tensor_scalar(stat[:, so + 7:so + 8], stat[:, so + 7:so + 8], -1.0, None, ALU.mult), "dv")
            p.wait(ACT, "dv", p.cnt["dv"])
            if it >= 2:
                p.wait(ACT, "te", tr_done[it - 2])
            xnbs = xnb[:, bo:bo + D]
            ev_done[it] = p.inc(ACT.activation(xnbs, xs, AF.Identity,
                                               bias=stat[:, so + 7:so + 8],
                                               scale=stat[:, so + 6:so + 7]), "ac")
            p.wait(TE, "ac", p.cnt["ac"])
            p.wait(TE, "qs", consts_done)
            if it >= 1:
                p.wait(TE, "dv", tr_evac[it - 1])
            for j in range(8):
                ins = TE.transpose(pst[:, ts(j, 128)], xnbs[:, ts(j, 128)], idb_s)
            tr_done[it] = p.inc(ins, "te")
            p.wait(DV, "te", tr_done[it])
            tr_evac[it] = p.inc(DV.tensor_copy(
                dstT[:, :].rearrange("p (j tt) -> p j tt", j=8)[:, :, ts(it, 128)],
                pst[:, :].rearrange("p (j tt) -> p j tt", j=8)), "dv")
        return tr_evac[ntile - 1]

    ln_a_done = ln_phase(Xb, T, xnT, "a", stp=lambda: step_pre(2))
    p.wait(SY, "te", p.cnt["te"])
    p.wait(SY, "dv", p.cnt["dv"])
    p.wait(SY, "ac", p.cnt["ac"])
    wp_full = p.sbuf("wp_full", [128, 8 * 2048], BF16, scope=scopeA)
    for kt in range(8):
        p.inc(GP.dma_start(out=wp_full[:, ts(kt, 2048)], in_=Wp_c[kt * 128:(kt + 1) * 128, :]), "wp", 16)
    wp_ld = p.cnt["wp"]
    p.inc(GP.dma_start(out=gw_sb[:, :].rearrange("p (k e) -> p k e", k=8),
                       in_=gwT.rearrange("(k p) e -> p k e", p=128)), "gw", 16)
    ln_h_done = ln_phase(Xh, TH, xnhT, "h", stp=lambda: step_pre(1))

    if "xnT" in dbg:
        p.wait(GP, "dv", ln_a_done)
        p.inc(GP.dma_start(out=dbg["xnT"].rearrange("(j p) tt -> p j tt", p=128),
                           in_=xnT[:, :].rearrange("p (j tt) -> p j tt", j=8)), "qg", 16)
    if "xnhT" in dbg:
        p.wait(GP, "dv", ln_h_done)
        p.inc(GP.dma_start(out=dbg["xnhT"].rearrange("(j p) tt -> p j tt", p=128),
                           in_=xnhT[:, :].rearrange("p (j tt) -> p j tt", j=8)), "qg", 16)

    def gat_gen():
        p.wait(TE, "gw", p.cnt["gw"])
        p.wait(TE, "dv", ln_h_done)
        for tt_ in range(16):
            gating_unit(tt_)
            yield
    _gatit = gat_gen()
    _gat_done = [False]

    def step_gat(n=1):
        if _gat_done[0]:
            return
        for _ in range(n):
            try:
                next(_gatit)
            except StopIteration:
                _gat_done[0] = True
                return


    xp_ps = [p.psum(f"xp_ps{i}", [128, 512], scope=scopeA) for i in range(2)]
    xp_ev = p.sbuf("xp_ev", [128, 2 * 512], BF16, scope=scopeA)
    p.wait(TE, "wp", wp_ld)
    p.wait(TE, "dv", ln_a_done)
    xp_ev_done = {}
    for ci in range(16):
        for tc in range(8):
            idx = ci * 8 + tc
            if idx % 4 == 0:
                step_pre(1)
            if idx % 8 == 4:
                step_gat(1)
            pb = idx % 2
            if idx >= 2:
                p.wait(TE, "ac", xp_ev_done[idx - 2])
                p.wait(ACT, f"xs{pb}", p.cnt[f"xs{pb}"])
            for kt in range(8):
                ins = TE.matmul(xp_ps[pb][:, :],
                                wp_full[:, kt * 2048 + ci * 128: kt * 2048 + ci * 128 + 128],
                                xnT[:, kt * T + tc * 512: kt * T + (tc + 1) * 512],
                                start=(kt == 0), stop=(kt == 7))
            mm = p.inc(ins, "te")
            p.wait(ACT, "te", mm)
            xp_ev_done[idx] = p.inc(
                ACT.activation(xp_ev[:, ts(pb, 512)], xp_ps[pb][:, :],
                               AF.Identity, bias=bp_s[:, ci:ci + 1]), "ac")
            xq = SY if pb == 0 else ACT
            p.wait(xq, "ac", xp_ev_done[idx])
            p.inc(xq.dma_start(out=xp_hbm.ap()[ci * 128:(ci + 1) * 128, ts(tc, 512)],
                               in_=xp_ev[:, ts(pb, 512)]), f"xs{pb}", 16)
    xp_written = (p.cnt["xs0"], p.cnt["xs1"])
    if "xp" in dbg:
        p.wait(GP, "xs0", xp_written[0])
        p.wait(GP, "xs1", xp_written[1])
        p.inc(GP.dma_start(out=dbg["xp"], in_=xp_hbm.ap()), "qg", 16)

    while not (_pre_done[0] and _gat_done[0]):
        step_pre(2)
        step_gat(2)
    comb_done = p.cnt["dv"]

    if BUILD_UNTIL == "p2":
        _finish_stub([scopeA])
        p.ctx.close()
        return nc
    full_barrier()
    scopeA.close()
    if BUILD_UNTIL == "p3proj":
        _finish_stub([scopeB])
        p.ctx.close()
        return nc
    out_fm = p.sbuf("out_fm", [128, 8 * TH], BF16)
    y_sb = p.sbuf("y_sb", [128, NQ * 512], BF16, scope=scopeB)
    fre_one = p.sbuf("fre_sb0", [128, 2048], BF16, scope=scopeB)
    fre_sb = [fre_one, fre_one]
    gt_one = p.sbuf("gt_sb0", [128, 512], BF16, scope=scopeB)
    gt_sb = [gt_one, gt_one]
    att_one = f_ev
    att_sb = [att_one, att_one]
    p.wait(ACT, "xs0", xp_written[0])
    p.wait(ACT, "xs1", xp_written[1])
    y0_q = {}

    def emit_y0_quarter(k):
        for qi in range(8 * k, 8 * k + 8):
            for half in range(2):
                ysrc = bass.AP(xp_hbm, (16 * qi + 8 * half) * T, [[64, 64], [T, 8], [1, 64]])
                p.inc(ACT.dma_start(out=y_sb[64 * half:64 * half + 64, ts(qi, 512)]
                                    .rearrange("p (c n) -> p c n", c=8), in_=ysrc), f"y0q{k}", 16)
        y0_q[k] = p.cnt[f"y0q{k}"]

    emit_y0_quarter(0)

    if "Fsh" in dbg:
        p.wait(GP, "fz0", p.cnt["fz0"])
        p.inc(GP.dma_start(out=dbg["Fsh"], in_=agF_in.ap()), "qg", 16)

    emit_fag(1)
    emit_fag(2)
    F_ready = p.cnt["cc"]
    p.wait(ACT, "cc", F_thr[0])

    if BUILD_UNTIL == "gating":
        _finish_stub([scopeC])
        p.ctx.close()
        return nc

    w1f_sb = p.sbuf("w1f_sb", [128, 8 * 2816], BF16, scope=scopeB)
    w2_sb = p.sbuf("w2_sb", [128, 11 * 1024], BF16, scope=scopeB)
    m_sb = fscr[:, 0:2816]
    sg_sb = fscr[:, 2816:3328]
    crep_sb = fscr[:, 3328:3840]
    comb_ct = fscr[0:1, 4608:4864]
    ps_gu = [p.psum(f"ps_gu{i}", [128, 512], scope=scopeB) for i in range(2)]
    ps_o2 = [p.psum("ps_o2_0", [128, 512], scope=scopeB)]

    def ffn_feeder():
        for s_ in ["te", "dv", "ac"]:
            p.wait(GP, s_)
        def comb_dma(e_, ci_):
            p.claim(GP, "comb_ct")
            for h in range(2):
                cc = 2 * ci_ + h
                p.inc(GP.dma_start(out=comb_ct[0:1, h * 128:(h + 1) * 128],
                                   in_=comb_tok[:, cc * 8 + e_:cc * 8 + e_ + 1]), "cr", 16)
            return p.cnt["cr"]

        for e in range(E):
            p.wait(GP, "dv", comb_done)
            cr_next = comb_dma(e, 0)
            p.claim(GP, "w1f_sb")
            for kt in range(8):
                p.inc(GP.dma_start(out=w1f_sb[:, ts(kt, 2816)],
                                   in_=w1[e, kt * 128:(kt + 1) * 128, :]), "w1s", 16)
            w1r = p.cnt["w1s"]
            p.claim(GP, "w2_sb")
            for ft in range(11):
                p.inc(GP.dma_start(out=w2_sb[:, ts(ft, 1024)],
                                   in_=w2[e, ft * 128:(ft + 1) * 128, :]), "w2s", 16)
            w2r = p.cnt["w2s"]
            yield
            for ci in range(8):
                p.wait(TE, "cr", cr_next)
                p.claim(TE, "ps_gu0")
                crm = p.inc(TE.matmul(ps_gu[0][:, 0:256], ones_s[0:1, :],
                                      comb_ct[0:1, :],
                                      start=True, stop=True), "te")
                p.use("comb_ct", "te", crm)
                if ci < 7:
                    cr_next = comb_dma(e, ci + 1)
                p.wait(ACT, "te", crm)
                p.claim(ACT, f"crep_{ci % 2}")
                crv = p.inc(ACT.activation(crep_sb[:, ts(ci % 2, 256)], ps_gu[0][:, 0:256], AF.Copy), "ac")
                p.use("ps_gu0", "ac", crv)
                yield
                p.wait(TE, "w1s", w1r)
                for j in range(11):
                    jp = j % 2
                    p.claim(TE, f"ps_gu{jp}")
                    pg = ps_gu[jp][:, 0:256]
                    pu = ps_gu[jp][:, 256:512]
                    for kt in range(8):
                        TE.matmul(pg, w1f_sb[:, kt * 2816 + j * 128: kt * 2816 + j * 128 + 128],
                                  xnhT[:, kt * TH + ci * 256: kt * TH + ci * 256 + 256],
                                  start=(kt == 0), stop=(kt == 7))
                    for kt in range(8):
                        ins = TE.matmul(pu,
                                        w1f_sb[:, kt * 2816 + FFN + j * 128: kt * 2816 + FFN + j * 128 + 128],
                                        xnhT[:, kt * TH + ci * 256: kt * TH + ci * 256 + 256],
                                        start=(kt == 0), stop=(kt == 7))
                    gum = p.inc(ins, "te")
                    p.wait(ACT, "te", gum)
                    p.claim(ACT, f"sg_{jp}")
                    sgv = p.inc(ACT.activation(sg_sb[:, ts(jp, 256)], pg, AF.Silu), "ac")
                    p.wait(DV, "ac", sgv)
                    p.wait(DV, "ac", crv)
                    p.inc(DV.tensor_tensor(sg_sb[:, ts(jp, 256)], sg_sb[:, ts(jp, 256)], pu, ALU.mult), "dv")
                    p.use(f"ps_gu{jp}", "dv", p.cnt["dv"])
                    p.wait(DV, "dv", p.cnt["dv"])
                    p.claim(DV, f"m_{j}")
                    mj = p.inc(DV.tensor_tensor(m_sb[:, ts(j, 256)], sg_sb[:, ts(jp, 256)],
                                                crep_sb[:, ts(ci % 2, 256)], ALU.mult), "dv")
                    p.use(f"sg_{jp}", "dv", mj)
                    p.use(f"crep_{ci % 2}", "dv", mj)
                    yield
                if ci == 7:
                    p.use("w1f_sb", "te", p.cnt["te"])
                mdone = p.cnt["dv"]
                p.wait(TE, "dv", mdone)
                p.wait(TE, "w2s", w2r)
                for dtp in range(4):
                    po = ps_o2[0]
                    p.claim(TE, "ps_o2_0")
                    for half in range(2):
                        dt = 2 * dtp + half
                        sl = po[:, half * 256:half * 256 + 256]
                        for ft in range(11):
                            ins = TE.matmul(sl, w2_sb[:, ft * 1024 + dt * 128: ft * 1024 + dt * 128 + 128],
                                            m_sb[:, ts(ft, 256)],
                                            start=(ft == 0), stop=(ft == 10))
                    w2m = p.inc(ins, "te")
                    p.wait(DV, "te", w2m)
                    osl = out_fm[:, :].rearrange("p (dt tt2) -> p dt tt2", dt=8)[
                        :, 2 * dtp:2 * dtp + 2, ci * 256:(ci + 1) * 256]
                    p.claim(DV, f"ofm_{ci}_{dtp}")
                    pov = po[:, :].rearrange("p (h tt2) -> p h tt2", h=2)
                    if e == 0:
                        fv = p.inc(DV.tensor_copy(osl, pov), "dv")
                    else:
                        fv = p.inc(DV.tensor_tensor(osl, pov, osl, ALU.add), "dv")
                    p.use("ps_o2_0", "dv", fv)
                    p.use(f"ofm_{ci}_{dtp}", "dv", fv)
                    yield
                for j in range(11):
                    p.use(f"m_{j}", "te", p.cnt["te"])
            p.use("w2_sb", "te", p.cnt["te"])

    _feed = ffn_feeder()
    _feed_done = [False]

    def step_ffn(n=1):
        if _feed_done[0]:
            return
        for _ in range(n):
            try:
                next(_feed)
            except StopIteration:
                _feed_done[0] = True
                return

    step_ffn(1)

    A_thr = [0, 0]

    def emit_aag(part):
        r0, r1 = (0, 384) if part == 0 else (384, 512)
        dst = agA_a if part == 0 else agA_b
        nr = r1 - r0
        p.wait(GP, "as0", p.cnt["as0"])
        p.wait(GP, "as1", p.cnt["as1"])
        if SIM_LOCAL:
            for r_ in range(2):
                p.inc(GP.dma_start(out=dst.ap()[r_ * nr:(r_ + 1) * nr, :],
                                   in_=agA_in.ap()[r0:r1, :]), "cc", 16)
            GP.wait_ge(p.sems["cc"], p.cnt["cc"])
        else:
            p.inc(GP.collective_compute(
                "AllGather", ALU.bypass,
                replica_groups=[[0, 1], [2, 3], [4, 5], [6, 7]],
                ins=[agA_in.ap()[r0:r1, :].opt()], outs=[dst.ap().opt()]), "cc", 1)
        A_thr[part] = p.cnt["cc"]

    gate_done = {}
    for st in range(1, STAGES + 1):
        p.wait(DV, "qg", p.cnt["qg"])
        p.wait(ACT, "cc", F_thr[st - 1])
        for qi in range(NQ):
            par = qi % 2
            if st == 1 and qi in (2, 4, 6):
                emit_y0_quarter(qi // 2)
            p.claim(SY, "gt0")
            for half in range(2):
                row0 = st * 512 + 16 * qi + 8 * half
                gsrc = bass.AP(xp_hbm, row0 * T, [[64, 64], [T, 8], [1, 64]])
                p.inc(SY.dma_start(out=gt_sb[par][64 * half:64 * half + 64, :]
                                   .rearrange("p (c n) -> p c n", c=8), in_=gsrc), f"gl{par}", 16)
            gld = p.cnt[f"gl{par}"]
            p.claim(ACT, "fre0")
            for c2 in range(4):
                for pl in range(2):
                    for half in range(2):
                        row0 = 16 * qi + 4 * c2 + half
                        fsrc2 = bass.AP(agF_s[st - 1], row0 * FROW + pl * 8192,
                                        [[128, 64], [2 * FROW, 2], [1, 128]])
                        ins = ACT.dma_start(
                            out=fre_sb[par][64 * half:64 * half + 64,
                                            c2 * 512 + pl * 256:c2 * 512 + pl * 256 + 256]
                            .rearrange("p (pr k) -> p pr k", pr=2), in_=fsrc2)
                        p.inc(ins, f"fa{par}", 16)
            fldc = p.cnt[f"fa{par}"]
            if st > 1:
                p.wait(TE, "dv", gate_done[(st - 1, qi)])
            else:
                p.wait(TE, f"y0q{qi // 8}", y0_q[qi // 8])

            def mk_fld(par_, cnt_):
                def f(c2):
                    p.wait(DV, f"fa{par_}", cnt_)
                    return cnt_
                return f
            step_ffn(1)
            o = fwd_first_half(0, par, y_sb[0:64, ts(qi, 512)], y_sb[64:128, ts(qi, 512)],
                               mk_fld(par, fldc), (st, qi), stp=step_ffn)
            g = bwd_second_half(0, par, st < STAGES, qi, st, gld, o, stp=step_ffn)
            gate_done[(st, qi)] = g
            if st == STAGES and qi == 23:
                emit_aag(0)
    while not _feed_done[0]:
        step_ffn(8)

    emit_aag(1)
    A_ready = p.cnt["cc"]
    if "attn" in dbg:
        p.wait(GP, "as0", p.cnt["as0"])
        p.wait(GP, "as1", p.cnt["as1"])
        p.inc(GP.dma_start(out=dbg["attn"], in_=agA_in.ap()), "qg", 16)

    if BUILD_UNTIL == "conv":
        _finish_stub([scopeB])
        p.ctx.close()
        return nc

    if BUILD_UNTIL == "aag":
        _finish_stub([scopeB])
        p.ctx.close()
        return nc
    full_barrier()
    scopeB.close()
    scopeC.close()
    attn_fm = p.sbuf("attn_fm", [128, 8 * TH], BF16, scope=scopeD)
    idf_s = p.sbuf("idf_s", [128, 128], F32, scope=scopeD)
    p.inc(SY.dma_start(out=idf_s[:, :], in_=identf), "idf", 16)
    cnd_sb = [p.sbuf(f"cnd{i}", [128, TH], BF16, scope=scopeD) for i in range(2)]
    o1_sb = p.sbuf("o1_sb", [128, 1024], F32, scope=scopeD)
    xh_sb = [p.sbuf(f"xh{i}", [128, 1024], F32, scope=scopeD) for i in range(6)]
    outb = p.sbuf("outb", [128, 2 * 1024], F32, scope=scopeD)
    ps_pt = [p.psum(f"ps_pt{i}", [128, 512], scope=scopeD) for i in range(2)]
    p.wait(SY, "dv", p.cnt["dv"])
    p.wait(SY, "te", p.cnt["te"])
    acc_prev = {}
    for dt in range(8):
        hsrc = 0 if dt < 4 else 1
        for colh in range(2):
            par = (dt * 2 + colh) % 2
            ldq = SY if par == 0 else ACT
            p.wait(ldq, "cc", A_thr[1] if dt % 4 == 3 else A_thr[0])
            p.claim(ldq, f"cnd{par}")
            if dt % 4 == 3:
                asrc = agA_b.ap()[hsrc * 128:hsrc * 128 + 128, colh * TH:(colh + 1) * TH]
            else:
                asrc = agA_a.ap()[hsrc * 384 + (dt % 4) * 128:
                                  hsrc * 384 + (dt % 4) * 128 + 128,
                                  colh * TH:(colh + 1) * TH]
            p.inc(ldq.dma_start(out=cnd_sb[par][:, :], in_=asrc), f"cn{par}", 16)
            p.wait(DV, f"cn{par}")
            dstsl = attn_fm[:, dt * TH:(dt + 1) * TH]
            if colh == 0:
                a = p.inc(DV.tensor_scalar(dstsl, cnd_sb[par][:, :], cm_s[:, colh:colh + 1],
                                           None, ALU.mult), "dv")
            else:
                p.wait(DV, "dv", acc_prev[dt])
                a = p.inc(DV.scalar_tensor_tensor(dstsl, cnd_sb[par][:, :], cm_s[:, colh:colh + 1],
                                                  dstsl, ALU.mult, ALU.add), "dv")
            acc_prev[dt] = a
            p.use(f"cnd{par}", "dv", a)
    attnfm_done = p.cnt["dv"]
    if "attnfm" in dbg:
        p.wait(GP, "dv", attnfm_done)
        p.inc(GP.dma_start(out=dbg["attnfm"].rearrange("(j p) tt -> p j tt", p=128),
                           in_=attn_fm[:, :].rearrange("p (j tt) -> p j tt", j=8)), "qg", 16)

    xh_q = [SY, ACT, GP]

    def xh_load(tt_):
        s = tt_ % 6
        q = xh_q[s % 3]
        p.claim(q, f"xh{s}")
        p.inc(q.dma_start(out=xh_sb[s][:, :], in_=Xh[tt_ * 128:(tt_ + 1) * 128, :]), f"xh{s}", 16)

    for tt in range(6):
        xh_load(tt)
    out_done = {}
    for tt in range(16):
        s = tt % 6
        ob_p = tt % 2
        p.wait(DV, "dv", attnfm_done)
        p.claim(DV, "o1_sb")
        o1 = p.inc(DV.tensor_tensor(
            o1_sb[:, :].rearrange("p (dt t2) -> p dt t2", dt=8),
            out_fm[:, :].rearrange("p (dt tt2) -> p dt tt2", dt=8)[:, :, ts(tt, 128)],
            attn_fm[:, :].rearrange("p (dt tt2) -> p dt tt2", dt=8)[:, :, ts(tt, 128)],
            ALU.add), "dv")
        p.wait(TE, "dv", o1)
        p.wait(TE, "idf")
        p.claim(TE, "ps_pt")
        for dt in range(8):
            ins = TE.transpose(ps_pt[dt // 4][:, (dt % 4) * 128:(dt % 4) * 128 + 128],
                               o1_sb[:, ts(dt, 128)], idf_s[:, :])
        tpm = p.inc(ins, "te")
        p.use("o1_sb", "te", tpm)
        p.wait(DV, "te", tpm)
        p.wait(DV, f"xh{s}")
        p.claim(DV, f"outb{ob_p}")
        DV.tensor_tensor(outb[:, ob_p * 1024:ob_p * 1024 + 512], ps_pt[0][:, :],
                         xh_sb[s][:, 0:512], ALU.add)
        ob = p.inc(DV.tensor_tensor(outb[:, ob_p * 1024 + 512:ob_p * 1024 + 1024], ps_pt[1][:, :],
                                    xh_sb[s][:, 512:1024], ALU.add), "dv")
        p.use("ps_pt", "dv", ob)
        p.use(f"xh{s}", "dv", ob)
        oq, osem = (SY, "ostr") if tt % 2 == 0 else (ACT, "osta")
        p.wait(oq, "dv", ob)
        od = p.inc(oq.dma_start(out=out[tt * 128:(tt + 1) * 128, :],
                                in_=outb[:, ob_p * 1024:(ob_p + 1) * 1024]), osem, 16)
        p.use(f"outb{ob_p}", osem, od)
        out_done[osem] = od
        if tt + 6 < 16:
            xh_load(tt + 6)
    p.wait(GP, "qg", p.cnt["qg"])
    p.wait(SY, "ostr", out_done["ostr"])
    p.wait(SY, "osta", out_done["osta"])

    scopeD.close()
    p.ctx.close()
    return nc


def kernel(**inputs):
    maps = _shard_inputs(inputs)
    nc = build_program(debug=DEBUG_OUTS)
    res = run_bass_kernel_spmd(nc, maps, core_ids=list(range(8)))
    full = np.zeros((B, T, D), np.float32)
    for c in range(8):
        b, h = c >> 1, c & 1
        full[b, h * TH:(h + 1) * TH, :] = res.results[c]["out"]
    if DEBUG_OUTS:
        kernel.debug = [dict(res.results[c]) for c in range(8)]
    return full


# version 18
# speedup vs baseline: 1.0186x; 1.0186x over previous
import numpy as np
import ml_dtypes
from contextlib import ExitStack

import concourse.bass as bass
import concourse.mybir as mybir
from concourse.bass_utils import run_bass_kernel_spmd

F32 = mybir.dt.float32
BF16 = mybir.dt.bfloat16
AF = mybir.ActivationFunctionType
ALU = mybir.AluOpType
AX = mybir.AxisListType
bf16 = ml_dtypes.bfloat16
ts = bass.ts

B, T, D = 4, 4096, 1024
L = 2 * T
N1, N2 = 128, 64
STAGES = 3
E, FFN = 8, 1408
DH, TH = D // 2, T // 2
NQ = DH // 16
FSH = STAGES * DH // 4
FROW = 16384

DEBUG_OUTS = ()
BUILD_UNTIL = None
SIM_LOCAL = False


def _tables():
    n1 = np.arange(N1)[:, None]
    k1 = np.arange(N1)[None, :]
    n2 = np.arange(N2)[:, None]
    k2 = np.arange(N2)[None, :]
    W128 = np.exp(-2j * np.pi * n1 * k1 / N1)
    W64 = np.exp(-2j * np.pi * n2 * k2 / N2)
    W64i = np.exp(2j * np.pi * k2.T * n2.T / N2)
    W128i = np.exp(2j * np.pi * k1.T * n1.T / N1)
    tw = np.exp(-2j * np.pi * np.arange(N1)[:, None] * np.arange(N2)[None, :] / L)

    def b_(x):
        return np.ascontiguousarray(x).astype(np.float32).astype(bf16)

    t = {}
    w128c = np.concatenate([W128.real[:64, :], W128.imag[:64, :]], axis=1)
    t["W128c"] = b_(np.concatenate([w128c, w128c], axis=0))
    for nm, m in [("W64re", W64.real), ("W64im", W64.imag), ("W64imn", -W64.imag),
                  ("W64ire", W64i.real), ("W64iim", W64i.imag), ("W64iimn", -W64i.imag)]:
        t[nm] = b_(np.concatenate([m, m], axis=0))
    t["W128ire"] = b_(W128i.real[:, :64] / L)
    t["W128iimn"] = b_(-W128i.imag[:, :64] / L)
    t["twre"] = b_(np.tile(tw.real[:, None, :], (1, 8, 1)).reshape(128, 512))
    t["twim"] = b_(np.tile(tw.imag[:, None, :], (1, 8, 1)).reshape(128, 512))
    ctm = np.conj(tw).T
    ctp = np.concatenate([ctm, ctm], axis=0)
    t["ctre"] = b_(np.tile(ctp.real[:, None, :], (1, 4, 1)).reshape(128, 512))
    t["ctim"] = b_(np.tile(ctp.imag[:, None, :], (1, 4, 1)).reshape(128, 512))
    ident = np.eye(128, dtype=np.float32)
    t["identb"] = b_(ident)
    t["identf"] = ident
    t["ones1"] = b_(np.ones((1, 128), np.float32))
    pos_t = np.arange(T, dtype=np.float32)[:, None]
    i = np.arange(D)[None, :]
    angle = pos_t / np.power(np.float32(10000.0), (2 * (i // 2)).astype(np.float32) / D)
    pos = np.where(i % 2 == 0, np.sin(angle), np.cos(angle)).astype(np.float32)
    t["posT"] = b_(pos.T)
    return t


def _shard_inputs(ins):
    X = np.asarray(ins["L0_attn_X"])
    Wp = np.asarray(ins["L0_attn_Wp"])
    bp = np.asarray(ins["L0_attn_bp"])
    Wf = np.asarray(ins["L0_attn_Wf"])
    bff = np.asarray(ins["L0_attn_bf"])
    gw = np.asarray(ins["L0_ffn_gate_w"])
    w1 = np.asarray(ins["L0_ffn_w1"])
    w2 = np.asarray(ins["L0_ffn_w2"])
    tabs = _tables()
    maps = []
    for c in range(8):
        b, h = c >> 1, c & 1
        dcols = np.arange(h * DH, h * DH + DH)
        pcols = np.concatenate([s * D + dcols for s in range(STAGES + 1)])
        r = c >> 1
        s_ = np.repeat(np.arange(STAGES), 128)
        j_ = np.tile(np.arange(128), STAGES)
        fcols = s_ * D + h * DH + r * 128 + j_
        cmask = np.zeros((128, 16), np.float32)
        cmask[:, h] = 1.0
        m = {
            "Xb": np.ascontiguousarray(X[b]),
            "Xh": np.ascontiguousarray(X[b, h * TH:(h + 1) * TH]),
            "Wp_c": np.ascontiguousarray(Wp[:, pcols]),
            "bp_c": np.ascontiguousarray(bp[pcols]).reshape(16, 128).T.copy(),
            "Wf_c": np.ascontiguousarray(Wf[:, fcols]),
            "bf_c": np.ascontiguousarray(bff[fcols]).reshape(FSH // 128, 128).T.copy(),
            "gwT": np.ascontiguousarray(gw.T),
            "w1": w1,
            "w2": w2,
            "cmask": cmask,
        }
        m.update(tabs)
        maps.append(m)
    return maps


class P:
    def __init__(self):
        self.nc = bass.Bass("TRN2", target_bir_lowering=False)
        self.ctx = ExitStack()
        self.cnt = {}
        self.sems = {}
        self.lastw = {}
        self.res = {}

    def sem(self, name):
        self.sems[name] = self.ctx.enter_context(self.nc.semaphore(name))
        self.cnt[name] = 0

    def qsem(self, name):
        if name not in self.sems:
            self.sem(name)
        return name

    def inc(self, instr, name, n=1):
        self.qsem(name)
        instr.then_inc(self.sems[name], n)
        self.cnt[name] += n
        return self.cnt[name]

    def wait(self, eng, name, val=None):
        self.qsem(name)
        if val is None:
            val = self.cnt[name]
        if val <= 0:
            return
        key = (id(eng), name)
        if self.lastw.get(key, -1) >= val:
            return
        eng.wait_ge(self.sems[name], val)
        self.lastw[key] = val

    def use(self, resname, semname, count):
        d = self.res.setdefault(resname, {})
        d[semname] = max(d.get(semname, 0), count)

    def claim(self, eng, resname):
        for s, c in self.res.get(resname, {}).items():
            self.wait(eng, s, c)

    def sbuf(self, name, shape, dtype, scope=None):
        return (scope or self.ctx).enter_context(self.nc.sbuf_tensor(name, shape, dtype))

    def psum(self, name, shape, dtype=F32, scope=None):
        return (scope or self.ctx).enter_context(self.nc.psum_tensor(name, shape, dtype))


def build_program(debug=()):
    p = P()
    nc = p.nc

    def din(name, shape, dtype=F32):
        return nc.dram_tensor(name, shape, dtype, kind="ExternalInput").ap()

    Xb = din("Xb", [T, D]); Xh = din("Xh", [TH, D])
    Wp_c = din("Wp_c", [D, 4 * DH]); bp_c = din("bp_c", [128, 16])
    Wf_c = din("Wf_c", [D, FSH]); bf_c = din("bf_c", [128, FSH // 128])
    gwT = din("gwT", [D, E])
    w1 = din("w1", [E, D, 2 * FFN]); w2 = din("w2", [E, FFN, D])
    cmask = din("cmask", [128, 16])
    CT = {nm: din(nm, shp, BF16) for nm, shp in [
        ("W128c", [128, 256]), ("W64re", [128, 64]), ("W64im", [128, 64]),
        ("W64imn", [128, 64]), ("W64ire", [128, 64]), ("W64iim", [128, 64]),
        ("W64iimn", [128, 64]), ("W128ire", [128, 64]), ("W128iimn", [128, 64]),
        ("twre", [128, 512]), ("twim", [128, 512]), ("ctre", [128, 512]),
        ("ctim", [128, 512]), ("identb", [128, 128]), ("ones1", [1, 128]),
        ("posT", [D, T])]}
    identf = din("identf", [128, 128], F32)

    out = nc.dram_tensor("out", [TH, D], F32, kind="ExternalOutput").ap()
    dbg = {}
    for name in debug:
        shp = {"xnT": [D, T], "xnhT": [D, TH], "xp": [4 * DH, T],
               "fhbm": [FSH, T], "Fsh": [FSH, FROW], "Ffull": [4 * FSH, FROW],
               "y1": [128, NQ * 512], "y2": [128, NQ * 512], "attn": [DH, T],
               "attnfm": [D, TH], "comb": [128, 128], "ffn": [D, TH]}[name]
        dbg[name] = nc.dram_tensor("dbg_" + name, shp, F32, kind="ExternalOutput").ap()

    xp_hbm = nc.dram_tensor("xp_hbm", [4 * DH, T], BF16)
    f_hbm = nc.dram_tensor("f_hbm", [FSH, T], BF16)
    agF_in = nc.dram_tensor("agF_in", [FSH, FROW], BF16)
    agF_s = [nc.dram_tensor(f"agF_o{s}", [512, FROW], BF16) for s in range(STAGES)]
    agA_in = nc.dram_tensor("agA_in", [DH, T], BF16)
    agA_a = nc.dram_tensor("agA_a", [2 * 384, T], BF16)
    agA_b = nc.dram_tensor("agA_b", [2 * 128, T], BF16)

    for s in ["qs", "qa", "qg", "te", "dv", "ac", "gp", "cc"]:
        p.sem(s)
    TE, DV, ACT, GP, SY = nc.tensor, nc.vector, nc.scalar, nc.gpsimd, nc.sync

    def full_barrier():
        for e in [TE, DV, ACT, GP, SY]:
            for s in list(p.cnt.keys()):
                if s != "cc":
                    p.wait(e, s)

    xnhT = p.sbuf("xnhT", [128, 8 * TH], BF16)
    cons = p.sbuf("cons", [128, 2944], BF16)
    w128c_s = cons[:, 0:256]
    w64_s = {nm: cons[:, 256 + 64 * i:256 + 64 * i + 64]
             for i, nm in enumerate(["W64re", "W64im", "W64imn", "W64ire", "W64iim", "W64iimn"])}
    w128i_s = cons[:, 640:768]
    twre_s = cons[:, 768:1280]; twim_s = cons[:, 1280:1792]
    ctre_s = cons[:, 1792:2304]; ctim_s = cons[:, 2304:2816]
    idb_s = cons[:, 2816:2944]
    ones_s = p.sbuf("ones_s", [1, 128], BF16)
    bp_s = p.sbuf("bp_s", [128, 16], F32)
    bf_s = p.sbuf("bf_s", [128, FSH // 128], F32)
    cm_s = p.sbuf("cm_s", [128, 16], F32)
    eps_s = p.sbuf("eps_s", [128, 1], F32)

    scopeA = ExitStack()
    scopeB = ExitStack()
    scopeC = ExitStack()
    scopeD = ExitStack()

    blk = p.ctx.enter_context(nc.Block())


    def _finish_stub(open_scopes):
        for e in [TE, DV, ACT, GP, SY]:
            for s in list(p.cnt.keys()):
                p.wait(e, s)
        GP.memset(out_fm[:, 0:1024], 0.0)
        p.inc(GP.engine_nop(), "gp")
        p.wait(GP, "gp", p.cnt["gp"])
        for it in range(TH // 128):
            p.inc(GP.dma_start(out=out[it * 128:(it + 1) * 128, :], in_=out_fm[:, 0:1024]), "qg", 16)
        p.wait(GP, "qg", p.cnt["qg"])
        p.wait(SY, "qg", p.cnt["qg"])
        for sc in open_scopes:
            sc.close()
    p.inc(DV.memset(eps_s[:, :], 1e-5), "dv")
    p.inc(SY.dma_start(out=w128c_s, in_=CT["W128c"]), "qs", 16)
    for nm in ["W64re", "W64im", "W64imn", "W64ire", "W64iim", "W64iimn"]:
        p.inc(SY.dma_start(out=w64_s[nm], in_=CT[nm]), "qs", 16)
    p.inc(SY.dma_start(out=w128i_s[:, 0:64], in_=CT["W128ire"]), "qs", 16)
    p.inc(SY.dma_start(out=w128i_s[:, 64:128], in_=CT["W128iimn"]), "qs", 16)
    for dst, nm in [(twre_s, "twre"), (twim_s, "twim"), (ctre_s, "ctre"),
                    (ctim_s, "ctim"), (idb_s, "identb")]:
        p.inc(SY.dma_start(out=dst, in_=CT[nm]), "qs", 16)
    p.inc(SY.dma_start(out=ones_s[:, :], in_=CT["ones1"]), "qs", 16)
    p.inc(SY.dma_start(out=bp_s[:, :], in_=bp_c), "qs", 16)
    p.inc(SY.dma_start(out=bf_s[:, :], in_=bf_c), "qs", 16)
    p.inc(SY.dma_start(out=cm_s[:, :], in_=cmask), "qs", 16)
    consts_done = p.cnt["qs"]

    s1b = [p.sbuf("s1b0", [128, 2048], BF16)]
    xtw = s1b
    twt = [p.sbuf("twt0", [128, 2048], BF16)]
    s2in = [p.sbuf("s2in0", [128, 2048], BF16)]
    wsb = [p.sbuf("wsb0", [128, 2048], BF16)]
    i1b = [p.sbuf("i1b0", [128, 2048], BF16)]
    i1t = i1b
    s3in = [p.sbuf("s3in0", [128, 2048], BF16)]
    fscr = p.sbuf("fscr", [128, 5120], BF16)
    wf_sb = fscr[:, 0:3072]
    pos_sb = fscr[:, 3072:4096]
    f_ev = fscr[:, 4096:4608]
    fin_sb = fscr[:, 4608:5120]
    gw_sb = p.sbuf("gw_sb", [128, 64], BF16)
    comb_tok = p.sbuf("comb_tok", [128, 128], BF16)
    gst = p.sbuf("gst", [128, 8], F32)
    esb = p.sbuf("esb", [128, 8], F32)
    msk = p.sbuf("msk", [128, 16], F32)
    combf = p.sbuf("combf", [128, 8], F32)
    ps_A = [[p.psum(f"cpA_{i}", [128, 512]) for i in range(2)]]
    ps_T = [p.psum("cpT", [128, 1024], BF16)]
    ps_Z = [p.psum("cpZ", [128, 512])]
    ps_Zb = p.psum("cpZ1", [128, 512])
    ps_s1 = ps_A[0]
    f_psn3 = [ps_A[0][0], ps_A[0][1], ps_Z[0]]
    sc_ps = ps_Z[0][:, 0:8]
    NCI = FSH // 128
    f_written = [0, 0]
    F_thr = [0, 0, 0]

    def emit_fag(s_):
        p.wait(GP, "fz0", p.cnt["fz0"])
        p.wait(GP, "fzg", p.cnt.get("fzg", 0))
        if SIM_LOCAL:
            for r_ in range(4):
                p.inc(GP.dma_start(out=agF_s[s_].ap()[r_ * 128:(r_ + 1) * 128, :],
                                   in_=agF_in.ap()[s_ * 128:(s_ + 1) * 128, :]), "cc", 16)
            GP.wait_ge(p.sems["cc"], p.cnt["cc"])
        else:
            p.inc(GP.collective_compute(
                "AllGather", ALU.bypass, replica_groups=[[0, 2, 4, 6], [1, 3, 5, 7]],
                ins=[agF_in.ap()[s_ * 128:(s_ + 1) * 128, :].opt()],
                outs=[agF_s[s_].ap().opt()]), "cc", 1)
        F_thr[s_] = p.cnt["cc"]

    def fwd_first_half(par, fpar, rhs_lo, rhs_hi, fld_fn, st_qi, stp=None, wbuf=None):
        o = {}
        W, wres = (wsb[par], f"wsb{par}") if wbuf is None else wbuf
        A0, A1 = ps_A[par]
        p.wait(TE, "qs", consts_done)
        p.claim(TE, f"psA{par}")
        TE.matmul(A0[:, :], w128c_s[0:64, 0:128], rhs_lo, start=True, stop=True)
        m1 = p.inc(TE.matmul(A1[:, :], w128c_s[0:64, 128:256], rhs_lo, start=True, stop=True), "te")
        p.wait(ACT, "te", m1)
        p.claim(ACT, f"s1b{par}")
        ACT.activation(s1b[par][:, 0:512], A0[:, :], AF.Copy)
        e1 = p.inc(ACT.activation(s1b[par][:, 512:1024], A1[:, :], AF.Copy), "ac")
        p.wait(TE, "ac", e1)
        TE.matmul(A0[:, :], w128c_s[64:128, 0:128], rhs_hi, start=True, stop=True,
                  tile_position=(64, 0))
        m2 = p.inc(TE.matmul(A1[:, :], w128c_s[64:128, 128:256], rhs_hi, start=True, stop=True,
                             tile_position=(64, 0)), "te")
        p.wait(ACT, "te", m2)
        ACT.activation(s1b[par][:, 1024:1536], A0[:, :], AF.Copy)
        e2 = p.inc(ACT.activation(s1b[par][:, 1536:2048], A1[:, :], AF.Copy), "ac")
        o["s1ev"] = e2
        p.use(f"psA{par}", "ac", e2)
        p.claim(DV, f"xtw{par}")
        p.claim(DV, f"twt{par}")
        tq = twt[par]
        tws = []
        for g in range(2):
            p.wait(DV, "ac", e1 if g == 0 else e2)
            re = s1b[par][:, g * 1024:g * 1024 + 512]
            im = s1b[par][:, g * 1024 + 512:g * 1024 + 1024]
            DV.tensor_tensor(tq[:, 0:512], re, twre_s, ALU.mult)
            DV.tensor_tensor(tq[:, 512:1024], im, twim_s, ALU.mult)
            DV.tensor_tensor(tq[:, 1024:1536], re, twim_s, ALU.mult)
            p.inc(DV.tensor_tensor(tq[:, 1536:2048], im, twre_s, ALU.mult), "dv")
            p.wait(DV, "dv", p.cnt["dv"])
            DV.tensor_tensor(xtw[par][:, g * 1024:g * 1024 + 512], tq[:, 0:512],
                             tq[:, 512:1024], ALU.subtract)
            tw = p.inc(DV.tensor_tensor(xtw[par][:, g * 1024 + 512:g * 1024 + 1024],
                                        tq[:, 1024:1536], tq[:, 1536:2048], ALU.add), "dv")
            p.wait(DV, "dv", tw)
            tws.append(tw)
        o["tw"] = tw
        p.use(f"s1b{par}", "dv", tw)
        p.use(f"twt{par}", "dv", tw)
        if stp:
            stp(2)
        tevs = []
        for half in range(2):
            p.wait(TE, "dv", tws[half])
            if half == 0:
                p.claim(TE, f"psT{par}")
            else:
                p.claim(TE, f"psA{par}")
            for pr in range(4):
                for pl in range(2):
                    t8 = pr * 2 + pl
                    sl = xtw[par][:, half * 1024 + pl * 512 + pr * 128:
                                  half * 1024 + pl * 512 + pr * 128 + 128]
                    if half == 0:
                        ins = TE.transpose(ps_T[par][:, ts(t8, 128)], sl, idb_s)
                    else:
                        ins = TE.matmul((A0 if t8 < 4 else A1)[:, ts(t8 % 4, 128)],
                                        sl, idb_s, start=True, stop=True)
            tra = p.inc(ins, "te")
            p.wait(ACT, "te", tra)
            if half == 0:
                p.claim(ACT, f"s2in{par}")
                tev = p.inc(ACT.activation(s2in[par][:, 0:1024],
                                           ps_T[par][:, :], AF.Copy), "ac")
                p.use(f"psT{par}", "ac", tev)
            else:
                ACT.activation(s2in[par][:, 1024:1536], A0[:, :], AF.Copy)
                tev = p.inc(ACT.activation(s2in[par][:, 1536:2048], A1[:, :], AF.Copy), "ac")
                p.use(f"psA{par}", "ac", tev)
            tevs.append(tev)
        o["s2ev"] = ("ac", tev)
        p.use(f"xtw{par}", "te", p.cnt["te"])
        lay = s2in[par][:, :].rearrange("p (c pr pl k) -> p c pr pl k", c=2, pr=4, pl=2)
        fmc = []
        for c2 in range(4):
            Z = ps_Z[par] if c2 % 2 == 0 else ps_Zb
            g, prh = c2 // 2, (c2 % 2) * 2
            rc = lay[:, g, prh:prh + 2, 0, :]
            ic = lay[:, g, prh:prh + 2, 1, :]
            if stp:
                stp(1)
            p.wait(TE, "ac", tevs[g])
            if c2 == 0:
                p.claim(TE, f"psZ{par}")
            elif c2 == 1:
                p.claim(TE, "psZb")
            else:
                p.wait(TE, "dv", fmc[c2 - 2])
            TE.matmul(Z[0:64, 0:256], w64_s["W64re"][0:64, :], rc[0:64], start=True, stop=False)
            TE.matmul(Z[0:64, 0:256], w64_s["W64imn"][0:64, :], ic[0:64], start=False, stop=True)
            TE.matmul(Z[64:128, 0:256], w64_s["W64re"][64:128, :], rc[64:128], start=True, stop=False,
                      tile_position=(64, 64))
            TE.matmul(Z[64:128, 0:256], w64_s["W64imn"][64:128, :], ic[64:128], start=False, stop=True,
                      tile_position=(64, 64))
            TE.matmul(Z[0:64, 256:512], w64_s["W64im"][0:64, :], rc[0:64], start=True, stop=False)
            TE.matmul(Z[0:64, 256:512], w64_s["W64re"][0:64, :], ic[0:64], start=False, stop=True)
            TE.matmul(Z[64:128, 256:512], w64_s["W64im"][64:128, :], rc[64:128], start=True, stop=False,
                      tile_position=(64, 64))
            zmm = p.inc(TE.matmul(Z[64:128, 256:512], w64_s["W64re"][64:128, :], ic[64:128],
                                  start=False, stop=True, tile_position=(64, 64)), "te")
            p.wait(DV, "te", zmm)
            fw = fld_fn(c2) if fld_fn is not None else None
            wof = c2 * 512
            tq = twt[par]
            p.claim(DV, f"twt{par}")
            if c2 == 0:
                p.claim(DV, wres)
            if fw is not None:
                fr = fre_sb[par][:, 0:256].rearrange("p (w k) -> p w k", w=2) if False else None
                fre = fre_sb[fpar][:, c2 * 512:c2 * 512 + 256]
                fim = fre_sb[fpar][:, c2 * 512 + 256:c2 * 512 + 512]
                DV.tensor_tensor(tq[:, 0:256], Z[:, 0:256], fre, ALU.mult)
                DV.tensor_tensor(tq[:, 256:512], Z[:, 256:512], fim, ALU.mult)
                DV.tensor_tensor(tq[:, 512:768], Z[:, 0:256], fim, ALU.mult)
                p.inc(DV.tensor_tensor(tq[:, 768:1024], Z[:, 256:512], fre, ALU.mult), "dv")
                p.wait(DV, "dv", p.cnt["dv"])
                DV.tensor_tensor(W[:, wof:wof + 256], tq[:, 0:256], tq[:, 256:512], ALU.subtract)
                fm = p.inc(DV.tensor_tensor(W[:, wof + 256:wof + 512], tq[:, 512:768],
                                            tq[:, 768:1024], ALU.add), "dv")
                p.wait(DV, "dv", fm)
                p.use(f"twt{par}", "dv", fm)
                p.use("fre0", "dv", fm) if c2 == 3 else None
            else:
                DV.tensor_copy(W[:, wof:wof + 256], Z[:, 0:256])
                fm = p.inc(DV.tensor_copy(W[:, wof + 256:wof + 512], Z[:, 256:512]), "dv")
            fmc.append(fm)
        o["fm01"] = fmc[1]
        o["fm"] = fmc[3]
        p.use(f"psZ{par}", "dv", fmc[2])
        p.use("psZb", "dv", fmc[3])
        p.use(f"s2in{par}", "te", p.cnt["te"])
        return o

    def bwd_second_half(par, gpar, gate_to_y, qi, st, gld, o, stp=None):
        A0, A1 = ps_A[par]
        i1es = []
        for cp in range(2):
            if stp:
                stp(1)
            p.wait(TE, "dv", o["fm01"] if cp == 0 else o["fm"])
            p.claim(TE, f"psA{par}")
            for c2 in (2 * cp, 2 * cp + 1):
                wof = c2 * 512
                wre = wsb[par][:, wof:wof + 256].rearrange("p (pr k) -> p pr k", pr=2)
                wim = wsb[par][:, wof + 256:wof + 512].rearrange("p (pr k) -> p pr k", pr=2)
                co = (c2 % 2) * 256
                TE.matmul(A0[0:64, co:co + 256], w64_s["W64ire"][0:64, :], wre[0:64], start=True, stop=False)
                TE.matmul(A0[0:64, co:co + 256], w64_s["W64iimn"][0:64, :], wim[0:64], start=False, stop=True)
                TE.matmul(A0[64:128, co:co + 256], w64_s["W64ire"][64:128, :], wre[64:128], start=True,
                          stop=False, tile_position=(64, 64))
                TE.matmul(A0[64:128, co:co + 256], w64_s["W64iimn"][64:128, :], wim[64:128], start=False,
                          stop=True, tile_position=(64, 64))
                TE.matmul(A1[0:64, co:co + 256], w64_s["W64iim"][0:64, :], wre[0:64], start=True, stop=False)
                TE.matmul(A1[0:64, co:co + 256], w64_s["W64ire"][0:64, :], wim[0:64], start=False, stop=True)
                TE.matmul(A1[64:128, co:co + 256], w64_s["W64iim"][64:128, :], wre[64:128], start=True,
                          stop=False, tile_position=(64, 64))
                i1m = p.inc(TE.matmul(A1[64:128, co:co + 256], w64_s["W64ire"][64:128, :], wim[64:128],
                                      start=False, stop=True, tile_position=(64, 64)), "te")
            p.wait(ACT, "te", i1m)
            if cp == 0:
                p.claim(ACT, f"i1b{par}")
            ACT.activation(i1b[par][:, cp * 1024:cp * 1024 + 512], A0[:, :], AF.Copy)
            i1e = p.inc(ACT.activation(i1b[par][:, cp * 1024 + 512:cp * 1024 + 1024], A1[:, :], AF.Copy), "ac")
            p.use(f"psA{par}", "ac", i1e)
            i1es.append(i1e)
        p.claim(DV, f"i1t{par}")
        tq = twt[par]
        ctws = []
        for cp in range(2):
            p.wait(DV, "ac", i1es[cp])
            re = i1b[par][:, cp * 1024:cp * 1024 + 512]
            im = i1b[par][:, cp * 1024 + 512:cp * 1024 + 1024]
            ctr = ctre_s[:, 0:512]
            cti = ctim_s[:, 0:512]
            p.claim(DV, f"twt{par}")
            DV.tensor_tensor(tq[:, 0:512], re, ctr, ALU.mult)
            DV.tensor_tensor(tq[:, 512:1024], im, cti, ALU.mult)
            DV.tensor_tensor(tq[:, 1024:1536], re, cti, ALU.mult)
            p.inc(DV.tensor_tensor(tq[:, 1536:2048], im, ctr, ALU.mult), "dv")
            p.wait(DV, "dv", p.cnt["dv"])
            DV.tensor_tensor(i1t[par][:, cp * 1024:cp * 1024 + 512], tq[:, 0:512],
                             tq[:, 512:1024], ALU.subtract)
            ctw = p.inc(DV.tensor_tensor(i1t[par][:, cp * 1024 + 512:cp * 1024 + 1024],
                                         tq[:, 1024:1536], tq[:, 1536:2048], ALU.add), "dv")
            p.wait(DV, "dv", ctw)
            p.use(f"twt{par}", "dv", ctw)
            ctws.append(ctw)
        p.use(f"i1b{par}", "dv", ctw)
        if stp:
            stp(2)
        for half in range(2):
            p.wait(TE, "dv", ctws[half])
            if half == 0:
                p.claim(TE, f"psT{par}")
            else:
                p.claim(TE, f"psA{par}")
            for i in range(4):
                c2 = half * 2 + i // 2
                pr = i % 2
                cp, co = c2 // 2, (c2 % 2) * 256
                for pl in range(2):
                    t8 = i * 2 + pl
                    sl = i1t[par][:, cp * 1024 + pl * 512 + co + pr * 128:
                                  cp * 1024 + pl * 512 + co + pr * 128 + 128]
                    if half == 0:
                        ins = TE.transpose(ps_T[par][:, ts(t8, 128)], sl, idb_s)
                    else:
                        ins = TE.matmul((A0 if t8 < 4 else A1)[:, ts(t8 % 4, 128)],
                                        sl, idb_s, start=True, stop=True)
            trb = p.inc(ins, "te")
            p.wait(ACT, "te", trb)
            if half == 0:
                p.claim(ACT, f"s3in{par}")
                s3e = p.inc(ACT.activation(s3in[par][:, 0:1024],
                                           ps_T[par][:, :], AF.Copy), "ac")
                p.use(f"psT{par}", "ac", s3e)
            else:
                ACT.activation(s3in[par][:, 1024:1536], A0[:, :], AF.Copy)
                s3e = p.inc(ACT.activation(s3in[par][:, 1536:2048], A1[:, :], AF.Copy), "ac")
                p.use(f"psA{par}", "ac", s3e)
        p.use(f"i1t{par}", "te", p.cnt["te"])
        p.wait(TE, "ac", s3e)
        p.claim(TE, f"psZ{par}")
        Z = ps_Z[par]
        lay3 = s3in[par][:, :].rearrange("p (h i pl n) -> p h i pl n", h=2, i=4, pl=2)
        TE.matmul(Z[0:64, :], w128i_s[:, 0:64], lay3[:, 0, :, 0, :], start=True, stop=False)
        TE.matmul(Z[0:64, :], w128i_s[:, 64:128], lay3[:, 0, :, 1, :], start=False, stop=True)
        TE.matmul(Z[64:128, :], w128i_s[:, 0:64], lay3[:, 1, :, 0, :], start=True, stop=False,
                  tile_position=(0, 64))
        invsb = p.inc(TE.matmul(Z[64:128, :], w128i_s[:, 64:128], lay3[:, 1, :, 1, :],
                                start=False, stop=True, tile_position=(0, 64)), "te")
        p.use(f"s3in{par}", "te", invsb)
        p.wait(DV, "te", invsb)
        p.wait(DV, f"gl{gpar}")
        if gate_to_y:
            g = p.inc(DV.tensor_tensor(y_sb[:, ts(qi, 512)], Z[:, :], gt_sb[gpar][:, :], ALU.mult), "dv")
        else:
            p.claim(DV, "att0")
            g = p.inc(DV.tensor_tensor(att_sb[gpar][:, :], Z[:, :], gt_sb[gpar][:, :], ALU.mult), "dv")
            p.wait(SY, "dv", g)
            for half in range(2):
                dst = bass.AP(agA_in, (16 * qi + 8 * half) * T, [[64, 64], [T, 8], [1, 64]])
                p.inc(SY.dma_start(out=dst,
                                   in_=att_sb[gpar][64 * half:64 * half + 64, :]
                                   .rearrange("p (c n) -> p c n", c=8)), f"as{gpar}", 16)
            p.use("att0", f"as{gpar}", p.cnt[f"as{gpar}"])
        p.use(f"psZ{par}", "dv", g)
        p.use("gt0", "dv", g)
        return g


    def gating_unit(tt):
        p.claim(TE, "psZ0")
        for kt in range(8):
            ins = TE.matmul(sc_ps, xnhT[:, kt * TH + tt * 128: kt * TH + tt * 128 + 128],
                            gw_sb[:, kt * 8:kt * 8 + 8], start=(kt == 0), stop=(kt == 7))
        scm = p.inc(ins, "te")
        p.wait(DV, "te", scm)
        p.claim(DV, "gst")
        p.inc(DV.reduce_max(gst[:, 0:1], sc_ps, axis=AX.X, negate=True), "dv")
        p.wait(ACT, "dv", p.cnt["dv"])
        p.claim(ACT, "esb")
        p.inc(ACT.activation(esb[:, :], sc_ps, AF.Exp, bias=gst[:, 0:1]), "ac")
        p.use("psZ0", "ac", p.cnt["ac"])
        p.use("gst", "ac", p.cnt["ac"])
        p.wait(DV, "ac", p.cnt["ac"])
        p.inc(DV.reduce_max(gst[:, 1:2], esb[:, :], axis=AX.X), "dv")
        p.wait(DV, "dv", p.cnt["dv"])
        p.inc(DV.tensor_scalar(msk[:, 0:8], esb[:, :], gst[:, 1:2], None, ALU.is_equal), "dv")
        p.wait(DV, "dv", p.cnt["dv"])
        p.inc(DV.scalar_tensor_tensor(msk[:, 8:16], msk[:, 0:8], -2.0, esb[:, :], ALU.mult, ALU.add), "dv")
        p.wait(DV, "dv", p.cnt["dv"])
        p.inc(DV.reduce_max(gst[:, 2:3], msk[:, 8:16], axis=AX.X), "dv")
        p.wait(DV, "dv", p.cnt["dv"])
        p.inc(DV.tensor_scalar(msk[:, 8:16], msk[:, 8:16], gst[:, 2:3], None, ALU.is_equal), "dv")
        p.inc(DV.tensor_tensor(gst[:, 3:4], gst[:, 1:2], gst[:, 2:3], ALU.add), "dv")
        p.wait(DV, "dv", p.cnt["dv"])
        p.inc(DV.reciprocal(gst[:, 4:5], gst[:, 3:4]), "dv")
        p.wait(DV, "dv", p.cnt["dv"])
        p.inc(DV.tensor_tensor(gst[:, 5:6], gst[:, 1:2], gst[:, 4:5], ALU.mult), "dv")
        p.inc(DV.tensor_tensor(gst[:, 6:7], gst[:, 2:3], gst[:, 4:5], ALU.mult), "dv")
        p.wait(DV, "dv", p.cnt["dv"])
        p.inc(DV.tensor_scalar(combf[:, :], msk[:, 0:8], gst[:, 5:6], None, ALU.mult), "dv")
        p.wait(DV, "dv", p.cnt["dv"])
        p.inc(DV.scalar_tensor_tensor(combf[:, :], msk[:, 8:16], gst[:, 6:7], combf[:, :],
                                      ALU.mult, ALU.add), "dv")
        p.wait(DV, "dv", p.cnt["dv"])
        cb = p.inc(DV.tensor_copy(comb_tok[:, tt * 8:tt * 8 + 8], combf[:, :]), "dv")
        p.use("gst", "dv", cb)
        p.use("esb", "dv", cb)
        if "comb" in dbg:
            p.wait(GP, "dv", cb)
            p.inc(GP.dma_start(out=dbg["comb"][:, tt * 8:tt * 8 + 8], in_=combf[:, :]), "qg", 16)

    def fproj_gen():
        for kt_ in range(8):
            p.inc(GP.dma_start(out=wf_sb[:, ts(kt_, FSH)],
                               in_=Wf_c[kt_ * 128:(kt_ + 1) * 128, :]), "wf", 16)
        p.wait(TE, "wf")
        pos_used = {}
        fpk = 0
        for tc in range(8):
            for kt_ in range(8):
                lb = fpk % 2
                if fpk >= 2:
                    p.wait(GP, "te", pos_used[fpk - 2])
                p.inc(GP.dma_start(out=pos_sb[:, ts(lb, 512)],
                                   in_=CT["posT"][kt_ * 128:(kt_ + 1) * 128, ts(tc, 512)]),
                      f"po{lb}", 16)
                p.wait(TE, f"po{lb}")
                if tc == 0 and kt_ == 0:
                    p.claim(TE, "psA0")
                    p.claim(TE, "psZ0")
                elif kt_ == 0:
                    p.wait(TE, "ac", p.cnt["ac"])
                for ci_ in range(NCI):
                    ins = TE.matmul(f_psn3[ci_][:, :],
                                    wf_sb[:, kt_ * FSH + ci_ * 128: kt_ * FSH + ci_ * 128 + 128],
                                    pos_sb[:, ts(lb, 512)],
                                    start=(kt_ == 0), stop=(kt_ == 7))
                pos_used[fpk] = p.inc(ins, "te")
                fpk += 1
                yield
            mm = p.cnt["te"]
            p.wait(ACT, "te", mm)
            for ci_ in range(NCI):
                p.claim(ACT, "f_ev")
                fpv = p.inc(ACT.activation(f_ev[:, :], f_psn3[ci_][:, :],
                                           AF.Identity, bias=bf_s[:, ci_:ci_ + 1]), "ac")
                p.wait(SY, "ac", fpv)
                fst = p.inc(SY.dma_start(out=f_hbm.ap()[ci_ * 128:(ci_ + 1) * 128, ts(tc, 512)],
                                         in_=f_ev[:, :]), "fs0", 16)
                p.use("f_ev", "fs0", fst)
            yield
        p.use("psA0", "ac", p.cnt["ac"])
        p.use("psZ0", "ac", p.cnt["ac"])
        f_written[0] = p.cnt["fs0"]
        if "fhbm" in dbg:
            p.wait(GP, "fs0", f_written[0])
            p.inc(GP.dma_start(out=dbg["fhbm"], in_=f_hbm.ap()), "qg", 16)

    def fprep_gen():
        p.wait(ACT, "fs0", f_written[0])
        for qi_ in range(FSH // 16):
            row0 = 16 * qi_
            p.claim(ACT, "fin_sb")
            for half in range(2):
                fsrc = bass.AP(f_hbm, (row0 + 8 * half) * T, [[64, 64], [T, 8], [1, 64]])
                p.inc(ACT.dma_start(out=fin_sb[64 * half:64 * half + 64, :]
                                    .rearrange("p (c n) -> p c n", c=8), in_=fsrc), "fi0", 16)
            p.wait(TE, "fi0")
            wb = (wsb[0], "wsb0") if qi_ % 2 == 0 else (s3in[0], "s3in0")
            fq, fsem = GP, ("fz0" if qi_ % 2 == 0 else "fzg")
            o = fwd_first_half(0, 0, fin_sb[0:64, :], fin_sb[64:128, :], None, None,
                               wbuf=wb)
            p.use("fin_sb", "te", p.cnt["te"])
            p.wait(fq, "dv", o["fm"])
            for c2 in range(4):
                for pl in range(2):
                    for half in range(2):
                        dst = bass.AP(agF_in, (row0 + 4 * c2 + half) * FROW + pl * 8192,
                                      [[128, 64], [2 * FROW, 2], [1, 128]])
                        ins = fq.dma_start(out=dst,
                                           in_=wb[0][64 * half:64 * half + 64,
                                                     c2 * 512 + pl * 256:c2 * 512 + pl * 256 + 256]
                                           .rearrange("p (pr k) -> p pr k", pr=2))
                        p.inc(ins, fsem, 16)
            p.use(wb[1], fsem, p.cnt[fsem])
            if qi_ == 7:
                emit_fag(0)
            yield

    def _pre_units():
        yield from fproj_gen()
        yield from fprep_gen()
    _preit = _pre_units()
    _pre_done = [False]

    def step_pre(n=1):
        if _pre_done[0]:
            return
        for _ in range(n):
            try:
                next(_preit)
            except StopIteration:
                _pre_done[0] = True
                return

    xnT = p.sbuf("xnT", [128, 8 * T], BF16, scope=scopeA)

    ln_xt = p.sbuf("ln_xt", [128, 2 * D], F32, scope=scopeA)
    ln_sq = p.sbuf("ln_sq", [128, D], F32, scope=scopeA)
    ln_st = p.sbuf("ln_st", [128, 16], F32, scope=scopeA)
    ln_nb = p.sbuf("ln_nb", [128, 2 * D], BF16, scope=scopeA)
    ln_ps = p.psum("ln_ps", [128, 1024], BF16, scope=scopeA)

    def ln_phase(src_ap, n_tok, dstT, tag, stp=None):
        xt, sq, stat, xnb, pst = ln_xt, ln_sq, ln_st, ln_nb, ln_ps
        ntile = n_tok // 128
        tr_done, ev_done, tr_evac = {}, {}, {}
        for it in range(ntile):
            if stp:
                stp()
            bo = (it % 2) * D
            so = (it % 2) * 8
            xs = xt[:, bo:bo + D]
            qx = f"lx{tag}{it % 2}"
            if it >= 2:
                p.wait(SY, "ac", ev_done[it - 2])
            p.inc(SY.dma_start(out=xs, in_=src_ap[it * 128:(it + 1) * 128, :]), qx, 16)
            p.wait(DV, qx)
            if it >= 2:
                p.wait(DV, "ac", ev_done[it - 2])
            p.inc(DV.reduce_sum(stat[:, so + 0:so + 1], xs, axis=AX.X), "dv")
            p.wait(ACT, qx)
            if it >= 2:
                p.wait(ACT, "dv", p.cnt["dv"])
            p.inc(ACT.activation(sq[:, :], xs, AF.Square), "ac")
            p.wait(DV, "ac", p.cnt["ac"])
            p.inc(DV.reduce_sum(stat[:, so + 1:so + 2], sq[:, :], axis=AX.X), "dv")
            p.wait(DV, "dv", p.cnt["dv"])
            p.inc(DV.tensor_scalar(stat[:, so + 2:so + 3], stat[:, so + 0:so + 1], 1.0 / D, None, ALU.mult), "dv")
            p.inc(DV.tensor_scalar(stat[:, so + 3:so + 4], stat[:, so + 1:so + 2], 1.0 / D, None, ALU.mult), "dv")
            p.wait(DV, "dv", p.cnt["dv"])
            p.inc(DV.tensor_tensor(stat[:, so + 4:so + 5], stat[:, so + 2:so + 3], stat[:, so + 2:so + 3], ALU.mult), "dv")
            p.wait(DV, "dv", p.cnt["dv"])
            p.inc(DV.tensor_tensor(stat[:, so + 5:so + 6], stat[:, so + 3:so + 4], stat[:, so + 4:so + 5], ALU.subtract), "dv")
            p.wait(ACT, "dv", p.cnt["dv"])
            p.inc(ACT.activation(stat[:, so + 6:so + 7], stat[:, so + 5:so + 6], AF.Ln, bias=eps_s[:, :]), "ac")
            p.wait(ACT, "ac", p.cnt["ac"])
            p.inc(ACT.activation(stat[:, so + 6:so + 7], stat[:, so + 6:so + 7], AF.Exp, scale=-0.5), "ac")
            p.wait(DV, "ac", p.cnt["ac"])
            p.inc(DV.tensor_tensor(stat[:, so + 7:so + 8], stat[:, so + 2:so + 3], stat[:, so + 6:so + 7], ALU.mult), "dv")
            p.wait(DV, "dv", p.cnt["dv"])
            p.inc(DV.tensor_scalar(stat[:, so + 7:so + 8], stat[:, so + 7:so + 8], -1.0, None, ALU.mult), "dv")
            p.wait(ACT, "dv", p.cnt["dv"])
            if it >= 2:
                p.wait(ACT, "te", tr_done[it - 2])
            xnbs = xnb[:, bo:bo + D]
            ev_done[it] = p.inc(ACT.activation(xnbs, xs, AF.Identity,
                                               bias=stat[:, so + 7:so + 8],
                                               scale=stat[:, so + 6:so + 7]), "ac")
            p.wait(TE, "ac", p.cnt["ac"])
            p.wait(TE, "qs", consts_done)
            if it >= 1:
                p.wait(TE, "dv", tr_evac[it - 1])
            for j in range(8):
                ins = TE.transpose(pst[:, ts(j, 128)], xnbs[:, ts(j, 128)], idb_s)
            tr_done[it] = p.inc(ins, "te")
            p.wait(DV, "te", tr_done[it])
            tr_evac[it] = p.inc(DV.tensor_copy(
                dstT[:, :].rearrange("p (j tt) -> p j tt", j=8)[:, :, ts(it, 128)],
                pst[:, :].rearrange("p (j tt) -> p j tt", j=8)), "dv")
        return tr_evac[ntile - 1]

    ln_a_done = ln_phase(Xb, T, xnT, "a", stp=lambda: step_pre(3))
    p.wait(SY, "te", p.cnt["te"])
    p.wait(SY, "dv", p.cnt["dv"])
    p.wait(SY, "ac", p.cnt["ac"])
    wp_full = p.sbuf("wp_full", [128, 8 * 2048], BF16, scope=scopeA)
    for kt in range(8):
        p.inc(GP.dma_start(out=wp_full[:, ts(kt, 2048)], in_=Wp_c[kt * 128:(kt + 1) * 128, :]), "wp", 16)
    wp_ld = p.cnt["wp"]
    p.inc(GP.dma_start(out=gw_sb[:, :].rearrange("p (k e) -> p k e", k=8),
                       in_=gwT.rearrange("(k p) e -> p k e", p=128)), "gw", 16)
    ln_h_done = ln_phase(Xh, TH, xnhT, "h", stp=lambda: step_pre(1))

    if "xnT" in dbg:
        p.wait(GP, "dv", ln_a_done)
        p.inc(GP.dma_start(out=dbg["xnT"].rearrange("(j p) tt -> p j tt", p=128),
                           in_=xnT[:, :].rearrange("p (j tt) -> p j tt", j=8)), "qg", 16)
    if "xnhT" in dbg:
        p.wait(GP, "dv", ln_h_done)
        p.inc(GP.dma_start(out=dbg["xnhT"].rearrange("(j p) tt -> p j tt", p=128),
                           in_=xnhT[:, :].rearrange("p (j tt) -> p j tt", j=8)), "qg", 16)

    def gat_gen():
        p.wait(TE, "gw", p.cnt["gw"])
        p.wait(TE, "dv", ln_h_done)
        for tt_ in range(16):
            gating_unit(tt_)
            yield
    _gatit = gat_gen()
    _gat_done = [False]

    def step_gat(n=1):
        if _gat_done[0]:
            return
        for _ in range(n):
            try:
                next(_gatit)
            except StopIteration:
                _gat_done[0] = True
                return


    xp_ps = [p.psum(f"xp_ps{i}", [128, 512], scope=scopeA) for i in range(2)]
    xp_ev = p.sbuf("xp_ev", [128, 2 * 512], BF16, scope=scopeA)
    p.wait(TE, "wp", wp_ld)
    p.wait(TE, "dv", ln_a_done)
    xp_ev_done = {}
    for ci in range(16):
        for tc in range(8):
            idx = ci * 8 + tc
            if idx % 4 == 0:
                step_pre(1)
            if idx % 8 == 4:
                step_gat(1)
            pb = idx % 2
            if idx >= 2:
                p.wait(TE, "ac", xp_ev_done[idx - 2])
                p.wait(ACT, f"xs{pb}", p.cnt[f"xs{pb}"])
            for kt in range(8):
                ins = TE.matmul(xp_ps[pb][:, :],
                                wp_full[:, kt * 2048 + ci * 128: kt * 2048 + ci * 128 + 128],
                                xnT[:, kt * T + tc * 512: kt * T + (tc + 1) * 512],
                                start=(kt == 0), stop=(kt == 7))
            mm = p.inc(ins, "te")
            p.wait(ACT, "te", mm)
            xp_ev_done[idx] = p.inc(
                ACT.activation(xp_ev[:, ts(pb, 512)], xp_ps[pb][:, :],
                               AF.Identity, bias=bp_s[:, ci:ci + 1]), "ac")
            xq = SY if pb == 0 else ACT
            p.wait(xq, "ac", xp_ev_done[idx])
            p.inc(xq.dma_start(out=xp_hbm.ap()[ci * 128:(ci + 1) * 128, ts(tc, 512)],
                               in_=xp_ev[:, ts(pb, 512)]), f"xs{pb}", 16)
    xp_written = (p.cnt["xs0"], p.cnt["xs1"])
    if "xp" in dbg:
        p.wait(GP, "xs0", xp_written[0])
        p.wait(GP, "xs1", xp_written[1])
        p.inc(GP.dma_start(out=dbg["xp"], in_=xp_hbm.ap()), "qg", 16)

    while not (_pre_done[0] and _gat_done[0]):
        step_pre(2)
        step_gat(2)
    comb_done = p.cnt["dv"]

    if BUILD_UNTIL == "p2":
        _finish_stub([scopeA])
        p.ctx.close()
        return nc
    full_barrier()
    scopeA.close()
    if BUILD_UNTIL == "p3proj":
        _finish_stub([scopeB])
        p.ctx.close()
        return nc
    out_fm = p.sbuf("out_fm", [128, 8 * TH], BF16)
    y_sb = p.sbuf("y_sb", [128, NQ * 512], BF16, scope=scopeB)
    fre_one = p.sbuf("fre_sb0", [128, 2048], BF16, scope=scopeB)
    fre_sb = [fre_one, fre_one]
    gt_one = p.sbuf("gt_sb0", [128, 512], BF16, scope=scopeB)
    gt_sb = [gt_one, gt_one]
    att_one = f_ev
    att_sb = [att_one, att_one]
    p.wait(ACT, "xs0", xp_written[0])
    p.wait(ACT, "xs1", xp_written[1])
    y0_q = {}

    def emit_y0_quarter(k):
        for qi in range(8 * k, 8 * k + 8):
            for half in range(2):
                ysrc = bass.AP(xp_hbm, (16 * qi + 8 * half) * T, [[64, 64], [T, 8], [1, 64]])
                p.inc(ACT.dma_start(out=y_sb[64 * half:64 * half + 64, ts(qi, 512)]
                                    .rearrange("p (c n) -> p c n", c=8), in_=ysrc), f"y0q{k}", 16)
        y0_q[k] = p.cnt[f"y0q{k}"]

    emit_y0_quarter(0)

    if "Fsh" in dbg:
        p.wait(GP, "fz0", p.cnt["fz0"])
        p.inc(GP.dma_start(out=dbg["Fsh"], in_=agF_in.ap()), "qg", 16)

    emit_fag(1)
    emit_fag(2)
    F_ready = p.cnt["cc"]
    p.wait(ACT, "cc", F_thr[0])

    if BUILD_UNTIL == "gating":
        _finish_stub([scopeC])
        p.ctx.close()
        return nc

    w1f_sb = p.sbuf("w1f_sb", [128, 8 * 2816], BF16, scope=scopeB)
    w2_sb = p.sbuf("w2_sb", [128, 11 * 1024], BF16, scope=scopeB)
    m_sb = fscr[:, 0:2816]
    sg_sb = fscr[:, 2816:3328]
    crep_sb = fscr[:, 3328:3840]
    comb_ct = fscr[0:1, 4608:4864]
    ps_gu = [p.psum(f"ps_gu{i}", [128, 512], scope=scopeB) for i in range(2)]
    ps_o2 = [p.psum("ps_o2_0", [128, 512], scope=scopeB)]

    def ffn_feeder():
        for s_ in ["te", "dv", "ac"]:
            p.wait(GP, s_)
        def comb_dma(e_, ci_):
            p.claim(GP, "comb_ct")
            for h in range(2):
                cc = 2 * ci_ + h
                p.inc(GP.dma_start(out=comb_ct[0:1, h * 128:(h + 1) * 128],
                                   in_=comb_tok[:, cc * 8 + e_:cc * 8 + e_ + 1]), "cr", 16)
            return p.cnt["cr"]

        for e in range(E):
            p.wait(GP, "dv", comb_done)
            cr_next = comb_dma(e, 0)
            p.claim(GP, "w1f_sb")
            for kt in range(8):
                p.inc(GP.dma_start(out=w1f_sb[:, ts(kt, 2816)],
                                   in_=w1[e, kt * 128:(kt + 1) * 128, :]), "w1s", 16)
            w1r = p.cnt["w1s"]
            p.claim(GP, "w2_sb")
            for ft in range(11):
                p.inc(GP.dma_start(out=w2_sb[:, ts(ft, 1024)],
                                   in_=w2[e, ft * 128:(ft + 1) * 128, :]), "w2s", 16)
            w2r = p.cnt["w2s"]
            yield
            for ci in range(8):
                p.wait(TE, "cr", cr_next)
                p.claim(TE, "ps_gu0")
                crm = p.inc(TE.matmul(ps_gu[0][:, 0:256], ones_s[0:1, :],
                                      comb_ct[0:1, :],
                                      start=True, stop=True), "te")
                p.use("comb_ct", "te", crm)
                if ci < 7:
                    cr_next = comb_dma(e, ci + 1)
                p.wait(ACT, "te", crm)
                p.claim(ACT, f"crep_{ci % 2}")
                crv = p.inc(ACT.activation(crep_sb[:, ts(ci % 2, 256)], ps_gu[0][:, 0:256], AF.Copy), "ac")
                p.use("ps_gu0", "ac", crv)
                yield
                p.wait(TE, "w1s", w1r)
                for j in range(11):
                    jp = j % 2
                    p.claim(TE, f"ps_gu{jp}")
                    pg = ps_gu[jp][:, 0:256]
                    pu = ps_gu[jp][:, 256:512]
                    for kt in range(8):
                        TE.matmul(pg, w1f_sb[:, kt * 2816 + j * 128: kt * 2816 + j * 128 + 128],
                                  xnhT[:, kt * TH + ci * 256: kt * TH + ci * 256 + 256],
                                  start=(kt == 0), stop=(kt == 7))
                    for kt in range(8):
                        ins = TE.matmul(pu,
                                        w1f_sb[:, kt * 2816 + FFN + j * 128: kt * 2816 + FFN + j * 128 + 128],
                                        xnhT[:, kt * TH + ci * 256: kt * TH + ci * 256 + 256],
                                        start=(kt == 0), stop=(kt == 7))
                    gum = p.inc(ins, "te")
                    p.wait(ACT, "te", gum)
                    p.claim(ACT, f"sg_{jp}")
                    sgv = p.inc(ACT.activation(sg_sb[:, ts(jp, 256)], pg, AF.Silu), "ac")
                    p.wait(DV, "ac", sgv)
                    p.wait(DV, "ac", crv)
                    t0c = p.inc(DV.tensor_tensor(sg_sb[:, ts(jp, 256)], sg_sb[:, ts(jp, 256)],
                                                 pu, ALU.mult), "dv")
                    p.use(f"ps_gu{jp}", "dv", t0c)
                    p.wait(GP, "dv", t0c)
                    p.wait(GP, "ac", crv)
                    p.claim(GP, f"m_{j}")
                    mj = p.inc(GP.tensor_mul(m_sb[:, ts(j, 256)], sg_sb[:, ts(jp, 256)],
                                             crep_sb[:, ts(ci % 2, 256)]), "gpm")
                    p.use(f"sg_{jp}", "gpm", mj)
                    p.use(f"crep_{ci % 2}", "gpm", mj)
                    yield
                if ci == 7:
                    p.use("w1f_sb", "te", p.cnt["te"])
                mdone = p.cnt["gpm"]
                p.wait(TE, "gpm", mdone)
                p.wait(TE, "w2s", w2r)
                for dtp in range(4):
                    po = ps_o2[0]
                    p.claim(TE, "ps_o2_0")
                    for half in range(2):
                        dt = 2 * dtp + half
                        sl = po[:, half * 256:half * 256 + 256]
                        for ft in range(11):
                            ins = TE.matmul(sl, w2_sb[:, ft * 1024 + dt * 128: ft * 1024 + dt * 128 + 128],
                                            m_sb[:, ts(ft, 256)],
                                            start=(ft == 0), stop=(ft == 10))
                    w2m = p.inc(ins, "te")
                    p.wait(DV, "te", w2m)
                    osl = out_fm[:, :].rearrange("p (dt tt2) -> p dt tt2", dt=8)[
                        :, 2 * dtp:2 * dtp + 2, ci * 256:(ci + 1) * 256]
                    p.claim(DV, f"ofm_{ci}_{dtp}")
                    pov = po[:, :].rearrange("p (h tt2) -> p h tt2", h=2)
                    if e == 0:
                        fv = p.inc(DV.tensor_copy(osl, pov), "dv")
                    else:
                        fv = p.inc(DV.tensor_tensor(osl, pov, osl, ALU.add), "dv")
                    p.use("ps_o2_0", "dv", fv)
                    p.use(f"ofm_{ci}_{dtp}", "dv", fv)
                    yield
                for j in range(11):
                    p.use(f"m_{j}", "te", p.cnt["te"])
            p.use("w2_sb", "te", p.cnt["te"])

    _feed = ffn_feeder()
    _feed_done = [False]

    def step_ffn(n=1):
        if _feed_done[0]:
            return
        for _ in range(n):
            try:
                next(_feed)
            except StopIteration:
                _feed_done[0] = True
                return

    step_ffn(1)

    A_thr = [0, 0]

    def emit_aag(part):
        r0, r1 = (0, 384) if part == 0 else (384, 512)
        dst = agA_a if part == 0 else agA_b
        nr = r1 - r0
        p.wait(GP, "as0", p.cnt["as0"])
        p.wait(GP, "as1", p.cnt["as1"])
        if SIM_LOCAL:
            for r_ in range(2):
                p.inc(GP.dma_start(out=dst.ap()[r_ * nr:(r_ + 1) * nr, :],
                                   in_=agA_in.ap()[r0:r1, :]), "cc", 16)
            GP.wait_ge(p.sems["cc"], p.cnt["cc"])
        else:
            p.inc(GP.collective_compute(
                "AllGather", ALU.bypass,
                replica_groups=[[0, 1], [2, 3], [4, 5], [6, 7]],
                ins=[agA_in.ap()[r0:r1, :].opt()], outs=[dst.ap().opt()]), "cc", 1)
        A_thr[part] = p.cnt["cc"]

    gate_done = {}
    for st in range(1, STAGES + 1):
        p.wait(DV, "qg", p.cnt["qg"])
        p.wait(ACT, "cc", F_thr[st - 1])
        for qi in range(NQ):
            par = qi % 2
            if st == 1 and qi in (2, 4, 6):
                emit_y0_quarter(qi // 2)
            p.claim(SY, "gt0")
            for half in range(2):
                row0 = st * 512 + 16 * qi + 8 * half
                gsrc = bass.AP(xp_hbm, row0 * T, [[64, 64], [T, 8], [1, 64]])
                p.inc(SY.dma_start(out=gt_sb[par][64 * half:64 * half + 64, :]
                                   .rearrange("p (c n) -> p c n", c=8), in_=gsrc), f"gl{par}", 16)
            gld = p.cnt[f"gl{par}"]
            p.claim(ACT, "fre0")
            for c2 in range(4):
                for pl in range(2):
                    for half in range(2):
                        row0 = 16 * qi + 4 * c2 + half
                        fsrc2 = bass.AP(agF_s[st - 1], row0 * FROW + pl * 8192,
                                        [[128, 64], [2 * FROW, 2], [1, 128]])
                        ins = ACT.dma_start(
                            out=fre_sb[par][64 * half:64 * half + 64,
                                            c2 * 512 + pl * 256:c2 * 512 + pl * 256 + 256]
                            .rearrange("p (pr k) -> p pr k", pr=2), in_=fsrc2)
                        p.inc(ins, f"fa{par}", 16)
            fldc = p.cnt[f"fa{par}"]
            if st > 1:
                p.wait(TE, "dv", gate_done[(st - 1, qi)])
            else:
                p.wait(TE, f"y0q{qi // 8}", y0_q[qi // 8])

            def mk_fld(par_, cnt_):
                def f(c2):
                    p.wait(DV, f"fa{par_}", cnt_)
                    return cnt_
                return f
            step_ffn(1)
            o = fwd_first_half(0, par, y_sb[0:64, ts(qi, 512)], y_sb[64:128, ts(qi, 512)],
                               mk_fld(par, fldc), (st, qi), stp=step_ffn)
            g = bwd_second_half(0, par, st < STAGES, qi, st, gld, o, stp=step_ffn)
            gate_done[(st, qi)] = g
            if st == STAGES and qi == 23:
                emit_aag(0)
    while not _feed_done[0]:
        step_ffn(8)

    emit_aag(1)
    A_ready = p.cnt["cc"]
    if "attn" in dbg:
        p.wait(GP, "as0", p.cnt["as0"])
        p.wait(GP, "as1", p.cnt["as1"])
        p.inc(GP.dma_start(out=dbg["attn"], in_=agA_in.ap()), "qg", 16)

    if BUILD_UNTIL == "conv":
        _finish_stub([scopeB])
        p.ctx.close()
        return nc

    if BUILD_UNTIL == "aag":
        _finish_stub([scopeB])
        p.ctx.close()
        return nc
    full_barrier()
    scopeB.close()
    scopeC.close()
    attn_fm = p.sbuf("attn_fm", [128, 8 * TH], BF16, scope=scopeD)
    idf_s = p.sbuf("idf_s", [128, 128], F32, scope=scopeD)
    p.inc(SY.dma_start(out=idf_s[:, :], in_=identf), "idf", 16)
    cnd_sb = [p.sbuf(f"cnd{i}", [128, TH], BF16, scope=scopeD) for i in range(2)]
    o1_sb = p.sbuf("o1_sb", [128, 1024], F32, scope=scopeD)
    xh_sb = [p.sbuf(f"xh{i}", [128, 1024], F32, scope=scopeD) for i in range(6)]
    outb = p.sbuf("outb", [128, 2 * 1024], F32, scope=scopeD)
    ps_pt = [p.psum(f"ps_pt{i}", [128, 512], scope=scopeD) for i in range(2)]
    p.wait(SY, "dv", p.cnt["dv"])
    p.wait(SY, "te", p.cnt["te"])
    acc_prev = {}
    for dt in range(8):
        hsrc = 0 if dt < 4 else 1
        for colh in range(2):
            par = (dt * 2 + colh) % 2
            ldq = SY if par == 0 else ACT
            p.wait(ldq, "cc", A_thr[1] if dt % 4 == 3 else A_thr[0])
            p.claim(ldq, f"cnd{par}")
            if dt % 4 == 3:
                asrc = agA_b.ap()[hsrc * 128:hsrc * 128 + 128, colh * TH:(colh + 1) * TH]
            else:
                asrc = agA_a.ap()[hsrc * 384 + (dt % 4) * 128:
                                  hsrc * 384 + (dt % 4) * 128 + 128,
                                  colh * TH:(colh + 1) * TH]
            p.inc(ldq.dma_start(out=cnd_sb[par][:, :], in_=asrc), f"cn{par}", 16)
            p.wait(DV, f"cn{par}")
            dstsl = attn_fm[:, dt * TH:(dt + 1) * TH]
            if colh == 0:
                a = p.inc(DV.tensor_scalar(dstsl, cnd_sb[par][:, :], cm_s[:, colh:colh + 1],
                                           None, ALU.mult), "dv")
            else:
                p.wait(DV, "dv", acc_prev[dt])
                a = p.inc(DV.scalar_tensor_tensor(dstsl, cnd_sb[par][:, :], cm_s[:, colh:colh + 1],
                                                  dstsl, ALU.mult, ALU.add), "dv")
            acc_prev[dt] = a
            p.use(f"cnd{par}", "dv", a)
    attnfm_done = p.cnt["dv"]
    if "attnfm" in dbg:
        p.wait(GP, "dv", attnfm_done)
        p.inc(GP.dma_start(out=dbg["attnfm"].rearrange("(j p) tt -> p j tt", p=128),
                           in_=attn_fm[:, :].rearrange("p (j tt) -> p j tt", j=8)), "qg", 16)

    xh_q = [SY, ACT, GP]

    def xh_load(tt_):
        s = tt_ % 6
        q = xh_q[s % 3]
        p.claim(q, f"xh{s}")
        p.inc(q.dma_start(out=xh_sb[s][:, :], in_=Xh[tt_ * 128:(tt_ + 1) * 128, :]), f"xh{s}", 16)

    for tt in range(6):
        xh_load(tt)
    out_done = {}
    for tt in range(16):
        s = tt % 6
        ob_p = tt % 2
        p.wait(DV, "dv", attnfm_done)
        p.claim(DV, "o1_sb")
        o1 = p.inc(DV.tensor_tensor(
            o1_sb[:, :].rearrange("p (dt t2) -> p dt t2", dt=8),
            out_fm[:, :].rearrange("p (dt tt2) -> p dt tt2", dt=8)[:, :, ts(tt, 128)],
            attn_fm[:, :].rearrange("p (dt tt2) -> p dt tt2", dt=8)[:, :, ts(tt, 128)],
            ALU.add), "dv")
        p.wait(TE, "dv", o1)
        p.wait(TE, "idf")
        p.claim(TE, "ps_pt")
        for dt in range(8):
            ins = TE.transpose(ps_pt[dt // 4][:, (dt % 4) * 128:(dt % 4) * 128 + 128],
                               o1_sb[:, ts(dt, 128)], idf_s[:, :])
        tpm = p.inc(ins, "te")
        p.use("o1_sb", "te", tpm)
        p.wait(DV, "te", tpm)
        p.wait(DV, f"xh{s}")
        p.claim(DV, f"outb{ob_p}")
        DV.tensor_tensor(outb[:, ob_p * 1024:ob_p * 1024 + 512], ps_pt[0][:, :],
                         xh_sb[s][:, 0:512], ALU.add)
        ob = p.inc(DV.tensor_tensor(outb[:, ob_p * 1024 + 512:ob_p * 1024 + 1024], ps_pt[1][:, :],
                                    xh_sb[s][:, 512:1024], ALU.add), "dv")
        p.use("ps_pt", "dv", ob)
        p.use(f"xh{s}", "dv", ob)
        oq, osem = (SY, "ostr") if tt % 2 == 0 else (ACT, "osta")
        p.wait(oq, "dv", ob)
        od = p.inc(oq.dma_start(out=out[tt * 128:(tt + 1) * 128, :],
                                in_=outb[:, ob_p * 1024:(ob_p + 1) * 1024]), osem, 16)
        p.use(f"outb{ob_p}", osem, od)
        out_done[osem] = od
        if tt + 6 < 16:
            xh_load(tt + 6)
    p.wait(GP, "qg", p.cnt["qg"])
    p.wait(SY, "ostr", out_done["ostr"])
    p.wait(SY, "osta", out_done["osta"])

    scopeD.close()
    p.ctx.close()
    return nc


def kernel(**inputs):
    maps = _shard_inputs(inputs)
    nc = build_program(debug=DEBUG_OUTS)
    res = run_bass_kernel_spmd(nc, maps, core_ids=list(range(8)))
    full = np.zeros((B, T, D), np.float32)
    for c in range(8):
        b, h = c >> 1, c & 1
        full[b, h * TH:(h + 1) * TH, :] = res.results[c]["out"]
    if DEBUG_OUTS:
        kernel.debug = [dict(res.results[c]) for c in range(8)]
    return full


# version 19
# speedup vs baseline: 1.0326x; 1.0137x over previous
import numpy as np
import ml_dtypes
from contextlib import ExitStack

import concourse.bass as bass
import concourse.mybir as mybir
from concourse.bass_utils import run_bass_kernel_spmd

F32 = mybir.dt.float32
BF16 = mybir.dt.bfloat16
AF = mybir.ActivationFunctionType
ALU = mybir.AluOpType
AX = mybir.AxisListType
bf16 = ml_dtypes.bfloat16
ts = bass.ts

B, T, D = 4, 4096, 1024
L = 2 * T
N1, N2 = 128, 64
STAGES = 3
E, FFN = 8, 1408
DH, TH = D // 2, T // 2
NQ = DH // 16
FSH = STAGES * DH // 4
FROW = 16384

DEBUG_OUTS = ()
BUILD_UNTIL = None
SIM_LOCAL = False


def _tables():
    n1 = np.arange(N1)[:, None]
    k1 = np.arange(N1)[None, :]
    n2 = np.arange(N2)[:, None]
    k2 = np.arange(N2)[None, :]
    W128 = np.exp(-2j * np.pi * n1 * k1 / N1)
    W64 = np.exp(-2j * np.pi * n2 * k2 / N2)
    W64i = np.exp(2j * np.pi * k2.T * n2.T / N2)
    W128i = np.exp(2j * np.pi * k1.T * n1.T / N1)
    tw = np.exp(-2j * np.pi * np.arange(N1)[:, None] * np.arange(N2)[None, :] / L)

    def b_(x):
        return np.ascontiguousarray(x).astype(np.float32).astype(bf16)

    t = {}
    w128c = np.concatenate([W128.real[:64, :], W128.imag[:64, :]], axis=1)
    t["W128c"] = b_(np.concatenate([w128c, w128c], axis=0))
    for nm, m in [("W64re", W64.real), ("W64im", W64.imag), ("W64imn", -W64.imag),
                  ("W64ire", W64i.real), ("W64iim", W64i.imag), ("W64iimn", -W64i.imag)]:
        t[nm] = b_(np.concatenate([m, m], axis=0))
    t["W128ire"] = b_(W128i.real[:, :64] / L)
    t["W128iimn"] = b_(-W128i.imag[:, :64] / L)
    t["twre"] = b_(np.tile(tw.real[:, None, :], (1, 8, 1)).reshape(128, 512))
    t["twim"] = b_(np.tile(tw.imag[:, None, :], (1, 8, 1)).reshape(128, 512))
    ctm = np.conj(tw).T
    ctp = np.concatenate([ctm, ctm], axis=0)
    t["ctre"] = b_(np.tile(ctp.real[:, None, :], (1, 4, 1)).reshape(128, 512))
    t["ctim"] = b_(np.tile(ctp.imag[:, None, :], (1, 4, 1)).reshape(128, 512))
    ident = np.eye(128, dtype=np.float32)
    t["identb"] = b_(ident)
    t["identf"] = ident
    t["ones1"] = b_(np.ones((1, 128), np.float32))
    pos_t = np.arange(T, dtype=np.float32)[:, None]
    i = np.arange(D)[None, :]
    angle = pos_t / np.power(np.float32(10000.0), (2 * (i // 2)).astype(np.float32) / D)
    pos = np.where(i % 2 == 0, np.sin(angle), np.cos(angle)).astype(np.float32)
    t["posT"] = b_(pos.T)
    return t


def _shard_inputs(ins):
    X = np.asarray(ins["L0_attn_X"])
    Wp = np.asarray(ins["L0_attn_Wp"])
    bp = np.asarray(ins["L0_attn_bp"])
    Wf = np.asarray(ins["L0_attn_Wf"])
    bff = np.asarray(ins["L0_attn_bf"])
    gw = np.asarray(ins["L0_ffn_gate_w"])
    w1 = np.asarray(ins["L0_ffn_w1"])
    w2 = np.asarray(ins["L0_ffn_w2"])
    tabs = _tables()
    maps = []
    for c in range(8):
        b, h = c >> 1, c & 1
        dcols = np.arange(h * DH, h * DH + DH)
        pcols = np.concatenate([s * D + dcols for s in range(STAGES + 1)])
        r = c >> 1
        s_ = np.repeat(np.arange(STAGES), 128)
        j_ = np.tile(np.arange(128), STAGES)
        fcols = s_ * D + h * DH + r * 128 + j_
        cmask = np.zeros((128, 16), np.float32)
        cmask[:, h] = 1.0
        m = {
            "Xb": np.ascontiguousarray(X[b]),
            "Xh": np.ascontiguousarray(X[b, h * TH:(h + 1) * TH]),
            "Wp_c": np.ascontiguousarray(Wp[:, pcols]),
            "bp_c": np.ascontiguousarray(bp[pcols]).reshape(16, 128).T.copy(),
            "Wf_c": np.ascontiguousarray(Wf[:, fcols]),
            "bf_c": np.ascontiguousarray(bff[fcols]).reshape(FSH // 128, 128).T.copy(),
            "gwT": np.ascontiguousarray(gw.T),
            "w1": w1,
            "w2": w2,
            "cmask": cmask,
        }
        m.update(tabs)
        maps.append(m)
    return maps


class P:
    def __init__(self):
        self.nc = bass.Bass("TRN2", target_bir_lowering=False)
        self.ctx = ExitStack()
        self.cnt = {}
        self.sems = {}
        self.lastw = {}
        self.res = {}

    def sem(self, name):
        self.sems[name] = self.ctx.enter_context(self.nc.semaphore(name))
        self.cnt[name] = 0

    def qsem(self, name):
        if name not in self.sems:
            self.sem(name)
        return name

    def inc(self, instr, name, n=1):
        self.qsem(name)
        instr.then_inc(self.sems[name], n)
        self.cnt[name] += n
        return self.cnt[name]

    def wait(self, eng, name, val=None):
        self.qsem(name)
        if val is None:
            val = self.cnt[name]
        if val <= 0:
            return
        key = (id(eng), name)
        if self.lastw.get(key, -1) >= val:
            return
        eng.wait_ge(self.sems[name], val)
        self.lastw[key] = val

    def use(self, resname, semname, count):
        d = self.res.setdefault(resname, {})
        d[semname] = max(d.get(semname, 0), count)

    def claim(self, eng, resname):
        for s, c in self.res.get(resname, {}).items():
            self.wait(eng, s, c)

    def sbuf(self, name, shape, dtype, scope=None):
        return (scope or self.ctx).enter_context(self.nc.sbuf_tensor(name, shape, dtype))

    def psum(self, name, shape, dtype=F32, scope=None):
        return (scope or self.ctx).enter_context(self.nc.psum_tensor(name, shape, dtype))


def build_program(debug=()):
    p = P()
    nc = p.nc

    def din(name, shape, dtype=F32):
        return nc.dram_tensor(name, shape, dtype, kind="ExternalInput").ap()

    Xb = din("Xb", [T, D]); Xh = din("Xh", [TH, D])
    Wp_c = din("Wp_c", [D, 4 * DH]); bp_c = din("bp_c", [128, 16])
    Wf_c = din("Wf_c", [D, FSH]); bf_c = din("bf_c", [128, FSH // 128])
    gwT = din("gwT", [D, E])
    w1 = din("w1", [E, D, 2 * FFN]); w2 = din("w2", [E, FFN, D])
    cmask = din("cmask", [128, 16])
    CT = {nm: din(nm, shp, BF16) for nm, shp in [
        ("W128c", [128, 256]), ("W64re", [128, 64]), ("W64im", [128, 64]),
        ("W64imn", [128, 64]), ("W64ire", [128, 64]), ("W64iim", [128, 64]),
        ("W64iimn", [128, 64]), ("W128ire", [128, 64]), ("W128iimn", [128, 64]),
        ("twre", [128, 512]), ("twim", [128, 512]), ("ctre", [128, 512]),
        ("ctim", [128, 512]), ("identb", [128, 128]), ("ones1", [1, 128]),
        ("posT", [D, T])]}
    identf = din("identf", [128, 128], F32)

    out = nc.dram_tensor("out", [TH, D], F32, kind="ExternalOutput").ap()
    dbg = {}
    for name in debug:
        shp = {"xnT": [D, T], "xnhT": [D, TH], "xp": [4 * DH, T],
               "fhbm": [FSH, T], "Fsh": [FSH, FROW], "Ffull": [4 * FSH, FROW],
               "y1": [128, NQ * 512], "y2": [128, NQ * 512], "attn": [DH, T],
               "attnfm": [D, TH], "comb": [128, 128], "ffn": [D, TH]}[name]
        dbg[name] = nc.dram_tensor("dbg_" + name, shp, F32, kind="ExternalOutput").ap()

    xp_hbm = nc.dram_tensor("xp_hbm", [4 * DH, T], BF16)
    f_hbm = nc.dram_tensor("f_hbm", [FSH, T], BF16)
    agF_in = nc.dram_tensor("agF_in", [FSH, FROW], BF16)
    agF_s = [nc.dram_tensor(f"agF_o{s}", [512, FROW], BF16) for s in range(STAGES)]
    agA_in = nc.dram_tensor("agA_in", [DH, T], BF16)
    agA_a = nc.dram_tensor("agA_a", [2 * 384, T], BF16)
    agA_b = nc.dram_tensor("agA_b", [2 * 128, T], BF16)

    for s in ["qs", "qa", "qg", "te", "dv", "ac", "gp", "cc"]:
        p.sem(s)
    TE, DV, ACT, GP, SY = nc.tensor, nc.vector, nc.scalar, nc.gpsimd, nc.sync

    def full_barrier():
        for e in [TE, DV, ACT, GP, SY]:
            for s in list(p.cnt.keys()):
                if s != "cc":
                    p.wait(e, s)

    xnhT = p.sbuf("xnhT", [128, 8 * TH], BF16)
    cons = p.sbuf("cons", [128, 2944], BF16)
    w128c_s = cons[:, 0:256]
    w64_s = {nm: cons[:, 256 + 64 * i:256 + 64 * i + 64]
             for i, nm in enumerate(["W64re", "W64im", "W64imn", "W64ire", "W64iim", "W64iimn"])}
    w128i_s = cons[:, 640:768]
    twre_s = cons[:, 768:1280]; twim_s = cons[:, 1280:1792]
    ctre_s = cons[:, 1792:2304]; ctim_s = cons[:, 2304:2816]
    idb_s = cons[:, 2816:2944]
    ones_s = p.sbuf("ones_s", [1, 128], BF16)
    bp_s = p.sbuf("bp_s", [128, 16], F32)
    bf_s = p.sbuf("bf_s", [128, FSH // 128], F32)
    cm_s = p.sbuf("cm_s", [128, 16], F32)
    eps_s = p.sbuf("eps_s", [128, 1], F32)

    scopeA = ExitStack()
    scopeB = ExitStack()
    scopeC = ExitStack()
    scopeD = ExitStack()

    blk = p.ctx.enter_context(nc.Block())


    def _finish_stub(open_scopes):
        for e in [TE, DV, ACT, GP, SY]:
            for s in list(p.cnt.keys()):
                p.wait(e, s)
        GP.memset(out_fm[:, 0:1024], 0.0)
        p.inc(GP.engine_nop(), "gp")
        p.wait(GP, "gp", p.cnt["gp"])
        for it in range(TH // 128):
            p.inc(GP.dma_start(out=out[it * 128:(it + 1) * 128, :], in_=out_fm[:, 0:1024]), "qg", 16)
        p.wait(GP, "qg", p.cnt["qg"])
        p.wait(SY, "qg", p.cnt["qg"])
        for sc in open_scopes:
            sc.close()
    p.inc(DV.memset(eps_s[:, :], 1e-5), "dv")
    p.inc(SY.dma_start(out=w128c_s, in_=CT["W128c"]), "qs", 16)
    for nm in ["W64re", "W64im", "W64imn", "W64ire", "W64iim", "W64iimn"]:
        p.inc(SY.dma_start(out=w64_s[nm], in_=CT[nm]), "qs", 16)
    p.inc(SY.dma_start(out=w128i_s[:, 0:64], in_=CT["W128ire"]), "qs", 16)
    p.inc(SY.dma_start(out=w128i_s[:, 64:128], in_=CT["W128iimn"]), "qs", 16)
    for dst, nm in [(twre_s, "twre"), (twim_s, "twim"), (ctre_s, "ctre"),
                    (ctim_s, "ctim"), (idb_s, "identb")]:
        p.inc(SY.dma_start(out=dst, in_=CT[nm]), "qs", 16)
    p.inc(SY.dma_start(out=ones_s[:, :], in_=CT["ones1"]), "qs", 16)
    p.inc(SY.dma_start(out=bp_s[:, :], in_=bp_c), "qs", 16)
    p.inc(SY.dma_start(out=bf_s[:, :], in_=bf_c), "qs", 16)
    p.inc(SY.dma_start(out=cm_s[:, :], in_=cmask), "qs", 16)
    consts_done = p.cnt["qs"]

    s1b = [p.sbuf("s1b0", [128, 2048], BF16)]
    xtw = s1b
    twt = [p.sbuf("twt0", [128, 2048], BF16)]
    s2in = [p.sbuf("s2in0", [128, 2048], BF16)]
    wsb = [p.sbuf("wsb0", [128, 2048], BF16)]
    i1b = [p.sbuf("i1b0", [128, 2048], BF16)]
    i1t = i1b
    s3in = [p.sbuf("s3in0", [128, 2048], BF16)]
    fscr = p.sbuf("fscr", [128, 5120], BF16)
    wf_sb = fscr[:, 0:3072]
    pos_sb = fscr[:, 3072:4096]
    f_ev = fscr[:, 4096:4608]
    fin_sb = fscr[:, 4608:5120]
    gw_sb = p.sbuf("gw_sb", [128, 64], BF16)
    comb_tok = p.sbuf("comb_tok", [128, 128], BF16)
    gst = p.sbuf("gst", [128, 8], F32)
    esb = p.sbuf("esb", [128, 8], F32)
    msk = p.sbuf("msk", [128, 16], F32)
    combf = p.sbuf("combf", [128, 8], F32)
    ps_A = [[p.psum(f"cpA_{i}", [128, 512]) for i in range(2)]]
    ps_T = [p.psum("cpT", [128, 1024], BF16)]
    ps_Z = [p.psum("cpZ", [128, 512])]
    ps_Zb = p.psum("cpZ1", [128, 512])
    ps_s1 = ps_A[0]
    f_psn3 = [ps_A[0][0], ps_A[0][1], ps_Z[0]]
    sc_ps = ps_Z[0][:, 0:8]
    NCI = FSH // 128
    f_written = [0, 0]
    F_thr = [0, 0, 0]

    def emit_fag(s_):
        p.wait(GP, "fz0", p.cnt["fz0"])
        p.wait(GP, "fzg", p.cnt.get("fzg", 0))
        if SIM_LOCAL:
            for r_ in range(4):
                p.inc(GP.dma_start(out=agF_s[s_].ap()[r_ * 128:(r_ + 1) * 128, :],
                                   in_=agF_in.ap()[s_ * 128:(s_ + 1) * 128, :]), "cc", 16)
            GP.wait_ge(p.sems["cc"], p.cnt["cc"])
        else:
            p.inc(GP.collective_compute(
                "AllGather", ALU.bypass, replica_groups=[[0, 2, 4, 6], [1, 3, 5, 7]],
                ins=[agF_in.ap()[s_ * 128:(s_ + 1) * 128, :].opt()],
                outs=[agF_s[s_].ap().opt()]), "cc", 1)
        F_thr[s_] = p.cnt["cc"]

    def fwd_first_half(par, fpar, rhs_lo, rhs_hi, fld_fn, st_qi, stp=None, wbuf=None):
        o = {}
        W, wres = (wsb[par], f"wsb{par}") if wbuf is None else wbuf
        A0, A1 = ps_A[par]
        p.wait(TE, "qs", consts_done)
        p.claim(TE, f"psA{par}")
        TE.matmul(A0[:, :], w128c_s[0:64, 0:128], rhs_lo, start=True, stop=True)
        m1 = p.inc(TE.matmul(A1[:, :], w128c_s[0:64, 128:256], rhs_lo, start=True, stop=True), "te")
        p.wait(ACT, "te", m1)
        p.claim(ACT, f"s1b{par}")
        ACT.activation(s1b[par][:, 0:512], A0[:, :], AF.Copy)
        e1 = p.inc(ACT.activation(s1b[par][:, 512:1024], A1[:, :], AF.Copy), "ac")
        p.wait(TE, "ac", e1)
        TE.matmul(A0[:, :], w128c_s[64:128, 0:128], rhs_hi, start=True, stop=True,
                  tile_position=(64, 0))
        m2 = p.inc(TE.matmul(A1[:, :], w128c_s[64:128, 128:256], rhs_hi, start=True, stop=True,
                             tile_position=(64, 0)), "te")
        p.wait(ACT, "te", m2)
        ACT.activation(s1b[par][:, 1024:1536], A0[:, :], AF.Copy)
        e2 = p.inc(ACT.activation(s1b[par][:, 1536:2048], A1[:, :], AF.Copy), "ac")
        o["s1ev"] = e2
        p.use(f"psA{par}", "ac", e2)
        p.claim(DV, f"xtw{par}")
        p.claim(DV, f"twt{par}")
        tq = twt[par]
        tws = []
        for g in range(2):
            p.wait(DV, "ac", e1 if g == 0 else e2)
            re = s1b[par][:, g * 1024:g * 1024 + 512]
            im = s1b[par][:, g * 1024 + 512:g * 1024 + 1024]
            DV.tensor_tensor(tq[:, 0:512], re, twre_s, ALU.mult)
            DV.tensor_tensor(tq[:, 512:1024], im, twim_s, ALU.mult)
            DV.tensor_tensor(tq[:, 1024:1536], re, twim_s, ALU.mult)
            p.inc(DV.tensor_tensor(tq[:, 1536:2048], im, twre_s, ALU.mult), "dv")
            p.wait(DV, "dv", p.cnt["dv"])
            DV.tensor_tensor(xtw[par][:, g * 1024:g * 1024 + 512], tq[:, 0:512],
                             tq[:, 512:1024], ALU.subtract)
            tw = p.inc(DV.tensor_tensor(xtw[par][:, g * 1024 + 512:g * 1024 + 1024],
                                        tq[:, 1024:1536], tq[:, 1536:2048], ALU.add), "dv")
            p.wait(DV, "dv", tw)
            tws.append(tw)
        o["tw"] = tw
        p.use(f"s1b{par}", "dv", tw)
        p.use(f"twt{par}", "dv", tw)
        if stp:
            stp(2)
        tevs = []
        for half in range(2):
            p.wait(TE, "dv", tws[half])
            if half == 0:
                p.claim(TE, f"psT{par}")
            else:
                p.claim(TE, f"psA{par}")
            for pr in range(4):
                for pl in range(2):
                    t8 = pr * 2 + pl
                    sl = xtw[par][:, half * 1024 + pl * 512 + pr * 128:
                                  half * 1024 + pl * 512 + pr * 128 + 128]
                    if half == 0:
                        ins = TE.transpose(ps_T[par][:, ts(t8, 128)], sl, idb_s)
                    else:
                        ins = TE.matmul((A0 if t8 < 4 else A1)[:, ts(t8 % 4, 128)],
                                        sl, idb_s, start=True, stop=True)
            tra = p.inc(ins, "te")
            p.wait(ACT, "te", tra)
            if half == 0:
                p.claim(ACT, f"s2in{par}")
                tev = p.inc(ACT.activation(s2in[par][:, 0:1024],
                                           ps_T[par][:, :], AF.Copy), "ac")
                p.use(f"psT{par}", "ac", tev)
            else:
                ACT.activation(s2in[par][:, 1024:1536], A0[:, :], AF.Copy)
                tev = p.inc(ACT.activation(s2in[par][:, 1536:2048], A1[:, :], AF.Copy), "ac")
                p.use(f"psA{par}", "ac", tev)
            tevs.append(tev)
        o["s2ev"] = ("ac", tev)
        p.use(f"xtw{par}", "te", p.cnt["te"])
        lay = s2in[par][:, :].rearrange("p (c pr pl k) -> p c pr pl k", c=2, pr=4, pl=2)
        fmc = []
        for c2 in range(4):
            Z = ps_Z[par] if c2 % 2 == 0 else ps_Zb
            g, prh = c2 // 2, (c2 % 2) * 2
            rc = lay[:, g, prh:prh + 2, 0, :]
            ic = lay[:, g, prh:prh + 2, 1, :]
            if stp:
                stp(1)
            p.wait(TE, "ac", tevs[g])
            if c2 == 0:
                p.claim(TE, f"psZ{par}")
            elif c2 == 1:
                p.claim(TE, "psZb")
            else:
                p.wait(TE, "dv", fmc[c2 - 2])
            TE.matmul(Z[0:64, 0:256], w64_s["W64re"][0:64, :], rc[0:64], start=True, stop=False)
            TE.matmul(Z[0:64, 0:256], w64_s["W64imn"][0:64, :], ic[0:64], start=False, stop=True)
            TE.matmul(Z[64:128, 0:256], w64_s["W64re"][64:128, :], rc[64:128], start=True, stop=False,
                      tile_position=(64, 64))
            TE.matmul(Z[64:128, 0:256], w64_s["W64imn"][64:128, :], ic[64:128], start=False, stop=True,
                      tile_position=(64, 64))
            TE.matmul(Z[0:64, 256:512], w64_s["W64im"][0:64, :], rc[0:64], start=True, stop=False)
            TE.matmul(Z[0:64, 256:512], w64_s["W64re"][0:64, :], ic[0:64], start=False, stop=True)
            TE.matmul(Z[64:128, 256:512], w64_s["W64im"][64:128, :], rc[64:128], start=True, stop=False,
                      tile_position=(64, 64))
            zmm = p.inc(TE.matmul(Z[64:128, 256:512], w64_s["W64re"][64:128, :], ic[64:128],
                                  start=False, stop=True, tile_position=(64, 64)), "te")
            p.wait(DV, "te", zmm)
            fw = fld_fn(c2) if fld_fn is not None else None
            wof = c2 * 512
            tq = twt[par]
            p.claim(DV, f"twt{par}")
            if c2 == 0:
                p.claim(DV, wres)
            if fw is not None:
                fr = fre_sb[par][:, 0:256].rearrange("p (w k) -> p w k", w=2) if False else None
                fre = fre_sb[fpar][:, c2 * 512:c2 * 512 + 256]
                fim = fre_sb[fpar][:, c2 * 512 + 256:c2 * 512 + 512]
                DV.tensor_tensor(tq[:, 0:256], Z[:, 0:256], fre, ALU.mult)
                DV.tensor_tensor(tq[:, 256:512], Z[:, 256:512], fim, ALU.mult)
                DV.tensor_tensor(tq[:, 512:768], Z[:, 0:256], fim, ALU.mult)
                p.inc(DV.tensor_tensor(tq[:, 768:1024], Z[:, 256:512], fre, ALU.mult), "dv")
                p.wait(DV, "dv", p.cnt["dv"])
                DV.tensor_tensor(W[:, wof:wof + 256], tq[:, 0:256], tq[:, 256:512], ALU.subtract)
                fm = p.inc(DV.tensor_tensor(W[:, wof + 256:wof + 512], tq[:, 512:768],
                                            tq[:, 768:1024], ALU.add), "dv")
                p.wait(DV, "dv", fm)
                p.use(f"twt{par}", "dv", fm)
                p.use("fre0", "dv", fm) if c2 == 3 else None
            else:
                DV.tensor_copy(W[:, wof:wof + 256], Z[:, 0:256])
                fm = p.inc(DV.tensor_copy(W[:, wof + 256:wof + 512], Z[:, 256:512]), "dv")
            fmc.append(fm)
        o["fm01"] = fmc[1]
        o["fm"] = fmc[3]
        p.use(f"psZ{par}", "dv", fmc[2])
        p.use("psZb", "dv", fmc[3])
        p.use(f"s2in{par}", "te", p.cnt["te"])
        return o

    def bwd_second_half(par, gpar, gate_to_y, qi, st, gld, o, stp=None):
        A0, A1 = ps_A[par]
        i1es = []
        for cp in range(2):
            if stp:
                stp(1)
            p.wait(TE, "dv", o["fm01"] if cp == 0 else o["fm"])
            p.claim(TE, f"psA{par}")
            for c2 in (2 * cp, 2 * cp + 1):
                wof = c2 * 512
                wre = wsb[par][:, wof:wof + 256].rearrange("p (pr k) -> p pr k", pr=2)
                wim = wsb[par][:, wof + 256:wof + 512].rearrange("p (pr k) -> p pr k", pr=2)
                co = (c2 % 2) * 256
                TE.matmul(A0[0:64, co:co + 256], w64_s["W64ire"][0:64, :], wre[0:64], start=True, stop=False)
                TE.matmul(A0[0:64, co:co + 256], w64_s["W64iimn"][0:64, :], wim[0:64], start=False, stop=True)
                TE.matmul(A0[64:128, co:co + 256], w64_s["W64ire"][64:128, :], wre[64:128], start=True,
                          stop=False, tile_position=(64, 64))
                TE.matmul(A0[64:128, co:co + 256], w64_s["W64iimn"][64:128, :], wim[64:128], start=False,
                          stop=True, tile_position=(64, 64))
                TE.matmul(A1[0:64, co:co + 256], w64_s["W64iim"][0:64, :], wre[0:64], start=True, stop=False)
                TE.matmul(A1[0:64, co:co + 256], w64_s["W64ire"][0:64, :], wim[0:64], start=False, stop=True)
                TE.matmul(A1[64:128, co:co + 256], w64_s["W64iim"][64:128, :], wre[64:128], start=True,
                          stop=False, tile_position=(64, 64))
                i1m = p.inc(TE.matmul(A1[64:128, co:co + 256], w64_s["W64ire"][64:128, :], wim[64:128],
                                      start=False, stop=True, tile_position=(64, 64)), "te")
            p.wait(ACT, "te", i1m)
            if cp == 0:
                p.claim(ACT, f"i1b{par}")
            ACT.activation(i1b[par][:, cp * 1024:cp * 1024 + 512], A0[:, :], AF.Copy)
            i1e = p.inc(ACT.activation(i1b[par][:, cp * 1024 + 512:cp * 1024 + 1024], A1[:, :], AF.Copy), "ac")
            p.use(f"psA{par}", "ac", i1e)
            i1es.append(i1e)
        p.claim(DV, f"i1t{par}")
        tq = twt[par]
        ctws = []
        for cp in range(2):
            p.wait(DV, "ac", i1es[cp])
            re = i1b[par][:, cp * 1024:cp * 1024 + 512]
            im = i1b[par][:, cp * 1024 + 512:cp * 1024 + 1024]
            ctr = ctre_s[:, 0:512]
            cti = ctim_s[:, 0:512]
            p.claim(DV, f"twt{par}")
            DV.tensor_tensor(tq[:, 0:512], re, ctr, ALU.mult)
            DV.tensor_tensor(tq[:, 512:1024], im, cti, ALU.mult)
            DV.tensor_tensor(tq[:, 1024:1536], re, cti, ALU.mult)
            p.inc(DV.tensor_tensor(tq[:, 1536:2048], im, ctr, ALU.mult), "dv")
            p.wait(DV, "dv", p.cnt["dv"])
            DV.tensor_tensor(i1t[par][:, cp * 1024:cp * 1024 + 512], tq[:, 0:512],
                             tq[:, 512:1024], ALU.subtract)
            ctw = p.inc(DV.tensor_tensor(i1t[par][:, cp * 1024 + 512:cp * 1024 + 1024],
                                         tq[:, 1024:1536], tq[:, 1536:2048], ALU.add), "dv")
            p.wait(DV, "dv", ctw)
            p.use(f"twt{par}", "dv", ctw)
            ctws.append(ctw)
        p.use(f"i1b{par}", "dv", ctw)
        if stp:
            stp(2)
        for half in range(2):
            p.wait(TE, "dv", ctws[half])
            if half == 0:
                p.claim(TE, f"psT{par}")
            else:
                p.claim(TE, f"psA{par}")
            for i in range(4):
                c2 = half * 2 + i // 2
                pr = i % 2
                cp, co = c2 // 2, (c2 % 2) * 256
                for pl in range(2):
                    t8 = i * 2 + pl
                    sl = i1t[par][:, cp * 1024 + pl * 512 + co + pr * 128:
                                  cp * 1024 + pl * 512 + co + pr * 128 + 128]
                    if half == 0:
                        ins = TE.transpose(ps_T[par][:, ts(t8, 128)], sl, idb_s)
                    else:
                        ins = TE.matmul((A0 if t8 < 4 else A1)[:, ts(t8 % 4, 128)],
                                        sl, idb_s, start=True, stop=True)
            trb = p.inc(ins, "te")
            p.wait(ACT, "te", trb)
            if half == 0:
                p.claim(ACT, f"s3in{par}")
                s3e = p.inc(ACT.activation(s3in[par][:, 0:1024],
                                           ps_T[par][:, :], AF.Copy), "ac")
                p.use(f"psT{par}", "ac", s3e)
            else:
                ACT.activation(s3in[par][:, 1024:1536], A0[:, :], AF.Copy)
                s3e = p.inc(ACT.activation(s3in[par][:, 1536:2048], A1[:, :], AF.Copy), "ac")
                p.use(f"psA{par}", "ac", s3e)
        p.use(f"i1t{par}", "te", p.cnt["te"])
        p.wait(TE, "ac", s3e)
        p.claim(TE, f"psZ{par}")
        Z = ps_Z[par]
        lay3 = s3in[par][:, :].rearrange("p (h i pl n) -> p h i pl n", h=2, i=4, pl=2)
        TE.matmul(Z[0:64, :], w128i_s[:, 0:64], lay3[:, 0, :, 0, :], start=True, stop=False)
        TE.matmul(Z[0:64, :], w128i_s[:, 64:128], lay3[:, 0, :, 1, :], start=False, stop=True)
        TE.matmul(Z[64:128, :], w128i_s[:, 0:64], lay3[:, 1, :, 0, :], start=True, stop=False,
                  tile_position=(0, 64))
        invsb = p.inc(TE.matmul(Z[64:128, :], w128i_s[:, 64:128], lay3[:, 1, :, 1, :],
                                start=False, stop=True, tile_position=(0, 64)), "te")
        p.use(f"s3in{par}", "te", invsb)
        p.wait(DV, "te", invsb)
        p.wait(DV, f"gl{gpar}")
        if gate_to_y:
            g = p.inc(DV.tensor_tensor(y_sb[:, ts(qi, 512)], Z[:, :], gt_sb[gpar][:, :], ALU.mult), "dv")
        else:
            p.claim(DV, "att0")
            g = p.inc(DV.tensor_tensor(att_sb[gpar][:, :], Z[:, :], gt_sb[gpar][:, :], ALU.mult), "dv")
            p.wait(SY, "dv", g)
            for half in range(2):
                dst = bass.AP(agA_in, (16 * qi + 8 * half) * T, [[64, 64], [T, 8], [1, 64]])
                p.inc(SY.dma_start(out=dst,
                                   in_=att_sb[gpar][64 * half:64 * half + 64, :]
                                   .rearrange("p (c n) -> p c n", c=8)), f"as{gpar}", 16)
            p.use("att0", f"as{gpar}", p.cnt[f"as{gpar}"])
        p.use(f"psZ{par}", "dv", g)
        p.use("gt0", "dv", g)
        return g


    def gating_unit(tt):
        p.claim(TE, "psZ0")
        for kt in range(8):
            ins = TE.matmul(sc_ps, xnhT[:, kt * TH + tt * 128: kt * TH + tt * 128 + 128],
                            gw_sb[:, kt * 8:kt * 8 + 8], start=(kt == 0), stop=(kt == 7))
        scm = p.inc(ins, "te")
        p.wait(DV, "te", scm)
        p.claim(DV, "gst")
        p.inc(DV.reduce_max(gst[:, 0:1], sc_ps, axis=AX.X, negate=True), "dv")
        p.wait(ACT, "dv", p.cnt["dv"])
        p.claim(ACT, "esb")
        p.inc(ACT.activation(esb[:, :], sc_ps, AF.Exp, bias=gst[:, 0:1]), "ac")
        p.use("psZ0", "ac", p.cnt["ac"])
        p.use("gst", "ac", p.cnt["ac"])
        p.wait(DV, "ac", p.cnt["ac"])
        p.inc(DV.reduce_max(gst[:, 1:2], esb[:, :], axis=AX.X), "dv")
        p.wait(DV, "dv", p.cnt["dv"])
        p.inc(DV.tensor_scalar(msk[:, 0:8], esb[:, :], gst[:, 1:2], None, ALU.is_equal), "dv")
        p.wait(DV, "dv", p.cnt["dv"])
        p.inc(DV.scalar_tensor_tensor(msk[:, 8:16], msk[:, 0:8], -2.0, esb[:, :], ALU.mult, ALU.add), "dv")
        p.wait(DV, "dv", p.cnt["dv"])
        p.inc(DV.reduce_max(gst[:, 2:3], msk[:, 8:16], axis=AX.X), "dv")
        p.wait(DV, "dv", p.cnt["dv"])
        p.inc(DV.tensor_scalar(msk[:, 8:16], msk[:, 8:16], gst[:, 2:3], None, ALU.is_equal), "dv")
        p.inc(DV.tensor_tensor(gst[:, 3:4], gst[:, 1:2], gst[:, 2:3], ALU.add), "dv")
        p.wait(DV, "dv", p.cnt["dv"])
        p.inc(DV.reciprocal(gst[:, 4:5], gst[:, 3:4]), "dv")
        p.wait(DV, "dv", p.cnt["dv"])
        p.inc(DV.tensor_tensor(gst[:, 5:6], gst[:, 1:2], gst[:, 4:5], ALU.mult), "dv")
        p.inc(DV.tensor_tensor(gst[:, 6:7], gst[:, 2:3], gst[:, 4:5], ALU.mult), "dv")
        p.wait(DV, "dv", p.cnt["dv"])
        p.inc(DV.tensor_scalar(combf[:, :], msk[:, 0:8], gst[:, 5:6], None, ALU.mult), "dv")
        p.wait(DV, "dv", p.cnt["dv"])
        p.inc(DV.scalar_tensor_tensor(combf[:, :], msk[:, 8:16], gst[:, 6:7], combf[:, :],
                                      ALU.mult, ALU.add), "dv")
        p.wait(DV, "dv", p.cnt["dv"])
        cb = p.inc(DV.tensor_copy(comb_tok[:, tt * 8:tt * 8 + 8], combf[:, :]), "dv")
        p.use("gst", "dv", cb)
        p.use("esb", "dv", cb)
        if "comb" in dbg:
            p.wait(GP, "dv", cb)
            p.inc(GP.dma_start(out=dbg["comb"][:, tt * 8:tt * 8 + 8], in_=combf[:, :]), "qg", 16)

    def fproj_gen():
        for kt_ in range(8):
            p.inc(GP.dma_start(out=wf_sb[:, ts(kt_, FSH)],
                               in_=Wf_c[kt_ * 128:(kt_ + 1) * 128, :]), "wf", 16)
        p.wait(TE, "wf")
        pos_used = {}
        fpk = 0
        for tc in range(8):
            for kt_ in range(8):
                lb = fpk % 2
                if fpk >= 2:
                    p.wait(GP, "te", pos_used[fpk - 2])
                p.inc(GP.dma_start(out=pos_sb[:, ts(lb, 512)],
                                   in_=CT["posT"][kt_ * 128:(kt_ + 1) * 128, ts(tc, 512)]),
                      f"po{lb}", 16)
                p.wait(TE, f"po{lb}")
                if tc == 0 and kt_ == 0:
                    p.claim(TE, "psA0")
                    p.claim(TE, "psZ0")
                elif kt_ == 0:
                    p.wait(TE, "ac", p.cnt["ac"])
                for ci_ in range(NCI):
                    ins = TE.matmul(f_psn3[ci_][:, :],
                                    wf_sb[:, kt_ * FSH + ci_ * 128: kt_ * FSH + ci_ * 128 + 128],
                                    pos_sb[:, ts(lb, 512)],
                                    start=(kt_ == 0), stop=(kt_ == 7))
                pos_used[fpk] = p.inc(ins, "te")
                fpk += 1
                yield
            mm = p.cnt["te"]
            p.wait(ACT, "te", mm)
            for ci_ in range(NCI):
                p.claim(ACT, "f_ev")
                fpv = p.inc(ACT.activation(f_ev[:, :], f_psn3[ci_][:, :],
                                           AF.Identity, bias=bf_s[:, ci_:ci_ + 1]), "ac")
                p.wait(SY, "ac", fpv)
                fst = p.inc(SY.dma_start(out=f_hbm.ap()[ci_ * 128:(ci_ + 1) * 128, ts(tc, 512)],
                                         in_=f_ev[:, :]), "fs0", 16)
                p.use("f_ev", "fs0", fst)
            yield
        p.use("psA0", "ac", p.cnt["ac"])
        p.use("psZ0", "ac", p.cnt["ac"])
        f_written[0] = p.cnt["fs0"]
        if "fhbm" in dbg:
            p.wait(GP, "fs0", f_written[0])
            p.inc(GP.dma_start(out=dbg["fhbm"], in_=f_hbm.ap()), "qg", 16)

    def fprep_gen():
        p.wait(ACT, "fs0", f_written[0])
        for qi_ in range(FSH // 16):
            row0 = 16 * qi_
            p.claim(ACT, "fin_sb")
            for half in range(2):
                fsrc = bass.AP(f_hbm, (row0 + 8 * half) * T, [[64, 64], [T, 8], [1, 64]])
                p.inc(ACT.dma_start(out=fin_sb[64 * half:64 * half + 64, :]
                                    .rearrange("p (c n) -> p c n", c=8), in_=fsrc), "fi0", 16)
            p.wait(TE, "fi0")
            wb = (wsb[0], "wsb0") if qi_ % 2 == 0 else (s3in[0], "s3in0")
            fq, fsem = GP, ("fz0" if qi_ % 2 == 0 else "fzg")
            o = fwd_first_half(0, 0, fin_sb[0:64, :], fin_sb[64:128, :], None, None,
                               wbuf=wb)
            p.use("fin_sb", "te", p.cnt["te"])
            p.wait(fq, "dv", o["fm"])
            for c2 in range(4):
                for pl in range(2):
                    for half in range(2):
                        dst = bass.AP(agF_in, (row0 + 4 * c2 + half) * FROW + pl * 8192,
                                      [[128, 64], [2 * FROW, 2], [1, 128]])
                        ins = fq.dma_start(out=dst,
                                           in_=wb[0][64 * half:64 * half + 64,
                                                     c2 * 512 + pl * 256:c2 * 512 + pl * 256 + 256]
                                           .rearrange("p (pr k) -> p pr k", pr=2))
                        p.inc(ins, fsem, 16)
            p.use(wb[1], fsem, p.cnt[fsem])
            if qi_ == 7:
                emit_fag(0)
            yield

    def _pre_units():
        yield from fproj_gen()
        yield from fprep_gen()
    _preit = _pre_units()
    _pre_done = [False]

    def step_pre(n=1):
        if _pre_done[0]:
            return
        for _ in range(n):
            try:
                next(_preit)
            except StopIteration:
                _pre_done[0] = True
                return

    xnT = p.sbuf("xnT", [128, 8 * T], BF16, scope=scopeA)

    ln_xt = p.sbuf("ln_xt", [128, 2 * D], F32, scope=scopeA)
    ln_sq = p.sbuf("ln_sq", [128, D], F32, scope=scopeA)
    ln_st = p.sbuf("ln_st", [128, 16], F32, scope=scopeA)
    ln_nb = p.sbuf("ln_nb", [128, 2 * D], BF16, scope=scopeA)
    ln_ps = p.psum("ln_ps", [128, 1024], BF16, scope=scopeA)

    def ln_phase(src_ap, n_tok, dstT, tag, stp=None):
        xt, sq, stat, xnb, pst = ln_xt, ln_sq, ln_st, ln_nb, ln_ps
        ntile = n_tok // 128
        tr_done, ev_done, tr_evac = {}, {}, {}
        for it in range(ntile):
            if stp:
                stp()
            bo = (it % 2) * D
            so = (it % 2) * 8
            xs = xt[:, bo:bo + D]
            qx = f"lx{tag}{it % 2}"
            if it >= 2:
                p.wait(SY, "ac", ev_done[it - 2])
            p.inc(SY.dma_start(out=xs, in_=src_ap[it * 128:(it + 1) * 128, :]), qx, 16)
            p.wait(DV, qx)
            if it >= 2:
                p.wait(DV, "ac", ev_done[it - 2])
            p.inc(DV.reduce_sum(stat[:, so + 0:so + 1], xs, axis=AX.X), "dv")
            p.wait(ACT, qx)
            if it >= 2:
                p.wait(ACT, "dv", p.cnt["dv"])
            p.inc(ACT.activation(sq[:, :], xs, AF.Square), "ac")
            p.wait(DV, "ac", p.cnt["ac"])
            p.inc(DV.reduce_sum(stat[:, so + 1:so + 2], sq[:, :], axis=AX.X), "dv")
            p.wait(DV, "dv", p.cnt["dv"])
            p.inc(DV.tensor_scalar(stat[:, so + 2:so + 3], stat[:, so + 0:so + 1], 1.0 / D, None, ALU.mult), "dv")
            p.inc(DV.tensor_scalar(stat[:, so + 3:so + 4], stat[:, so + 1:so + 2], 1.0 / D, None, ALU.mult), "dv")
            p.wait(DV, "dv", p.cnt["dv"])
            p.inc(DV.tensor_tensor(stat[:, so + 4:so + 5], stat[:, so + 2:so + 3], stat[:, so + 2:so + 3], ALU.mult), "dv")
            p.wait(DV, "dv", p.cnt["dv"])
            p.inc(DV.tensor_tensor(stat[:, so + 5:so + 6], stat[:, so + 3:so + 4], stat[:, so + 4:so + 5], ALU.subtract), "dv")
            p.wait(ACT, "dv", p.cnt["dv"])
            p.inc(ACT.activation(stat[:, so + 6:so + 7], stat[:, so + 5:so + 6], AF.Ln, bias=eps_s[:, :]), "ac")
            p.wait(ACT, "ac", p.cnt["ac"])
            p.inc(ACT.activation(stat[:, so + 6:so + 7], stat[:, so + 6:so + 7], AF.Exp, scale=-0.5), "ac")
            p.wait(DV, "ac", p.cnt["ac"])
            p.inc(DV.tensor_tensor(stat[:, so + 7:so + 8], stat[:, so + 2:so + 3], stat[:, so + 6:so + 7], ALU.mult), "dv")
            p.wait(DV, "dv", p.cnt["dv"])
            p.inc(DV.tensor_scalar(stat[:, so + 7:so + 8], stat[:, so + 7:so + 8], -1.0, None, ALU.mult), "dv")
            p.wait(ACT, "dv", p.cnt["dv"])
            if it >= 2:
                p.wait(ACT, "te", tr_done[it - 2])
            xnbs = xnb[:, bo:bo + D]
            ev_done[it] = p.inc(ACT.activation(xnbs, xs, AF.Identity,
                                               bias=stat[:, so + 7:so + 8],
                                               scale=stat[:, so + 6:so + 7]), "ac")
            p.wait(TE, "ac", p.cnt["ac"])
            p.wait(TE, "qs", consts_done)
            if it >= 1:
                p.wait(TE, "dv", tr_evac[it - 1])
            for j in range(8):
                ins = TE.transpose(pst[:, ts(j, 128)], xnbs[:, ts(j, 128)], idb_s)
            tr_done[it] = p.inc(ins, "te")
            p.wait(DV, "te", tr_done[it])
            tr_evac[it] = p.inc(DV.tensor_copy(
                dstT[:, :].rearrange("p (j tt) -> p j tt", j=8)[:, :, ts(it, 128)],
                pst[:, :].rearrange("p (j tt) -> p j tt", j=8)), "dv")
        return tr_evac[ntile - 1]

    ln_a_done = ln_phase(Xb, T, xnT, "a", stp=lambda: step_pre(3))
    p.wait(SY, "te", p.cnt["te"])
    p.wait(SY, "dv", p.cnt["dv"])
    p.wait(SY, "ac", p.cnt["ac"])
    wp_full = p.sbuf("wp_full", [128, 8 * 2048], BF16, scope=scopeA)
    for kt in range(8):
        p.inc(GP.dma_start(out=wp_full[:, ts(kt, 2048)], in_=Wp_c[kt * 128:(kt + 1) * 128, :]), "wp", 16)
    wp_ld = p.cnt["wp"]
    p.inc(GP.dma_start(out=gw_sb[:, :].rearrange("p (k e) -> p k e", k=8),
                       in_=gwT.rearrange("(k p) e -> p k e", p=128)), "gw", 16)
    ln_h_done = ln_phase(Xh, TH, xnhT, "h", stp=lambda: step_pre(1))

    if "xnT" in dbg:
        p.wait(GP, "dv", ln_a_done)
        p.inc(GP.dma_start(out=dbg["xnT"].rearrange("(j p) tt -> p j tt", p=128),
                           in_=xnT[:, :].rearrange("p (j tt) -> p j tt", j=8)), "qg", 16)
    if "xnhT" in dbg:
        p.wait(GP, "dv", ln_h_done)
        p.inc(GP.dma_start(out=dbg["xnhT"].rearrange("(j p) tt -> p j tt", p=128),
                           in_=xnhT[:, :].rearrange("p (j tt) -> p j tt", j=8)), "qg", 16)

    def gat_gen():
        p.wait(TE, "gw", p.cnt["gw"])
        p.wait(TE, "dv", ln_h_done)
        for tt_ in range(16):
            gating_unit(tt_)
            yield
    _gatit = gat_gen()
    _gat_done = [False]

    def step_gat(n=1):
        if _gat_done[0]:
            return
        for _ in range(n):
            try:
                next(_gatit)
            except StopIteration:
                _gat_done[0] = True
                return


    xp_ps = [p.psum(f"xp_ps{i}", [128, 512], scope=scopeA) for i in range(2)]
    xp_ev = p.sbuf("xp_ev", [128, 2 * 512], BF16, scope=scopeA)
    p.wait(TE, "wp", wp_ld)
    p.wait(TE, "dv", ln_a_done)
    xp_ev_done = {}
    for ci in range(16):
        for tc in range(8):
            idx = ci * 8 + tc
            if idx % 4 == 0:
                step_pre(1)
            if idx % 8 == 4:
                step_gat(1)
            pb = idx % 2
            if idx >= 2:
                p.wait(TE, "ac", xp_ev_done[idx - 2])
                p.wait(ACT, f"xs{pb}", p.cnt[f"xs{pb}"])
            for kt in range(8):
                ins = TE.matmul(xp_ps[pb][:, :],
                                wp_full[:, kt * 2048 + ci * 128: kt * 2048 + ci * 128 + 128],
                                xnT[:, kt * T + tc * 512: kt * T + (tc + 1) * 512],
                                start=(kt == 0), stop=(kt == 7))
            mm = p.inc(ins, "te")
            p.wait(ACT, "te", mm)
            xp_ev_done[idx] = p.inc(
                ACT.activation(xp_ev[:, ts(pb, 512)], xp_ps[pb][:, :],
                               AF.Identity, bias=bp_s[:, ci:ci + 1]), "ac")
            xq = SY if pb == 0 else ACT
            p.wait(xq, "ac", xp_ev_done[idx])
            p.inc(xq.dma_start(out=xp_hbm.ap()[ci * 128:(ci + 1) * 128, ts(tc, 512)],
                               in_=xp_ev[:, ts(pb, 512)]), f"xs{pb}", 16)
    xp_written = (p.cnt["xs0"], p.cnt["xs1"])
    if "xp" in dbg:
        p.wait(GP, "xs0", xp_written[0])
        p.wait(GP, "xs1", xp_written[1])
        p.inc(GP.dma_start(out=dbg["xp"], in_=xp_hbm.ap()), "qg", 16)

    while not (_pre_done[0] and _gat_done[0]):
        step_pre(2)
        step_gat(2)
    comb_done = p.cnt["dv"]

    if BUILD_UNTIL == "p2":
        _finish_stub([scopeA])
        p.ctx.close()
        return nc
    full_barrier()
    scopeA.close()
    if BUILD_UNTIL == "p3proj":
        _finish_stub([scopeB])
        p.ctx.close()
        return nc
    out_fm = p.sbuf("out_fm", [128, 8 * TH], BF16)
    y_sb = p.sbuf("y_sb", [128, NQ * 512], BF16, scope=scopeB)
    fre_one = p.sbuf("fre_sb0", [128, 2048], BF16, scope=scopeB)
    fre_sb = [fre_one, fre_one]
    gt_one = p.sbuf("gt_sb0", [128, 512], BF16, scope=scopeB)
    gt_sb = [gt_one, gt_one]
    att_one = f_ev
    att_sb = [att_one, att_one]
    p.wait(ACT, "xs0", xp_written[0])
    p.wait(ACT, "xs1", xp_written[1])
    y0_q = {}

    def emit_y0_quarter(k):
        for qi in range(8 * k, 8 * k + 8):
            for half in range(2):
                ysrc = bass.AP(xp_hbm, (16 * qi + 8 * half) * T, [[64, 64], [T, 8], [1, 64]])
                p.inc(ACT.dma_start(out=y_sb[64 * half:64 * half + 64, ts(qi, 512)]
                                    .rearrange("p (c n) -> p c n", c=8), in_=ysrc), f"y0q{k}", 16)
        y0_q[k] = p.cnt[f"y0q{k}"]

    emit_y0_quarter(0)

    if "Fsh" in dbg:
        p.wait(GP, "fz0", p.cnt["fz0"])
        p.inc(GP.dma_start(out=dbg["Fsh"], in_=agF_in.ap()), "qg", 16)

    emit_fag(1)
    emit_fag(2)
    F_ready = p.cnt["cc"]
    p.wait(ACT, "cc", F_thr[0])

    if BUILD_UNTIL == "gating":
        _finish_stub([scopeC])
        p.ctx.close()
        return nc

    w1f_sb = p.sbuf("w1f_sb", [128, 8 * 2816], BF16, scope=scopeB)
    w2_sb = p.sbuf("w2_sb", [128, 11 * 1024], BF16, scope=scopeB)
    m_sb = fscr[:, 0:2816]
    sg_sb = fscr[:, 2816:3328]
    crep_sb = fscr[:, 3328:3840]
    comb_ct = fscr[0:1, 4608:4864]
    ps_gu = [p.psum(f"ps_gu{i}", [128, 512], scope=scopeB) for i in range(2)]
    ps_o2 = [p.psum("ps_o2_0", [128, 512], scope=scopeB)]

    def ffn_feeder():
        for s_ in ["te", "dv", "ac"]:
            p.wait(GP, s_)
        def comb_dma(e_, ci_):
            p.claim(GP, "comb_ct")
            for h in range(2):
                cc = 2 * ci_ + h
                p.inc(GP.dma_start(out=comb_ct[0:1, h * 128:(h + 1) * 128],
                                   in_=comb_tok[:, cc * 8 + e_:cc * 8 + e_ + 1]), "cr", 16)
            return p.cnt["cr"]

        for e in range(E):
            p.wait(GP, "dv", comb_done)
            cr_next = comb_dma(e, 0)
            p.claim(GP, "w1f_sb")
            for kt in range(8):
                p.inc(GP.dma_start(out=w1f_sb[:, ts(kt, 2816)],
                                   in_=w1[e, kt * 128:(kt + 1) * 128, :]), "w1s", 16)
            w1r = p.cnt["w1s"]
            p.claim(GP, "w2_sb")
            for ft in range(11):
                p.inc(GP.dma_start(out=w2_sb[:, ts(ft, 1024)],
                                   in_=w2[e, ft * 128:(ft + 1) * 128, :]), "w2s", 16)
            w2r = p.cnt["w2s"]
            yield
            for ci in range(8):
                p.wait(TE, "cr", cr_next)
                p.claim(TE, "ps_gu0")
                crm = p.inc(TE.matmul(ps_gu[0][:, 0:256], ones_s[0:1, :],
                                      comb_ct[0:1, :],
                                      start=True, stop=True), "te")
                p.use("comb_ct", "te", crm)
                if ci < 7:
                    cr_next = comb_dma(e, ci + 1)
                p.wait(ACT, "te", crm)
                p.claim(ACT, f"crep_{ci % 2}")
                crv = p.inc(ACT.activation(crep_sb[:, ts(ci % 2, 256)], ps_gu[0][:, 0:256], AF.Copy), "ac")
                p.use("ps_gu0", "ac", crv)
                yield
                p.wait(TE, "w1s", w1r)
                for j in range(11):
                    jp = j % 2
                    p.claim(TE, f"ps_gu{jp}")
                    pg = ps_gu[jp][:, 0:256]
                    pu = ps_gu[jp][:, 256:512]
                    for kt in range(8):
                        TE.matmul(pg, w1f_sb[:, kt * 2816 + j * 128: kt * 2816 + j * 128 + 128],
                                  xnhT[:, kt * TH + ci * 256: kt * TH + ci * 256 + 256],
                                  start=(kt == 0), stop=(kt == 7))
                    for kt in range(8):
                        ins = TE.matmul(pu,
                                        w1f_sb[:, kt * 2816 + FFN + j * 128: kt * 2816 + FFN + j * 128 + 128],
                                        xnhT[:, kt * TH + ci * 256: kt * TH + ci * 256 + 256],
                                        start=(kt == 0), stop=(kt == 7))
                    gum = p.inc(ins, "te")
                    p.wait(ACT, "te", gum)
                    p.claim(ACT, f"sg_{jp}")
                    sgv = p.inc(ACT.activation(sg_sb[:, ts(jp, 256)], pg, AF.Silu), "ac")
                    p.wait(DV, "ac", sgv)
                    p.wait(DV, "ac", crv)
                    t0c = p.inc(DV.tensor_tensor(sg_sb[:, ts(jp, 256)], sg_sb[:, ts(jp, 256)],
                                                 pu, ALU.mult), "dv")
                    p.use(f"ps_gu{jp}", "dv", t0c)
                    p.wait(GP, "dv", t0c)
                    p.wait(GP, "ac", crv)
                    p.claim(GP, f"m_{j}")
                    mj = p.inc(GP.tensor_mul(m_sb[:, ts(j, 256)], sg_sb[:, ts(jp, 256)],
                                             crep_sb[:, ts(ci % 2, 256)]), "gpm")
                    p.use(f"sg_{jp}", "gpm", mj)
                    p.use(f"crep_{ci % 2}", "gpm", mj)
                    yield
                if ci == 7:
                    p.use("w1f_sb", "te", p.cnt["te"])
                mdone = p.cnt["gpm"]
                p.wait(TE, "gpm", mdone)
                p.wait(TE, "w2s", w2r)
                for dtp in range(4):
                    po = ps_o2[0]
                    p.claim(TE, "ps_o2_0")
                    for half in range(2):
                        dt = 2 * dtp + half
                        sl = po[:, half * 256:half * 256 + 256]
                        for ft in range(11):
                            ins = TE.matmul(sl, w2_sb[:, ft * 1024 + dt * 128: ft * 1024 + dt * 128 + 128],
                                            m_sb[:, ts(ft, 256)],
                                            start=(ft == 0), stop=(ft == 10))
                    w2m = p.inc(ins, "te")
                    p.wait(ACT, "te", w2m)
                    p.claim(ACT, "crep_0")
                    p.claim(ACT, "crep_1")
                    oev = p.inc(ACT.activation(crep_sb[:, :], po[:, :], AF.Copy), "ac")
                    p.use("ps_o2_0", "ac", oev)
                    osl = out_fm[:, :].rearrange("p (dt tt2) -> p dt tt2", dt=8)[
                        :, 2 * dtp:2 * dtp + 2, ci * 256:(ci + 1) * 256]
                    p.wait(DV, "ac", oev)
                    p.claim(DV, f"ofm_{ci}_{dtp}")
                    tv = crep_sb[:, :].rearrange("p (h tt2) -> p h tt2", h=2)
                    if e == 0:
                        fv = p.inc(DV.tensor_copy(osl, tv), "dv")
                    else:
                        fv = p.inc(DV.tensor_tensor(osl, tv, osl, ALU.add), "dv")
                    p.use("crep_0", "dv", fv)
                    p.use("crep_1", "dv", fv)
                    p.use(f"ofm_{ci}_{dtp}", "dv", fv)
                    yield
                for j in range(11):
                    p.use(f"m_{j}", "te", p.cnt["te"])
            p.use("w2_sb", "te", p.cnt["te"])

    _feed = ffn_feeder()
    _feed_done = [False]

    def step_ffn(n=1):
        if _feed_done[0]:
            return
        for _ in range(n):
            try:
                next(_feed)
            except StopIteration:
                _feed_done[0] = True
                return

    step_ffn(1)

    A_thr = [0, 0]

    def emit_aag(part):
        r0, r1 = (0, 384) if part == 0 else (384, 512)
        dst = agA_a if part == 0 else agA_b
        nr = r1 - r0
        p.wait(GP, "as0", p.cnt["as0"])
        p.wait(GP, "as1", p.cnt["as1"])
        if SIM_LOCAL:
            for r_ in range(2):
                p.inc(GP.dma_start(out=dst.ap()[r_ * nr:(r_ + 1) * nr, :],
                                   in_=agA_in.ap()[r0:r1, :]), "cc", 16)
            GP.wait_ge(p.sems["cc"], p.cnt["cc"])
        else:
            p.inc(GP.collective_compute(
                "AllGather", ALU.bypass,
                replica_groups=[[0, 1], [2, 3], [4, 5], [6, 7]],
                ins=[agA_in.ap()[r0:r1, :].opt()], outs=[dst.ap().opt()]), "cc", 1)
        A_thr[part] = p.cnt["cc"]

    gate_done = {}
    for st in range(1, STAGES + 1):
        p.wait(DV, "qg", p.cnt["qg"])
        p.wait(ACT, "cc", F_thr[st - 1])
        for qi in range(NQ):
            par = qi % 2
            if st == 1 and qi in (2, 4, 6):
                emit_y0_quarter(qi // 2)
            p.claim(SY, "gt0")
            for half in range(2):
                row0 = st * 512 + 16 * qi + 8 * half
                gsrc = bass.AP(xp_hbm, row0 * T, [[64, 64], [T, 8], [1, 64]])
                p.inc(SY.dma_start(out=gt_sb[par][64 * half:64 * half + 64, :]
                                   .rearrange("p (c n) -> p c n", c=8), in_=gsrc), f"gl{par}", 16)
            gld = p.cnt[f"gl{par}"]
            p.claim(ACT, "fre0")
            for c2 in range(4):
                for pl in range(2):
                    for half in range(2):
                        row0 = 16 * qi + 4 * c2 + half
                        fsrc2 = bass.AP(agF_s[st - 1], row0 * FROW + pl * 8192,
                                        [[128, 64], [2 * FROW, 2], [1, 128]])
                        ins = ACT.dma_start(
                            out=fre_sb[par][64 * half:64 * half + 64,
                                            c2 * 512 + pl * 256:c2 * 512 + pl * 256 + 256]
                            .rearrange("p (pr k) -> p pr k", pr=2), in_=fsrc2)
                        p.inc(ins, f"fa{par}", 16)
            fldc = p.cnt[f"fa{par}"]
            if st > 1:
                p.wait(TE, "dv", gate_done[(st - 1, qi)])
            else:
                p.wait(TE, f"y0q{qi // 8}", y0_q[qi // 8])

            def mk_fld(par_, cnt_):
                def f(c2):
                    p.wait(DV, f"fa{par_}", cnt_)
                    return cnt_
                return f
            step_ffn(1)
            o = fwd_first_half(0, par, y_sb[0:64, ts(qi, 512)], y_sb[64:128, ts(qi, 512)],
                               mk_fld(par, fldc), (st, qi), stp=step_ffn)
            g = bwd_second_half(0, par, st < STAGES, qi, st, gld, o, stp=step_ffn)
            gate_done[(st, qi)] = g
            if st == STAGES and qi == 23:
                emit_aag(0)
    while not _feed_done[0]:
        step_ffn(8)

    emit_aag(1)
    A_ready = p.cnt["cc"]
    if "attn" in dbg:
        p.wait(GP, "as0", p.cnt["as0"])
        p.wait(GP, "as1", p.cnt["as1"])
        p.inc(GP.dma_start(out=dbg["attn"], in_=agA_in.ap()), "qg", 16)

    if BUILD_UNTIL == "conv":
        _finish_stub([scopeB])
        p.ctx.close()
        return nc

    if BUILD_UNTIL == "aag":
        _finish_stub([scopeB])
        p.ctx.close()
        return nc
    full_barrier()
    scopeB.close()
    scopeC.close()
    attn_fm = p.sbuf("attn_fm", [128, 8 * TH], BF16, scope=scopeD)
    idf_s = p.sbuf("idf_s", [128, 128], F32, scope=scopeD)
    p.inc(SY.dma_start(out=idf_s[:, :], in_=identf), "idf", 16)
    cnd_sb = [p.sbuf(f"cnd{i}", [128, TH], BF16, scope=scopeD) for i in range(2)]
    o1_sb = p.sbuf("o1_sb", [128, 1024], F32, scope=scopeD)
    xh_sb = [p.sbuf(f"xh{i}", [128, 1024], F32, scope=scopeD) for i in range(6)]
    outb = p.sbuf("outb", [128, 2 * 1024], F32, scope=scopeD)
    ps_pt = [p.psum(f"ps_pt{i}", [128, 512], scope=scopeD) for i in range(2)]
    p.wait(SY, "dv", p.cnt["dv"])
    p.wait(SY, "te", p.cnt["te"])
    acc_prev = {}
    for dt in range(8):
        hsrc = 0 if dt < 4 else 1
        for colh in range(2):
            par = (dt * 2 + colh) % 2
            ldq = SY if par == 0 else ACT
            p.wait(ldq, "cc", A_thr[1] if dt % 4 == 3 else A_thr[0])
            p.claim(ldq, f"cnd{par}")
            if dt % 4 == 3:
                asrc = agA_b.ap()[hsrc * 128:hsrc * 128 + 128, colh * TH:(colh + 1) * TH]
            else:
                asrc = agA_a.ap()[hsrc * 384 + (dt % 4) * 128:
                                  hsrc * 384 + (dt % 4) * 128 + 128,
                                  colh * TH:(colh + 1) * TH]
            p.inc(ldq.dma_start(out=cnd_sb[par][:, :], in_=asrc), f"cn{par}", 16)
            p.wait(DV, f"cn{par}")
            dstsl = attn_fm[:, dt * TH:(dt + 1) * TH]
            if colh == 0:
                a = p.inc(DV.tensor_scalar(dstsl, cnd_sb[par][:, :], cm_s[:, colh:colh + 1],
                                           None, ALU.mult), "dv")
            else:
                p.wait(DV, "dv", acc_prev[dt])
                a = p.inc(DV.scalar_tensor_tensor(dstsl, cnd_sb[par][:, :], cm_s[:, colh:colh + 1],
                                                  dstsl, ALU.mult, ALU.add), "dv")
            acc_prev[dt] = a
            p.use(f"cnd{par}", "dv", a)
    attnfm_done = p.cnt["dv"]
    if "attnfm" in dbg:
        p.wait(GP, "dv", attnfm_done)
        p.inc(GP.dma_start(out=dbg["attnfm"].rearrange("(j p) tt -> p j tt", p=128),
                           in_=attn_fm[:, :].rearrange("p (j tt) -> p j tt", j=8)), "qg", 16)

    xh_q = [SY, ACT, GP]

    def xh_load(tt_):
        s = tt_ % 6
        q = xh_q[s % 3]
        p.claim(q, f"xh{s}")
        p.inc(q.dma_start(out=xh_sb[s][:, :], in_=Xh[tt_ * 128:(tt_ + 1) * 128, :]), f"xh{s}", 16)

    for tt in range(6):
        xh_load(tt)
    out_done = {}
    for tt in range(16):
        s = tt % 6
        ob_p = tt % 2
        p.wait(DV, "dv", attnfm_done)
        p.claim(DV, "o1_sb")
        o1 = p.inc(DV.tensor_tensor(
            o1_sb[:, :].rearrange("p (dt t2) -> p dt t2", dt=8),
            out_fm[:, :].rearrange("p (dt tt2) -> p dt tt2", dt=8)[:, :, ts(tt, 128)],
            attn_fm[:, :].rearrange("p (dt tt2) -> p dt tt2", dt=8)[:, :, ts(tt, 128)],
            ALU.add), "dv")
        p.wait(TE, "dv", o1)
        p.wait(TE, "idf")
        p.claim(TE, "ps_pt")
        for dt in range(8):
            ins = TE.transpose(ps_pt[dt // 4][:, (dt % 4) * 128:(dt % 4) * 128 + 128],
                               o1_sb[:, ts(dt, 128)], idf_s[:, :])
        tpm = p.inc(ins, "te")
        p.use("o1_sb", "te", tpm)
        p.wait(DV, "te", tpm)
        p.wait(DV, f"xh{s}")
        p.claim(DV, f"outb{ob_p}")
        DV.tensor_tensor(outb[:, ob_p * 1024:ob_p * 1024 + 512], ps_pt[0][:, :],
                         xh_sb[s][:, 0:512], ALU.add)
        ob = p.inc(DV.tensor_tensor(outb[:, ob_p * 1024 + 512:ob_p * 1024 + 1024], ps_pt[1][:, :],
                                    xh_sb[s][:, 512:1024], ALU.add), "dv")
        p.use("ps_pt", "dv", ob)
        p.use(f"xh{s}", "dv", ob)
        oq, osem = (SY, "ostr") if tt % 2 == 0 else (ACT, "osta")
        p.wait(oq, "dv", ob)
        od = p.inc(oq.dma_start(out=out[tt * 128:(tt + 1) * 128, :],
                                in_=outb[:, ob_p * 1024:(ob_p + 1) * 1024]), osem, 16)
        p.use(f"outb{ob_p}", osem, od)
        out_done[osem] = od
        if tt + 6 < 16:
            xh_load(tt + 6)
    p.wait(GP, "qg", p.cnt["qg"])
    p.wait(SY, "ostr", out_done["ostr"])
    p.wait(SY, "osta", out_done["osta"])

    scopeD.close()
    p.ctx.close()
    return nc


def kernel(**inputs):
    maps = _shard_inputs(inputs)
    nc = build_program(debug=DEBUG_OUTS)
    res = run_bass_kernel_spmd(nc, maps, core_ids=list(range(8)))
    full = np.zeros((B, T, D), np.float32)
    for c in range(8):
        b, h = c >> 1, c & 1
        full[b, h * TH:(h + 1) * TH, :] = res.results[c]["out"]
    if DEBUG_OUTS:
        kernel.debug = [dict(res.results[c]) for c in range(8)]
    return full


# version 20 (final)
# speedup vs baseline: 1.0607x; 1.0272x over previous
import numpy as np
import ml_dtypes
from contextlib import ExitStack

import concourse.bass as bass
import concourse.mybir as mybir
from concourse.bass_utils import run_bass_kernel_spmd

F32 = mybir.dt.float32
BF16 = mybir.dt.bfloat16
AF = mybir.ActivationFunctionType
ALU = mybir.AluOpType
AX = mybir.AxisListType
bf16 = ml_dtypes.bfloat16
ts = bass.ts

B, T, D = 4, 4096, 1024
L = 2 * T
N1, N2 = 128, 64
STAGES = 3
E, FFN = 8, 1408
DH, TH = D // 2, T // 2
NQ = DH // 16
FSH = STAGES * DH // 4
FROW = 16384

DEBUG_OUTS = ()
BUILD_UNTIL = None
SIM_LOCAL = False


def _tables():
    n1 = np.arange(N1)[:, None]
    k1 = np.arange(N1)[None, :]
    n2 = np.arange(N2)[:, None]
    k2 = np.arange(N2)[None, :]
    W128 = np.exp(-2j * np.pi * n1 * k1 / N1)
    W64 = np.exp(-2j * np.pi * n2 * k2 / N2)
    W64i = np.exp(2j * np.pi * k2.T * n2.T / N2)
    W128i = np.exp(2j * np.pi * k1.T * n1.T / N1)
    tw = np.exp(-2j * np.pi * np.arange(N1)[:, None] * np.arange(N2)[None, :] / L)

    def b_(x):
        return np.ascontiguousarray(x).astype(np.float32).astype(bf16)

    t = {}
    w128c = np.concatenate([W128.real[:64, :], W128.imag[:64, :]], axis=1)
    t["W128c"] = b_(np.concatenate([w128c, w128c], axis=0))
    for nm, m in [("W64re", W64.real), ("W64im", W64.imag), ("W64imn", -W64.imag),
                  ("W64ire", W64i.real), ("W64iim", W64i.imag), ("W64iimn", -W64i.imag)]:
        t[nm] = b_(np.concatenate([m, m], axis=0))
    t["W128ire"] = b_(W128i.real[:, :64] / L)
    t["W128iimn"] = b_(-W128i.imag[:, :64] / L)
    t["twre"] = b_(np.tile(tw.real[:, None, :], (1, 8, 1)).reshape(128, 512))
    t["twim"] = b_(np.tile(tw.imag[:, None, :], (1, 8, 1)).reshape(128, 512))
    ctm = np.conj(tw).T
    ctp = np.concatenate([ctm, ctm], axis=0)
    t["ctre"] = b_(np.tile(ctp.real[:, None, :], (1, 4, 1)).reshape(128, 512))
    t["ctim"] = b_(np.tile(ctp.imag[:, None, :], (1, 4, 1)).reshape(128, 512))
    ident = np.eye(128, dtype=np.float32)
    t["identb"] = b_(ident)
    t["identf"] = ident
    t["ones1"] = b_(np.ones((1, 128), np.float32))
    pos_t = np.arange(T, dtype=np.float32)[:, None]
    i = np.arange(D)[None, :]
    angle = pos_t / np.power(np.float32(10000.0), (2 * (i // 2)).astype(np.float32) / D)
    pos = np.where(i % 2 == 0, np.sin(angle), np.cos(angle)).astype(np.float32)
    t["posT"] = b_(pos.T)
    return t


def _shard_inputs(ins):
    X = np.asarray(ins["L0_attn_X"])
    Wp = np.asarray(ins["L0_attn_Wp"])
    bp = np.asarray(ins["L0_attn_bp"])
    Wf = np.asarray(ins["L0_attn_Wf"])
    bff = np.asarray(ins["L0_attn_bf"])
    gw = np.asarray(ins["L0_ffn_gate_w"])
    w1 = np.asarray(ins["L0_ffn_w1"])
    w2 = np.asarray(ins["L0_ffn_w2"])
    tabs = _tables()
    maps = []
    for c in range(8):
        b, h = c >> 1, c & 1
        dcols = np.arange(h * DH, h * DH + DH)
        pcols = np.concatenate([s * D + dcols for s in range(STAGES + 1)])
        r = c >> 1
        s_ = np.repeat(np.arange(STAGES), 128)
        j_ = np.tile(np.arange(128), STAGES)
        fcols = s_ * D + h * DH + r * 128 + j_
        cmask = np.zeros((128, 16), np.float32)
        cmask[:, h] = 1.0
        m = {
            "Xb": np.ascontiguousarray(X[b]),
            "Xh": np.ascontiguousarray(X[b, h * TH:(h + 1) * TH]),
            "Wp_c": np.ascontiguousarray(Wp[:, pcols]),
            "bp_c": np.ascontiguousarray(bp[pcols]).reshape(16, 128).T.copy(),
            "Wf_c": np.ascontiguousarray(Wf[:, fcols]),
            "bf_c": np.ascontiguousarray(bff[fcols]).reshape(FSH // 128, 128).T.copy(),
            "gwT": np.ascontiguousarray(gw.T),
            "w1": w1,
            "w2": w2,
            "cmask": cmask,
        }
        m.update(tabs)
        maps.append(m)
    return maps


class P:
    def __init__(self):
        self.nc = bass.Bass("TRN2", target_bir_lowering=False)
        self.ctx = ExitStack()
        self.cnt = {}
        self.sems = {}
        self.lastw = {}
        self.res = {}

    def sem(self, name):
        self.sems[name] = self.ctx.enter_context(self.nc.semaphore(name))
        self.cnt[name] = 0

    def qsem(self, name):
        if name not in self.sems:
            self.sem(name)
        return name

    def inc(self, instr, name, n=1):
        self.qsem(name)
        instr.then_inc(self.sems[name], n)
        self.cnt[name] += n
        return self.cnt[name]

    def wait(self, eng, name, val=None):
        self.qsem(name)
        if val is None:
            val = self.cnt[name]
        if val <= 0:
            return
        key = (id(eng), name)
        if self.lastw.get(key, -1) >= val:
            return
        eng.wait_ge(self.sems[name], val)
        self.lastw[key] = val

    def use(self, resname, semname, count):
        d = self.res.setdefault(resname, {})
        d[semname] = max(d.get(semname, 0), count)

    def claim(self, eng, resname):
        for s, c in self.res.get(resname, {}).items():
            self.wait(eng, s, c)

    def sbuf(self, name, shape, dtype, scope=None):
        return (scope or self.ctx).enter_context(self.nc.sbuf_tensor(name, shape, dtype))

    def psum(self, name, shape, dtype=F32, scope=None):
        return (scope or self.ctx).enter_context(self.nc.psum_tensor(name, shape, dtype))


def build_program(debug=()):
    p = P()
    nc = p.nc

    def din(name, shape, dtype=F32):
        return nc.dram_tensor(name, shape, dtype, kind="ExternalInput").ap()

    Xb = din("Xb", [T, D]); Xh = din("Xh", [TH, D])
    Wp_c = din("Wp_c", [D, 4 * DH]); bp_c = din("bp_c", [128, 16])
    Wf_c = din("Wf_c", [D, FSH]); bf_c = din("bf_c", [128, FSH // 128])
    gwT = din("gwT", [D, E])
    w1 = din("w1", [E, D, 2 * FFN]); w2 = din("w2", [E, FFN, D])
    cmask = din("cmask", [128, 16])
    CT = {nm: din(nm, shp, BF16) for nm, shp in [
        ("W128c", [128, 256]), ("W64re", [128, 64]), ("W64im", [128, 64]),
        ("W64imn", [128, 64]), ("W64ire", [128, 64]), ("W64iim", [128, 64]),
        ("W64iimn", [128, 64]), ("W128ire", [128, 64]), ("W128iimn", [128, 64]),
        ("twre", [128, 512]), ("twim", [128, 512]), ("ctre", [128, 512]),
        ("ctim", [128, 512]), ("identb", [128, 128]), ("ones1", [1, 128]),
        ("posT", [D, T])]}
    identf = din("identf", [128, 128], F32)

    out = nc.dram_tensor("out", [TH, D], F32, kind="ExternalOutput").ap()
    dbg = {}
    for name in debug:
        shp = {"xnT": [D, T], "xnhT": [D, TH], "xp": [4 * DH, T],
               "fhbm": [FSH, T], "Fsh": [FSH, FROW], "Ffull": [4 * FSH, FROW],
               "y1": [128, NQ * 512], "y2": [128, NQ * 512], "attn": [DH, T],
               "attnfm": [D, TH], "comb": [128, 128], "ffn": [D, TH]}[name]
        dbg[name] = nc.dram_tensor("dbg_" + name, shp, F32, kind="ExternalOutput").ap()

    xp_hbm = nc.dram_tensor("xp_hbm", [4 * DH, T], BF16)
    f_hbm = nc.dram_tensor("f_hbm", [FSH, T], BF16)
    agF_in = nc.dram_tensor("agF_in", [FSH, FROW], BF16)
    agF_s = [nc.dram_tensor(f"agF_o{s}", [512, FROW], BF16) for s in range(STAGES)]
    agA_in = nc.dram_tensor("agA_in", [DH, T], BF16)
    agA_a = nc.dram_tensor("agA_a", [2 * 384, T], BF16)
    agA_b = nc.dram_tensor("agA_b", [2 * 128, T], BF16)

    for s in ["qs", "qa", "qg", "te", "dv", "ac", "gp", "cc"]:
        p.sem(s)
    TE, DV, ACT, GP, SY = nc.tensor, nc.vector, nc.scalar, nc.gpsimd, nc.sync

    def full_barrier():
        for e in [TE, DV, ACT, GP, SY]:
            for s in list(p.cnt.keys()):
                if s != "cc":
                    p.wait(e, s)

    xnhT = p.sbuf("xnhT", [128, 8 * TH], BF16)
    cons = p.sbuf("cons", [128, 2944], BF16)
    w128c_s = cons[:, 0:256]
    w64_s = {nm: cons[:, 256 + 64 * i:256 + 64 * i + 64]
             for i, nm in enumerate(["W64re", "W64im", "W64imn", "W64ire", "W64iim", "W64iimn"])}
    w128i_s = cons[:, 640:768]
    twre_s = cons[:, 768:1280]; twim_s = cons[:, 1280:1792]
    ctre_s = cons[:, 1792:2304]; ctim_s = cons[:, 2304:2816]
    idb_s = cons[:, 2816:2944]
    ones_s = p.sbuf("ones_s", [1, 128], BF16)
    bp_s = p.sbuf("bp_s", [128, 16], F32)
    bf_s = p.sbuf("bf_s", [128, FSH // 128], F32)
    cm_s = p.sbuf("cm_s", [128, 16], F32)
    eps_s = p.sbuf("eps_s", [128, 1], F32)

    scopeA = ExitStack()
    scopeB = ExitStack()
    scopeC = ExitStack()
    scopeD = ExitStack()

    blk = p.ctx.enter_context(nc.Block())


    def _finish_stub(open_scopes):
        for e in [TE, DV, ACT, GP, SY]:
            for s in list(p.cnt.keys()):
                p.wait(e, s)
        GP.memset(out_fm[:, 0:1024], 0.0)
        p.inc(GP.engine_nop(), "gp")
        p.wait(GP, "gp", p.cnt["gp"])
        for it in range(TH // 128):
            p.inc(GP.dma_start(out=out[it * 128:(it + 1) * 128, :], in_=out_fm[:, 0:1024]), "qg", 16)
        p.wait(GP, "qg", p.cnt["qg"])
        p.wait(SY, "qg", p.cnt["qg"])
        for sc in open_scopes:
            sc.close()
    p.inc(DV.memset(eps_s[:, :], 1e-5), "dv")
    p.inc(SY.dma_start(out=w128c_s, in_=CT["W128c"]), "qs", 16)
    for nm in ["W64re", "W64im", "W64imn", "W64ire", "W64iim", "W64iimn"]:
        p.inc(SY.dma_start(out=w64_s[nm], in_=CT[nm]), "qs", 16)
    p.inc(SY.dma_start(out=w128i_s[:, 0:64], in_=CT["W128ire"]), "qs", 16)
    p.inc(SY.dma_start(out=w128i_s[:, 64:128], in_=CT["W128iimn"]), "qs", 16)
    for dst, nm in [(twre_s, "twre"), (twim_s, "twim"), (ctre_s, "ctre"),
                    (ctim_s, "ctim"), (idb_s, "identb")]:
        p.inc(SY.dma_start(out=dst, in_=CT[nm]), "qs", 16)
    p.inc(SY.dma_start(out=ones_s[:, :], in_=CT["ones1"]), "qs", 16)
    p.inc(SY.dma_start(out=bp_s[:, :], in_=bp_c), "qs", 16)
    p.inc(SY.dma_start(out=bf_s[:, :], in_=bf_c), "qs", 16)
    p.inc(SY.dma_start(out=cm_s[:, :], in_=cmask), "qs", 16)
    consts_done = p.cnt["qs"]

    s1b = [p.sbuf("s1b0", [128, 2048], BF16)]
    xtw = s1b
    twt = [p.sbuf("twt0", [128, 2048], BF16)]
    s2in = [p.sbuf("s2in0", [128, 2048], BF16)]
    wsb = [p.sbuf("wsb0", [128, 2048], BF16)]
    i1b = [p.sbuf("i1b0", [128, 2048], BF16)]
    i1t = i1b
    s3in = [p.sbuf("s3in0", [128, 2048], BF16)]
    fscr = p.sbuf("fscr", [128, 5120], BF16)
    wf_sb = fscr[:, 0:3072]
    pos_sb = fscr[:, 3072:4096]
    f_ev = fscr[:, 4096:4608]
    fin_sb = fscr[:, 4608:5120]
    gw_sb = p.sbuf("gw_sb", [128, 64], BF16)
    comb_tok = p.sbuf("comb_tok", [128, 128], BF16)
    gst = p.sbuf("gst", [128, 8], F32)
    esb = p.sbuf("esb", [128, 8], F32)
    msk = p.sbuf("msk", [128, 16], F32)
    combf = p.sbuf("combf", [128, 8], F32)
    ps_A = [[p.psum(f"cpA_{i}", [128, 512]) for i in range(2)]]
    ps_T = [p.psum("cpT", [128, 1024], BF16)]
    ps_Z = [p.psum("cpZ", [128, 512])]
    ps_Zb = p.psum("cpZ1", [128, 512])
    ps_s1 = ps_A[0]
    f_psn3 = [ps_A[0][0], ps_A[0][1], ps_Z[0]]
    sc_ps = ps_Z[0][:, 0:8]
    NCI = FSH // 128
    f_written = [0, 0]
    F_thr = [0, 0, 0]

    def emit_fag(s_):
        p.wait(GP, "fz0", p.cnt["fz0"])
        p.wait(GP, "fzg", p.cnt.get("fzg", 0))
        if SIM_LOCAL:
            for r_ in range(4):
                p.inc(GP.dma_start(out=agF_s[s_].ap()[r_ * 128:(r_ + 1) * 128, :],
                                   in_=agF_in.ap()[s_ * 128:(s_ + 1) * 128, :]), "cc", 16)
            GP.wait_ge(p.sems["cc"], p.cnt["cc"])
        else:
            p.inc(GP.collective_compute(
                "AllGather", ALU.bypass, replica_groups=[[0, 2, 4, 6], [1, 3, 5, 7]],
                ins=[agF_in.ap()[s_ * 128:(s_ + 1) * 128, :].opt()],
                outs=[agF_s[s_].ap().opt()]), "cc", 1)
        F_thr[s_] = p.cnt["cc"]

    def fwd_first_half(par, fpar, rhs_lo, rhs_hi, fld_fn, st_qi, stp=None, wbuf=None):
        o = {}
        W, wres = (wsb[par], f"wsb{par}") if wbuf is None else wbuf
        A0, A1 = ps_A[par]
        p.wait(TE, "qs", consts_done)
        p.claim(TE, f"psA{par}")
        TE.matmul(A0[:, :], w128c_s[0:64, 0:128], rhs_lo, start=True, stop=True)
        m1 = p.inc(TE.matmul(A1[:, :], w128c_s[0:64, 128:256], rhs_lo, start=True, stop=True), "te")
        p.wait(ACT, "te", m1)
        p.claim(ACT, f"s1b{par}")
        ACT.activation(s1b[par][:, 0:512], A0[:, :], AF.Copy)
        e1 = p.inc(ACT.activation(s1b[par][:, 512:1024], A1[:, :], AF.Copy), "ac")
        p.wait(TE, "ac", e1)
        TE.matmul(A0[:, :], w128c_s[64:128, 0:128], rhs_hi, start=True, stop=True,
                  tile_position=(64, 0))
        m2 = p.inc(TE.matmul(A1[:, :], w128c_s[64:128, 128:256], rhs_hi, start=True, stop=True,
                             tile_position=(64, 0)), "te")
        p.wait(ACT, "te", m2)
        ACT.activation(s1b[par][:, 1024:1536], A0[:, :], AF.Copy)
        e2 = p.inc(ACT.activation(s1b[par][:, 1536:2048], A1[:, :], AF.Copy), "ac")
        o["s1ev"] = e2
        p.use(f"psA{par}", "ac", e2)
        p.claim(DV, f"xtw{par}")
        p.claim(DV, f"twt{par}")
        tq = twt[par]
        tws = []
        for g in range(2):
            p.wait(DV, "ac", e1 if g == 0 else e2)
            re = s1b[par][:, g * 1024:g * 1024 + 512]
            im = s1b[par][:, g * 1024 + 512:g * 1024 + 1024]
            DV.tensor_tensor(tq[:, 0:512], re, twre_s, ALU.mult)
            DV.tensor_tensor(tq[:, 512:1024], im, twim_s, ALU.mult)
            DV.tensor_tensor(tq[:, 1024:1536], re, twim_s, ALU.mult)
            p.inc(DV.tensor_tensor(tq[:, 1536:2048], im, twre_s, ALU.mult), "dv")
            p.wait(DV, "dv", p.cnt["dv"])
            DV.tensor_tensor(xtw[par][:, g * 1024:g * 1024 + 512], tq[:, 0:512],
                             tq[:, 512:1024], ALU.subtract)
            tw = p.inc(DV.tensor_tensor(xtw[par][:, g * 1024 + 512:g * 1024 + 1024],
                                        tq[:, 1024:1536], tq[:, 1536:2048], ALU.add), "dv")
            p.wait(DV, "dv", tw)
            tws.append(tw)
        o["tw"] = tw
        p.use(f"s1b{par}", "dv", tw)
        p.use(f"twt{par}", "dv", tw)
        if stp:
            stp(2)
        tevs = []
        for half in range(2):
            p.wait(TE, "dv", tws[half])
            if half == 0:
                p.claim(TE, f"psT{par}")
            else:
                p.claim(TE, f"psA{par}")
            for pr in range(4):
                for pl in range(2):
                    t8 = pr * 2 + pl
                    sl = xtw[par][:, half * 1024 + pl * 512 + pr * 128:
                                  half * 1024 + pl * 512 + pr * 128 + 128]
                    if half == 0:
                        ins = TE.transpose(ps_T[par][:, ts(t8, 128)], sl, idb_s)
                    else:
                        ins = TE.matmul((A0 if t8 < 4 else A1)[:, ts(t8 % 4, 128)],
                                        sl, idb_s, start=True, stop=True)
            tra = p.inc(ins, "te")
            p.wait(ACT, "te", tra)
            if half == 0:
                p.claim(ACT, f"s2in{par}")
                tev = p.inc(ACT.activation(s2in[par][:, 0:1024],
                                           ps_T[par][:, :], AF.Copy), "ac")
                p.use(f"psT{par}", "ac", tev)
            else:
                ACT.activation(s2in[par][:, 1024:1536], A0[:, :], AF.Copy)
                tev = p.inc(ACT.activation(s2in[par][:, 1536:2048], A1[:, :], AF.Copy), "ac")
                p.use(f"psA{par}", "ac", tev)
            tevs.append(tev)
        o["s2ev"] = ("ac", tev)
        p.use(f"xtw{par}", "te", p.cnt["te"])
        lay = s2in[par][:, :].rearrange("p (c pr pl k) -> p c pr pl k", c=2, pr=4, pl=2)
        fmc = []
        for c2 in range(4):
            Z = ps_Z[par] if c2 % 2 == 0 else ps_Zb
            g, prh = c2 // 2, (c2 % 2) * 2
            rc = lay[:, g, prh:prh + 2, 0, :]
            ic = lay[:, g, prh:prh + 2, 1, :]
            if stp:
                stp(1)
            p.wait(TE, "ac", tevs[g])
            if c2 == 0:
                p.claim(TE, f"psZ{par}")
            elif c2 == 1:
                p.claim(TE, "psZb")
            else:
                p.wait(TE, "dv", fmc[c2 - 2])
            TE.matmul(Z[0:64, 0:256], w64_s["W64re"][0:64, :], rc[0:64], start=True, stop=False)
            TE.matmul(Z[0:64, 0:256], w64_s["W64imn"][0:64, :], ic[0:64], start=False, stop=True)
            TE.matmul(Z[64:128, 0:256], w64_s["W64re"][64:128, :], rc[64:128], start=True, stop=False,
                      tile_position=(64, 64))
            TE.matmul(Z[64:128, 0:256], w64_s["W64imn"][64:128, :], ic[64:128], start=False, stop=True,
                      tile_position=(64, 64))
            TE.matmul(Z[0:64, 256:512], w64_s["W64im"][0:64, :], rc[0:64], start=True, stop=False)
            TE.matmul(Z[0:64, 256:512], w64_s["W64re"][0:64, :], ic[0:64], start=False, stop=True)
            TE.matmul(Z[64:128, 256:512], w64_s["W64im"][64:128, :], rc[64:128], start=True, stop=False,
                      tile_position=(64, 64))
            zmm = p.inc(TE.matmul(Z[64:128, 256:512], w64_s["W64re"][64:128, :], ic[64:128],
                                  start=False, stop=True, tile_position=(64, 64)), "te")
            p.wait(DV, "te", zmm)
            fw = fld_fn(c2) if fld_fn is not None else None
            wof = c2 * 512
            tq = twt[par]
            p.claim(DV, f"twt{par}")
            if c2 == 0:
                p.claim(DV, wres)
            if fw is not None:
                fr = fre_sb[par][:, 0:256].rearrange("p (w k) -> p w k", w=2) if False else None
                fre = fre_sb[fpar][:, c2 * 512:c2 * 512 + 256]
                fim = fre_sb[fpar][:, c2 * 512 + 256:c2 * 512 + 512]
                DV.tensor_tensor(tq[:, 0:256], Z[:, 0:256], fre, ALU.mult)
                DV.tensor_tensor(tq[:, 256:512], Z[:, 256:512], fim, ALU.mult)
                DV.tensor_tensor(tq[:, 512:768], Z[:, 0:256], fim, ALU.mult)
                p.inc(DV.tensor_tensor(tq[:, 768:1024], Z[:, 256:512], fre, ALU.mult), "dv")
                p.wait(DV, "dv", p.cnt["dv"])
                DV.tensor_tensor(W[:, wof:wof + 256], tq[:, 0:256], tq[:, 256:512], ALU.subtract)
                fm = p.inc(DV.tensor_tensor(W[:, wof + 256:wof + 512], tq[:, 512:768],
                                            tq[:, 768:1024], ALU.add), "dv")
                p.wait(DV, "dv", fm)
                p.use(f"twt{par}", "dv", fm)
                p.use("fre0", "dv", fm) if c2 == 3 else None
            else:
                DV.tensor_copy(W[:, wof:wof + 256], Z[:, 0:256])
                fm = p.inc(DV.tensor_copy(W[:, wof + 256:wof + 512], Z[:, 256:512]), "dv")
            fmc.append(fm)
        o["fm01"] = fmc[1]
        o["fm"] = fmc[3]
        p.use(f"psZ{par}", "dv", fmc[2])
        p.use("psZb", "dv", fmc[3])
        p.use(f"s2in{par}", "te", p.cnt["te"])
        return o

    def bwd_second_half(par, gpar, gate_to_y, qi, st, gld, o, stp=None):
        A0, A1 = ps_A[par]
        i1es = []
        for cp in range(2):
            if stp:
                stp(1)
            p.wait(TE, "dv", o["fm01"] if cp == 0 else o["fm"])
            p.claim(TE, f"psA{par}")
            for c2 in (2 * cp, 2 * cp + 1):
                wof = c2 * 512
                wre = wsb[par][:, wof:wof + 256].rearrange("p (pr k) -> p pr k", pr=2)
                wim = wsb[par][:, wof + 256:wof + 512].rearrange("p (pr k) -> p pr k", pr=2)
                co = (c2 % 2) * 256
                TE.matmul(A0[0:64, co:co + 256], w64_s["W64ire"][0:64, :], wre[0:64], start=True, stop=False)
                TE.matmul(A0[0:64, co:co + 256], w64_s["W64iimn"][0:64, :], wim[0:64], start=False, stop=True)
                TE.matmul(A0[64:128, co:co + 256], w64_s["W64ire"][64:128, :], wre[64:128], start=True,
                          stop=False, tile_position=(64, 64))
                TE.matmul(A0[64:128, co:co + 256], w64_s["W64iimn"][64:128, :], wim[64:128], start=False,
                          stop=True, tile_position=(64, 64))
                TE.matmul(A1[0:64, co:co + 256], w64_s["W64iim"][0:64, :], wre[0:64], start=True, stop=False)
                TE.matmul(A1[0:64, co:co + 256], w64_s["W64ire"][0:64, :], wim[0:64], start=False, stop=True)
                TE.matmul(A1[64:128, co:co + 256], w64_s["W64iim"][64:128, :], wre[64:128], start=True,
                          stop=False, tile_position=(64, 64))
                i1m = p.inc(TE.matmul(A1[64:128, co:co + 256], w64_s["W64ire"][64:128, :], wim[64:128],
                                      start=False, stop=True, tile_position=(64, 64)), "te")
            p.wait(ACT, "te", i1m)
            if cp == 0:
                p.claim(ACT, f"i1b{par}")
            ACT.activation(i1b[par][:, cp * 1024:cp * 1024 + 512], A0[:, :], AF.Copy)
            i1e = p.inc(ACT.activation(i1b[par][:, cp * 1024 + 512:cp * 1024 + 1024], A1[:, :], AF.Copy), "ac")
            p.use(f"psA{par}", "ac", i1e)
            i1es.append(i1e)
        p.claim(DV, f"i1t{par}")
        tq = twt[par]
        ctws = []
        for cp in range(2):
            p.wait(DV, "ac", i1es[cp])
            re = i1b[par][:, cp * 1024:cp * 1024 + 512]
            im = i1b[par][:, cp * 1024 + 512:cp * 1024 + 1024]
            ctr = ctre_s[:, 0:512]
            cti = ctim_s[:, 0:512]
            p.claim(DV, f"twt{par}")
            DV.tensor_tensor(tq[:, 0:512], re, ctr, ALU.mult)
            DV.tensor_tensor(tq[:, 512:1024], im, cti, ALU.mult)
            DV.tensor_tensor(tq[:, 1024:1536], re, cti, ALU.mult)
            p.inc(DV.tensor_tensor(tq[:, 1536:2048], im, ctr, ALU.mult), "dv")
            p.wait(DV, "dv", p.cnt["dv"])
            DV.tensor_tensor(i1t[par][:, cp * 1024:cp * 1024 + 512], tq[:, 0:512],
                             tq[:, 512:1024], ALU.subtract)
            ctw = p.inc(DV.tensor_tensor(i1t[par][:, cp * 1024 + 512:cp * 1024 + 1024],
                                         tq[:, 1024:1536], tq[:, 1536:2048], ALU.add), "dv")
            p.wait(DV, "dv", ctw)
            p.use(f"twt{par}", "dv", ctw)
            ctws.append(ctw)
        p.use(f"i1b{par}", "dv", ctw)
        if stp:
            stp(2)
        for half in range(2):
            p.wait(TE, "dv", ctws[half])
            if half == 0:
                p.claim(TE, f"psT{par}")
            else:
                p.claim(TE, f"psA{par}")
            for i in range(4):
                c2 = half * 2 + i // 2
                pr = i % 2
                cp, co = c2 // 2, (c2 % 2) * 256
                for pl in range(2):
                    t8 = i * 2 + pl
                    sl = i1t[par][:, cp * 1024 + pl * 512 + co + pr * 128:
                                  cp * 1024 + pl * 512 + co + pr * 128 + 128]
                    if half == 0:
                        ins = TE.transpose(ps_T[par][:, ts(t8, 128)], sl, idb_s)
                    else:
                        ins = TE.matmul((A0 if t8 < 4 else A1)[:, ts(t8 % 4, 128)],
                                        sl, idb_s, start=True, stop=True)
            trb = p.inc(ins, "te")
            p.wait(ACT, "te", trb)
            if half == 0:
                p.claim(ACT, f"s3in{par}")
                s3e = p.inc(ACT.activation(s3in[par][:, 0:1024],
                                           ps_T[par][:, :], AF.Copy), "ac")
                p.use(f"psT{par}", "ac", s3e)
            else:
                ACT.activation(s3in[par][:, 1024:1536], A0[:, :], AF.Copy)
                s3e = p.inc(ACT.activation(s3in[par][:, 1536:2048], A1[:, :], AF.Copy), "ac")
                p.use(f"psA{par}", "ac", s3e)
        p.use(f"i1t{par}", "te", p.cnt["te"])
        p.wait(TE, "ac", s3e)
        p.claim(TE, f"psZ{par}")
        Z = ps_Z[par]
        lay3 = s3in[par][:, :].rearrange("p (h i pl n) -> p h i pl n", h=2, i=4, pl=2)
        TE.matmul(Z[0:64, :], w128i_s[:, 0:64], lay3[:, 0, :, 0, :], start=True, stop=False)
        TE.matmul(Z[0:64, :], w128i_s[:, 64:128], lay3[:, 0, :, 1, :], start=False, stop=True)
        TE.matmul(Z[64:128, :], w128i_s[:, 0:64], lay3[:, 1, :, 0, :], start=True, stop=False,
                  tile_position=(0, 64))
        invsb = p.inc(TE.matmul(Z[64:128, :], w128i_s[:, 64:128], lay3[:, 1, :, 1, :],
                                start=False, stop=True, tile_position=(0, 64)), "te")
        p.use(f"s3in{par}", "te", invsb)
        p.wait(DV, "te", invsb)
        p.wait(DV, f"gl{gpar}")
        if gate_to_y:
            g = p.inc(DV.tensor_tensor(y_sb[:, ts(qi, 512)], Z[:, :], gt_sb[gpar][:, :], ALU.mult), "dv")
        else:
            p.claim(DV, "att0")
            g = p.inc(DV.tensor_tensor(att_sb[gpar][:, :], Z[:, :], gt_sb[gpar][:, :], ALU.mult), "dv")
            p.wait(SY, "dv", g)
            for half in range(2):
                dst = bass.AP(agA_in, (16 * qi + 8 * half) * T, [[64, 64], [T, 8], [1, 64]])
                p.inc(SY.dma_start(out=dst,
                                   in_=att_sb[gpar][64 * half:64 * half + 64, :]
                                   .rearrange("p (c n) -> p c n", c=8)), f"as{gpar}", 16)
            p.use("att0", f"as{gpar}", p.cnt[f"as{gpar}"])
        p.use(f"psZ{par}", "dv", g)
        p.use("gt0", "dv", g)
        return g


    def gating_unit(tt):
        p.claim(TE, "psZ0")
        for kt in range(8):
            ins = TE.matmul(sc_ps, xnhT[:, kt * TH + tt * 128: kt * TH + tt * 128 + 128],
                            gw_sb[:, kt * 8:kt * 8 + 8], start=(kt == 0), stop=(kt == 7))
        scm = p.inc(ins, "te")
        p.wait(DV, "te", scm)
        p.claim(DV, "gst")
        p.inc(DV.reduce_max(gst[:, 0:1], sc_ps, axis=AX.X, negate=True), "dv")
        p.wait(ACT, "dv", p.cnt["dv"])
        p.claim(ACT, "esb")
        p.inc(ACT.activation(esb[:, :], sc_ps, AF.Exp, bias=gst[:, 0:1]), "ac")
        p.use("psZ0", "ac", p.cnt["ac"])
        p.use("gst", "ac", p.cnt["ac"])
        p.wait(DV, "ac", p.cnt["ac"])
        p.inc(DV.reduce_max(gst[:, 1:2], esb[:, :], axis=AX.X), "dv")
        p.wait(DV, "dv", p.cnt["dv"])
        p.inc(DV.tensor_scalar(msk[:, 0:8], esb[:, :], gst[:, 1:2], None, ALU.is_equal), "dv")
        p.wait(DV, "dv", p.cnt["dv"])
        p.inc(DV.scalar_tensor_tensor(msk[:, 8:16], msk[:, 0:8], -2.0, esb[:, :], ALU.mult, ALU.add), "dv")
        p.wait(DV, "dv", p.cnt["dv"])
        p.inc(DV.reduce_max(gst[:, 2:3], msk[:, 8:16], axis=AX.X), "dv")
        p.wait(DV, "dv", p.cnt["dv"])
        p.inc(DV.tensor_scalar(msk[:, 8:16], msk[:, 8:16], gst[:, 2:3], None, ALU.is_equal), "dv")
        p.inc(DV.tensor_tensor(gst[:, 3:4], gst[:, 1:2], gst[:, 2:3], ALU.add), "dv")
        p.wait(DV, "dv", p.cnt["dv"])
        p.inc(DV.reciprocal(gst[:, 4:5], gst[:, 3:4]), "dv")
        p.wait(DV, "dv", p.cnt["dv"])
        p.inc(DV.tensor_tensor(gst[:, 5:6], gst[:, 1:2], gst[:, 4:5], ALU.mult), "dv")
        p.inc(DV.tensor_tensor(gst[:, 6:7], gst[:, 2:3], gst[:, 4:5], ALU.mult), "dv")
        p.wait(DV, "dv", p.cnt["dv"])
        p.inc(DV.tensor_scalar(combf[:, :], msk[:, 0:8], gst[:, 5:6], None, ALU.mult), "dv")
        p.wait(DV, "dv", p.cnt["dv"])
        p.inc(DV.scalar_tensor_tensor(combf[:, :], msk[:, 8:16], gst[:, 6:7], combf[:, :],
                                      ALU.mult, ALU.add), "dv")
        p.wait(DV, "dv", p.cnt["dv"])
        cb = p.inc(DV.tensor_copy(comb_tok[:, tt * 8:tt * 8 + 8], combf[:, :]), "dv")
        p.use("gst", "dv", cb)
        p.use("esb", "dv", cb)
        if "comb" in dbg:
            p.wait(GP, "dv", cb)
            p.inc(GP.dma_start(out=dbg["comb"][:, tt * 8:tt * 8 + 8], in_=combf[:, :]), "qg", 16)

    def fproj_gen():
        for kt_ in range(8):
            p.inc(GP.dma_start(out=wf_sb[:, ts(kt_, FSH)],
                               in_=Wf_c[kt_ * 128:(kt_ + 1) * 128, :]), "wf", 16)
        p.wait(TE, "wf")
        pos_used = {}
        fpk = 0
        for tc in range(8):
            for kt_ in range(8):
                lb = fpk % 2
                if fpk >= 2:
                    p.wait(GP, "te", pos_used[fpk - 2])
                p.inc(GP.dma_start(out=pos_sb[:, ts(lb, 512)],
                                   in_=CT["posT"][kt_ * 128:(kt_ + 1) * 128, ts(tc, 512)]),
                      f"po{lb}", 16)
                p.wait(TE, f"po{lb}")
                if tc == 0 and kt_ == 0:
                    p.claim(TE, "psA0")
                    p.claim(TE, "psZ0")
                elif kt_ == 0:
                    p.wait(TE, "ac", p.cnt["ac"])
                for ci_ in range(NCI):
                    ins = TE.matmul(f_psn3[ci_][:, :],
                                    wf_sb[:, kt_ * FSH + ci_ * 128: kt_ * FSH + ci_ * 128 + 128],
                                    pos_sb[:, ts(lb, 512)],
                                    start=(kt_ == 0), stop=(kt_ == 7))
                pos_used[fpk] = p.inc(ins, "te")
                fpk += 1
                yield
            mm = p.cnt["te"]
            p.wait(ACT, "te", mm)
            for ci_ in range(NCI):
                p.claim(ACT, "f_ev")
                fpv = p.inc(ACT.activation(f_ev[:, :], f_psn3[ci_][:, :],
                                           AF.Identity, bias=bf_s[:, ci_:ci_ + 1]), "ac")
                p.wait(SY, "ac", fpv)
                fst = p.inc(SY.dma_start(out=f_hbm.ap()[ci_ * 128:(ci_ + 1) * 128, ts(tc, 512)],
                                         in_=f_ev[:, :]), "fs0", 16)
                p.use("f_ev", "fs0", fst)
            yield
        p.use("psA0", "ac", p.cnt["ac"])
        p.use("psZ0", "ac", p.cnt["ac"])
        f_written[0] = p.cnt["fs0"]
        if "fhbm" in dbg:
            p.wait(GP, "fs0", f_written[0])
            p.inc(GP.dma_start(out=dbg["fhbm"], in_=f_hbm.ap()), "qg", 16)

    def fprep_gen():
        p.wait(ACT, "fs0", f_written[0])
        for qi_ in range(FSH // 16):
            row0 = 16 * qi_
            p.claim(ACT, "fin_sb")
            for half in range(2):
                fsrc = bass.AP(f_hbm, (row0 + 8 * half) * T, [[64, 64], [T, 8], [1, 64]])
                p.inc(ACT.dma_start(out=fin_sb[64 * half:64 * half + 64, :]
                                    .rearrange("p (c n) -> p c n", c=8), in_=fsrc), "fi0", 16)
            p.wait(TE, "fi0")
            wb = (wsb[0], "wsb0") if qi_ % 2 == 0 else (s3in[0], "s3in0")
            fq, fsem = GP, ("fz0" if qi_ % 2 == 0 else "fzg")
            o = fwd_first_half(0, 0, fin_sb[0:64, :], fin_sb[64:128, :], None, None,
                               wbuf=wb)
            p.use("fin_sb", "te", p.cnt["te"])
            p.wait(fq, "dv", o["fm"])
            for c2 in range(4):
                for pl in range(2):
                    for half in range(2):
                        dst = bass.AP(agF_in, (row0 + 4 * c2 + half) * FROW + pl * 8192,
                                      [[128, 64], [2 * FROW, 2], [1, 128]])
                        ins = fq.dma_start(out=dst,
                                           in_=wb[0][64 * half:64 * half + 64,
                                                     c2 * 512 + pl * 256:c2 * 512 + pl * 256 + 256]
                                           .rearrange("p (pr k) -> p pr k", pr=2))
                        p.inc(ins, fsem, 16)
            p.use(wb[1], fsem, p.cnt[fsem])
            if qi_ == 7:
                emit_fag(0)
            yield

    def _pre_units():
        yield from fproj_gen()
        yield from fprep_gen()
    _preit = _pre_units()
    _pre_done = [False]

    def step_pre(n=1):
        if _pre_done[0]:
            return
        for _ in range(n):
            try:
                next(_preit)
            except StopIteration:
                _pre_done[0] = True
                return

    xnT = p.sbuf("xnT", [128, 8 * T], BF16, scope=scopeA)

    ln_xt = p.sbuf("ln_xt", [128, 2 * D], F32, scope=scopeA)
    ln_sq = p.sbuf("ln_sq", [128, D], F32, scope=scopeA)
    ln_st = p.sbuf("ln_st", [128, 16], F32, scope=scopeA)
    ln_nb = p.sbuf("ln_nb", [128, 2 * D], BF16, scope=scopeA)
    ln_ps = p.psum("ln_ps", [128, 1024], BF16, scope=scopeA)

    def ln_phase(src_ap, n_tok, dstT, tag, stp=None):
        xt, sq, stat, xnb, pst = ln_xt, ln_sq, ln_st, ln_nb, ln_ps
        ntile = n_tok // 128
        tr_done, ev_done, tr_evac = {}, {}, {}
        for it in range(ntile):
            if stp:
                stp()
            bo = (it % 2) * D
            so = (it % 2) * 8
            xs = xt[:, bo:bo + D]
            qx = f"lx{tag}{it % 2}"
            if it >= 2:
                p.wait(SY, "ac", ev_done[it - 2])
            p.inc(SY.dma_start(out=xs, in_=src_ap[it * 128:(it + 1) * 128, :]), qx, 16)
            p.wait(DV, qx)
            if it >= 2:
                p.wait(DV, "ac", ev_done[it - 2])
            p.inc(DV.reduce_sum(stat[:, so + 0:so + 1], xs, axis=AX.X), "dv")
            p.wait(ACT, qx)
            if it >= 2:
                p.wait(ACT, "dv", p.cnt["dv"])
            p.inc(ACT.activation(sq[:, :], xs, AF.Square), "ac")
            p.wait(DV, "ac", p.cnt["ac"])
            p.inc(DV.reduce_sum(stat[:, so + 1:so + 2], sq[:, :], axis=AX.X), "dv")
            p.wait(DV, "dv", p.cnt["dv"])
            p.inc(DV.tensor_scalar(stat[:, so + 2:so + 3], stat[:, so + 0:so + 1], 1.0 / D, None, ALU.mult), "dv")
            p.inc(DV.tensor_scalar(stat[:, so + 3:so + 4], stat[:, so + 1:so + 2], 1.0 / D, None, ALU.mult), "dv")
            p.wait(DV, "dv", p.cnt["dv"])
            p.inc(DV.tensor_tensor(stat[:, so + 4:so + 5], stat[:, so + 2:so + 3], stat[:, so + 2:so + 3], ALU.mult), "dv")
            p.wait(DV, "dv", p.cnt["dv"])
            p.inc(DV.tensor_tensor(stat[:, so + 5:so + 6], stat[:, so + 3:so + 4], stat[:, so + 4:so + 5], ALU.subtract), "dv")
            p.wait(ACT, "dv", p.cnt["dv"])
            p.inc(ACT.activation(stat[:, so + 6:so + 7], stat[:, so + 5:so + 6], AF.Ln, bias=eps_s[:, :]), "ac")
            p.wait(ACT, "ac", p.cnt["ac"])
            p.inc(ACT.activation(stat[:, so + 6:so + 7], stat[:, so + 6:so + 7], AF.Exp, scale=-0.5), "ac")
            p.wait(DV, "ac", p.cnt["ac"])
            p.inc(DV.tensor_tensor(stat[:, so + 7:so + 8], stat[:, so + 2:so + 3], stat[:, so + 6:so + 7], ALU.mult), "dv")
            p.wait(DV, "dv", p.cnt["dv"])
            p.inc(DV.tensor_scalar(stat[:, so + 7:so + 8], stat[:, so + 7:so + 8], -1.0, None, ALU.mult), "dv")
            p.wait(ACT, "dv", p.cnt["dv"])
            if it >= 2:
                p.wait(ACT, "te", tr_done[it - 2])
            xnbs = xnb[:, bo:bo + D]
            ev_done[it] = p.inc(ACT.activation(xnbs, xs, AF.Identity,
                                               bias=stat[:, so + 7:so + 8],
                                               scale=stat[:, so + 6:so + 7]), "ac")
            p.wait(TE, "ac", p.cnt["ac"])
            p.wait(TE, "qs", consts_done)
            if it >= 1:
                p.wait(TE, "dv", tr_evac[it - 1])
            for j in range(8):
                ins = TE.transpose(pst[:, ts(j, 128)], xnbs[:, ts(j, 128)], idb_s)
            tr_done[it] = p.inc(ins, "te")
            p.wait(DV, "te", tr_done[it])
            tr_evac[it] = p.inc(DV.tensor_copy(
                dstT[:, :].rearrange("p (j tt) -> p j tt", j=8)[:, :, ts(it, 128)],
                pst[:, :].rearrange("p (j tt) -> p j tt", j=8)), "dv")
        return tr_evac[ntile - 1]

    ln_a_done = ln_phase(Xb, T, xnT, "a", stp=lambda: step_pre(3))
    p.wait(SY, "te", p.cnt["te"])
    p.wait(SY, "dv", p.cnt["dv"])
    p.wait(SY, "ac", p.cnt["ac"])
    wp_full = p.sbuf("wp_full", [128, 8 * 2048], BF16, scope=scopeA)
    for kt in range(8):
        p.inc(GP.dma_start(out=wp_full[:, ts(kt, 2048)], in_=Wp_c[kt * 128:(kt + 1) * 128, :]), "wp", 16)
    wp_ld = p.cnt["wp"]
    p.inc(GP.dma_start(out=gw_sb[:, :].rearrange("p (k e) -> p k e", k=8),
                       in_=gwT.rearrange("(k p) e -> p k e", p=128)), "gw", 16)
    ln_h_done = ln_phase(Xh, TH, xnhT, "h", stp=lambda: step_pre(1))

    if "xnT" in dbg:
        p.wait(GP, "dv", ln_a_done)
        p.inc(GP.dma_start(out=dbg["xnT"].rearrange("(j p) tt -> p j tt", p=128),
                           in_=xnT[:, :].rearrange("p (j tt) -> p j tt", j=8)), "qg", 16)
    if "xnhT" in dbg:
        p.wait(GP, "dv", ln_h_done)
        p.inc(GP.dma_start(out=dbg["xnhT"].rearrange("(j p) tt -> p j tt", p=128),
                           in_=xnhT[:, :].rearrange("p (j tt) -> p j tt", j=8)), "qg", 16)

    def gat_gen():
        p.wait(TE, "gw", p.cnt["gw"])
        p.wait(TE, "dv", ln_h_done)
        for tt_ in range(16):
            gating_unit(tt_)
            yield
    _gatit = gat_gen()
    _gat_done = [False]

    def step_gat(n=1):
        if _gat_done[0]:
            return
        for _ in range(n):
            try:
                next(_gatit)
            except StopIteration:
                _gat_done[0] = True
                return


    xp_ps = [p.psum(f"xp_ps{i}", [128, 512], scope=scopeA) for i in range(2)]
    xp_ev = p.sbuf("xp_ev", [128, 2 * 512], BF16, scope=scopeA)
    p.wait(TE, "wp", wp_ld)
    p.wait(TE, "dv", ln_a_done)
    xp_ev_done = {}
    for ci in range(16):
        for tc in range(8):
            idx = ci * 8 + tc
            if idx % 4 == 0:
                step_pre(1)
            if idx % 8 == 4:
                step_gat(1)
            pb = idx % 2
            if idx >= 2:
                p.wait(TE, "ac", xp_ev_done[idx - 2])
                p.wait(ACT, f"xs{pb}", p.cnt[f"xs{pb}"])
            for kt in range(8):
                ins = TE.matmul(xp_ps[pb][:, :],
                                wp_full[:, kt * 2048 + ci * 128: kt * 2048 + ci * 128 + 128],
                                xnT[:, kt * T + tc * 512: kt * T + (tc + 1) * 512],
                                start=(kt == 0), stop=(kt == 7))
            mm = p.inc(ins, "te")
            p.wait(ACT, "te", mm)
            xp_ev_done[idx] = p.inc(
                ACT.activation(xp_ev[:, ts(pb, 512)], xp_ps[pb][:, :],
                               AF.Identity, bias=bp_s[:, ci:ci + 1]), "ac")
            xq = SY if pb == 0 else ACT
            p.wait(xq, "ac", xp_ev_done[idx])
            p.inc(xq.dma_start(out=xp_hbm.ap()[ci * 128:(ci + 1) * 128, ts(tc, 512)],
                               in_=xp_ev[:, ts(pb, 512)]), f"xs{pb}", 16)
    xp_written = (p.cnt["xs0"], p.cnt["xs1"])
    if "xp" in dbg:
        p.wait(GP, "xs0", xp_written[0])
        p.wait(GP, "xs1", xp_written[1])
        p.inc(GP.dma_start(out=dbg["xp"], in_=xp_hbm.ap()), "qg", 16)

    while not (_pre_done[0] and _gat_done[0]):
        step_pre(2)
        step_gat(2)
    comb_done = p.cnt["dv"]

    if BUILD_UNTIL == "p2":
        _finish_stub([scopeA])
        p.ctx.close()
        return nc
    full_barrier()
    scopeA.close()
    if BUILD_UNTIL == "p3proj":
        _finish_stub([scopeB])
        p.ctx.close()
        return nc
    out_fm = p.sbuf("out_fm", [128, 8 * TH], BF16)
    y_sb = p.sbuf("y_sb", [128, NQ * 512], BF16, scope=scopeB)
    fre_one = p.sbuf("fre_sb0", [128, 2048], BF16, scope=scopeB)
    fre_sb = [fre_one, fre_one]
    gt_one = p.sbuf("gt_sb0", [128, 512], BF16, scope=scopeB)
    gt_sb = [gt_one, gt_one]
    att_one = f_ev
    att_sb = [att_one, att_one]
    p.wait(ACT, "xs0", xp_written[0])
    p.wait(ACT, "xs1", xp_written[1])
    y0_q = {}

    def emit_y0_quarter(k):
        for qi in range(8 * k, 8 * k + 8):
            for half in range(2):
                ysrc = bass.AP(xp_hbm, (16 * qi + 8 * half) * T, [[64, 64], [T, 8], [1, 64]])
                p.inc(ACT.dma_start(out=y_sb[64 * half:64 * half + 64, ts(qi, 512)]
                                    .rearrange("p (c n) -> p c n", c=8), in_=ysrc), f"y0q{k}", 16)
        y0_q[k] = p.cnt[f"y0q{k}"]

    emit_y0_quarter(0)

    if "Fsh" in dbg:
        p.wait(GP, "fz0", p.cnt["fz0"])
        p.inc(GP.dma_start(out=dbg["Fsh"], in_=agF_in.ap()), "qg", 16)

    p.wait(ACT, "cc", F_thr[0])

    if BUILD_UNTIL == "gating":
        _finish_stub([scopeC])
        p.ctx.close()
        return nc

    w1f_sb = p.sbuf("w1f_sb", [128, 8 * 2816], BF16, scope=scopeB)
    w2_sb = p.sbuf("w2_sb", [128, 11 * 1024], BF16, scope=scopeB)
    m_sb = fscr[:, 0:2816]
    sg_sb = fscr[:, 2816:3328]
    crep_sb = fscr[:, 3328:3840]
    comb_ct = fscr[0:1, 4608:4864]
    ps_gu = [p.psum(f"ps_gu{i}", [128, 512], scope=scopeB) for i in range(2)]
    ps_o2 = [p.psum("ps_o2_0", [128, 512], scope=scopeB)]

    def ffn_feeder():
        for s_ in ["te", "dv", "ac"]:
            p.wait(GP, s_)
        def comb_dma(e_, ci_):
            p.claim(GP, "comb_ct")
            for h in range(2):
                cc = 2 * ci_ + h
                p.inc(GP.dma_start(out=comb_ct[0:1, h * 128:(h + 1) * 128],
                                   in_=comb_tok[:, cc * 8 + e_:cc * 8 + e_ + 1]), "cr", 16)
            return p.cnt["cr"]

        for e in range(E):
            p.wait(GP, "dv", comb_done)
            cr_next = comb_dma(e, 0)
            p.claim(GP, "w1f_sb")
            for kt in range(8):
                p.inc(GP.dma_start(out=w1f_sb[:, ts(kt, 2816)],
                                   in_=w1[e, kt * 128:(kt + 1) * 128, :]), "w1s", 16)
            w1r = ("w1s", p.cnt["w1s"])
            p.claim(GP, "w2_sb")
            for ft in range(11):
                p.inc(GP.dma_start(out=w2_sb[:, ts(ft, 1024)],
                                   in_=w2[e, ft * 128:(ft + 1) * 128, :]), "w2s", 16)
            w2r = ("w2s", p.cnt["w2s"])
            yield
            for ci in range(8):
                p.wait(TE, "cr", cr_next)
                p.claim(TE, "ps_gu0")
                crm = p.inc(TE.matmul(ps_gu[0][:, 0:256], ones_s[0:1, :],
                                      comb_ct[0:1, :],
                                      start=True, stop=True), "te")
                p.use("comb_ct", "te", crm)
                if ci < 7:
                    cr_next = comb_dma(e, ci + 1)
                p.wait(ACT, "te", crm)
                p.claim(ACT, f"crep_{ci % 2}")
                crv = p.inc(ACT.activation(crep_sb[:, ts(ci % 2, 256)], ps_gu[0][:, 0:256], AF.Copy), "ac")
                p.use("ps_gu0", "ac", crv)
                yield
                p.wait(TE, w1r[0], w1r[1])
                for j in range(11):
                    jp = j % 2
                    p.claim(TE, f"ps_gu{jp}")
                    pg = ps_gu[jp][:, 0:256]
                    pu = ps_gu[jp][:, 256:512]
                    for kt in range(8):
                        TE.matmul(pg, w1f_sb[:, kt * 2816 + j * 128: kt * 2816 + j * 128 + 128],
                                  xnhT[:, kt * TH + ci * 256: kt * TH + ci * 256 + 256],
                                  start=(kt == 0), stop=(kt == 7))
                    for kt in range(8):
                        ins = TE.matmul(pu,
                                        w1f_sb[:, kt * 2816 + FFN + j * 128: kt * 2816 + FFN + j * 128 + 128],
                                        xnhT[:, kt * TH + ci * 256: kt * TH + ci * 256 + 256],
                                        start=(kt == 0), stop=(kt == 7))
                    gum = p.inc(ins, "te")
                    p.wait(ACT, "te", gum)
                    p.claim(ACT, f"sg_{jp}")
                    sgv = p.inc(ACT.activation(sg_sb[:, ts(jp, 256)], pg, AF.Silu), "ac")
                    p.wait(DV, "ac", sgv)
                    p.wait(DV, "ac", crv)
                    t0c = p.inc(DV.tensor_tensor(sg_sb[:, ts(jp, 256)], sg_sb[:, ts(jp, 256)],
                                                 pu, ALU.mult), "dv")
                    p.use(f"ps_gu{jp}", "dv", t0c)
                    p.wait(GP, "dv", t0c)
                    p.wait(GP, "ac", crv)
                    p.claim(GP, f"m_{j}")
                    mj = p.inc(GP.tensor_mul(m_sb[:, ts(j, 256)], sg_sb[:, ts(jp, 256)],
                                             crep_sb[:, ts(ci % 2, 256)]), "gpm")
                    p.use(f"sg_{jp}", "gpm", mj)
                    p.use(f"crep_{ci % 2}", "gpm", mj)
                    yield
                if ci == 7:
                    p.use("w1f_sb", "te", p.cnt["te"])
                mdone = p.cnt["gpm"]
                p.wait(TE, "gpm", mdone)
                p.wait(TE, w2r[0], w2r[1])
                for dtp in range(4):
                    po = ps_o2[0]
                    p.claim(TE, "ps_o2_0")
                    for half in range(2):
                        dt = 2 * dtp + half
                        sl = po[:, half * 256:half * 256 + 256]
                        for ft in range(11):
                            ins = TE.matmul(sl, w2_sb[:, ft * 1024 + dt * 128: ft * 1024 + dt * 128 + 128],
                                            m_sb[:, ts(ft, 256)],
                                            start=(ft == 0), stop=(ft == 10))
                    w2m = p.inc(ins, "te")
                    p.wait(DV, "te", w2m)
                    osl = out_fm[:, :].rearrange("p (dt tt2) -> p dt tt2", dt=8)[
                        :, 2 * dtp:2 * dtp + 2, ci * 256:(ci + 1) * 256]
                    p.claim(DV, f"ofm_{ci}_{dtp}")
                    pov = po[:, :].rearrange("p (h tt2) -> p h tt2", h=2)
                    if e == 0:
                        fv = p.inc(DV.tensor_copy(osl, pov), "dv")
                    else:
                        fv = p.inc(DV.tensor_tensor(osl, pov, osl, ALU.add), "dv")
                    p.use("ps_o2_0", "dv", fv)
                    p.use(f"ofm_{ci}_{dtp}", "dv", fv)
                    yield
                for j in range(11):
                    p.use(f"m_{j}", "te", p.cnt["te"])
            p.use("w2_sb", "te", p.cnt["te"])

    _feed = ffn_feeder()
    _feed_done = [False]

    def step_ffn(n=1):
        if _feed_done[0]:
            return
        for _ in range(n):
            try:
                next(_feed)
            except StopIteration:
                _feed_done[0] = True
                return

    step_ffn(1)
    emit_fag(1)
    emit_fag(2)
    F_ready = p.cnt["cc"]

    A_thr = [0, 0]

    def emit_aag(part):
        r0, r1 = (0, 384) if part == 0 else (384, 512)
        dst = agA_a if part == 0 else agA_b
        nr = r1 - r0
        p.wait(GP, "as0", p.cnt["as0"])
        p.wait(GP, "as1", p.cnt["as1"])
        if SIM_LOCAL:
            for r_ in range(2):
                p.inc(GP.dma_start(out=dst.ap()[r_ * nr:(r_ + 1) * nr, :],
                                   in_=agA_in.ap()[r0:r1, :]), "cc", 16)
            GP.wait_ge(p.sems["cc"], p.cnt["cc"])
        else:
            p.inc(GP.collective_compute(
                "AllGather", ALU.bypass,
                replica_groups=[[0, 1], [2, 3], [4, 5], [6, 7]],
                ins=[agA_in.ap()[r0:r1, :].opt()], outs=[dst.ap().opt()]), "cc", 1)
        A_thr[part] = p.cnt["cc"]

    gate_done = {}
    for st in range(1, STAGES + 1):
        p.wait(DV, "qg", p.cnt["qg"])
        p.wait(ACT, "cc", F_thr[st - 1])
        for qi in range(NQ):
            par = qi % 2
            if st == 1 and qi in (2, 4, 6):
                emit_y0_quarter(qi // 2)
            p.claim(SY, "gt0")
            for half in range(2):
                row0 = st * 512 + 16 * qi + 8 * half
                gsrc = bass.AP(xp_hbm, row0 * T, [[64, 64], [T, 8], [1, 64]])
                p.inc(SY.dma_start(out=gt_sb[par][64 * half:64 * half + 64, :]
                                   .rearrange("p (c n) -> p c n", c=8), in_=gsrc), f"gl{par}", 16)
            gld = p.cnt[f"gl{par}"]
            p.claim(ACT, "fre0")
            for c2 in range(4):
                for pl in range(2):
                    for half in range(2):
                        row0 = 16 * qi + 4 * c2 + half
                        fsrc2 = bass.AP(agF_s[st - 1], row0 * FROW + pl * 8192,
                                        [[128, 64], [2 * FROW, 2], [1, 128]])
                        ins = ACT.dma_start(
                            out=fre_sb[par][64 * half:64 * half + 64,
                                            c2 * 512 + pl * 256:c2 * 512 + pl * 256 + 256]
                            .rearrange("p (pr k) -> p pr k", pr=2), in_=fsrc2)
                        p.inc(ins, f"fa{par}", 16)
            fldc = p.cnt[f"fa{par}"]
            if st > 1:
                p.wait(TE, "dv", gate_done[(st - 1, qi)])
            else:
                p.wait(TE, f"y0q{qi // 8}", y0_q[qi // 8])

            def mk_fld(par_, cnt_):
                def f(c2):
                    p.wait(DV, f"fa{par_}", cnt_)
                    return cnt_
                return f
            step_ffn(1)
            o = fwd_first_half(0, par, y_sb[0:64, ts(qi, 512)], y_sb[64:128, ts(qi, 512)],
                               mk_fld(par, fldc), (st, qi), stp=step_ffn)
            g = bwd_second_half(0, par, st < STAGES, qi, st, gld, o, stp=step_ffn)
            gate_done[(st, qi)] = g
            if st == STAGES and qi == 23:
                emit_aag(0)
    while not _feed_done[0]:
        step_ffn(8)

    emit_aag(1)
    A_ready = p.cnt["cc"]
    if "attn" in dbg:
        p.wait(GP, "as0", p.cnt["as0"])
        p.wait(GP, "as1", p.cnt["as1"])
        p.inc(GP.dma_start(out=dbg["attn"], in_=agA_in.ap()), "qg", 16)

    if BUILD_UNTIL == "conv":
        _finish_stub([scopeB])
        p.ctx.close()
        return nc

    if BUILD_UNTIL == "aag":
        _finish_stub([scopeB])
        p.ctx.close()
        return nc
    full_barrier()
    scopeB.close()
    scopeC.close()
    attn_fm = p.sbuf("attn_fm", [128, 8 * TH], BF16, scope=scopeD)
    idf_s = p.sbuf("idf_s", [128, 128], F32, scope=scopeD)
    p.inc(SY.dma_start(out=idf_s[:, :], in_=identf), "idf", 16)
    cnd_sb = [p.sbuf(f"cnd{i}", [128, TH], BF16, scope=scopeD) for i in range(2)]
    o1_sb = p.sbuf("o1_sb", [128, 1024], F32, scope=scopeD)
    xh_sb = [p.sbuf(f"xh{i}", [128, 1024], F32, scope=scopeD) for i in range(6)]
    outb = p.sbuf("outb", [128, 2 * 1024], F32, scope=scopeD)
    ps_pt = [p.psum(f"ps_pt{i}", [128, 512], scope=scopeD) for i in range(2)]
    p.wait(SY, "dv", p.cnt["dv"])
    p.wait(SY, "te", p.cnt["te"])
    acc_prev = {}
    for dt in range(8):
        hsrc = 0 if dt < 4 else 1
        for colh in range(2):
            par = (dt * 2 + colh) % 2
            ldq = SY if par == 0 else ACT
            p.wait(ldq, "cc", A_thr[1] if dt % 4 == 3 else A_thr[0])
            p.claim(ldq, f"cnd{par}")
            if dt % 4 == 3:
                asrc = agA_b.ap()[hsrc * 128:hsrc * 128 + 128, colh * TH:(colh + 1) * TH]
            else:
                asrc = agA_a.ap()[hsrc * 384 + (dt % 4) * 128:
                                  hsrc * 384 + (dt % 4) * 128 + 128,
                                  colh * TH:(colh + 1) * TH]
            p.inc(ldq.dma_start(out=cnd_sb[par][:, :], in_=asrc), f"cn{par}", 16)
            p.wait(DV, f"cn{par}")
            dstsl = attn_fm[:, dt * TH:(dt + 1) * TH]
            if colh == 0:
                a = p.inc(DV.tensor_scalar(dstsl, cnd_sb[par][:, :], cm_s[:, colh:colh + 1],
                                           None, ALU.mult), "dv")
            else:
                p.wait(DV, "dv", acc_prev[dt])
                a = p.inc(DV.scalar_tensor_tensor(dstsl, cnd_sb[par][:, :], cm_s[:, colh:colh + 1],
                                                  dstsl, ALU.mult, ALU.add), "dv")
            acc_prev[dt] = a
            p.use(f"cnd{par}", "dv", a)
    attnfm_done = p.cnt["dv"]
    if "attnfm" in dbg:
        p.wait(GP, "dv", attnfm_done)
        p.inc(GP.dma_start(out=dbg["attnfm"].rearrange("(j p) tt -> p j tt", p=128),
                           in_=attn_fm[:, :].rearrange("p (j tt) -> p j tt", j=8)), "qg", 16)

    xh_q = [SY, ACT, GP]

    def xh_load(tt_):
        s = tt_ % 6
        q = xh_q[s % 3]
        p.claim(q, f"xh{s}")
        p.inc(q.dma_start(out=xh_sb[s][:, :], in_=Xh[tt_ * 128:(tt_ + 1) * 128, :]), f"xh{s}", 16)

    for tt in range(6):
        xh_load(tt)
    out_done = {}
    for tt in range(16):
        s = tt % 6
        ob_p = tt % 2
        p.wait(DV, "dv", attnfm_done)
        p.claim(DV, "o1_sb")
        o1 = p.inc(DV.tensor_tensor(
            o1_sb[:, :].rearrange("p (dt t2) -> p dt t2", dt=8),
            out_fm[:, :].rearrange("p (dt tt2) -> p dt tt2", dt=8)[:, :, ts(tt, 128)],
            attn_fm[:, :].rearrange("p (dt tt2) -> p dt tt2", dt=8)[:, :, ts(tt, 128)],
            ALU.add), "dv")
        p.wait(TE, "dv", o1)
        p.wait(TE, "idf")
        p.claim(TE, "ps_pt")
        for dt in range(8):
            ins = TE.transpose(ps_pt[dt // 4][:, (dt % 4) * 128:(dt % 4) * 128 + 128],
                               o1_sb[:, ts(dt, 128)], idf_s[:, :])
        tpm = p.inc(ins, "te")
        p.use("o1_sb", "te", tpm)
        p.wait(DV, "te", tpm)
        p.wait(DV, f"xh{s}")
        p.claim(DV, f"outb{ob_p}")
        DV.tensor_tensor(outb[:, ob_p * 1024:ob_p * 1024 + 512], ps_pt[0][:, :],
                         xh_sb[s][:, 0:512], ALU.add)
        ob = p.inc(DV.tensor_tensor(outb[:, ob_p * 1024 + 512:ob_p * 1024 + 1024], ps_pt[1][:, :],
                                    xh_sb[s][:, 512:1024], ALU.add), "dv")
        p.use("ps_pt", "dv", ob)
        p.use(f"xh{s}", "dv", ob)
        oq, osem = (SY, "ostr") if tt % 2 == 0 else (ACT, "osta")
        p.wait(oq, "dv", ob)
        od = p.inc(oq.dma_start(out=out[tt * 128:(tt + 1) * 128, :],
                                in_=outb[:, ob_p * 1024:(ob_p + 1) * 1024]), osem, 16)
        p.use(f"outb{ob_p}", osem, od)
        out_done[osem] = od
        if tt + 6 < 16:
            xh_load(tt + 6)
    p.wait(GP, "qg", p.cnt["qg"])
    p.wait(SY, "ostr", out_done["ostr"])
    p.wait(SY, "osta", out_done["osta"])

    scopeD.close()
    p.ctx.close()
    return nc


def kernel(**inputs):
    maps = _shard_inputs(inputs)
    nc = build_program(debug=DEBUG_OUTS)
    res = run_bass_kernel_spmd(nc, maps, core_ids=list(range(8)))
    full = np.zeros((B, T, D), np.float32)
    for c in range(8):
        b, h = c >> 1, c & 1
        full[b, h * TH:(h + 1) * TH, :] = res.results[c]["out"]
    if DEBUG_OUTS:
        kernel.debug = [dict(res.results[c]) for c in range(8)]
    return full
